# version 02
# baseline (speedup 1.0000x reference)
import contextlib
import math

import numpy as np
import concourse.bass as bass
import concourse.mybir as mybir
from concourse.bass_utils import run_bass_kernel_spmd

F32 = mybir.dt.float32
BF16 = mybir.dt.bfloat16
I32 = mybir.dt.int32
AF = mybir.ActivationFunctionType
ALU = mybir.AluOpType
AX = mybir.AxisListType

B_PER_CORE = 2
D = 1024
H = 16
DH = 64
NF = D // 128
LN_EPS = 1e-5
PI = math.pi
PI2 = 2.0 * math.pi

T = 2048
SW = 512
NT = T // 128
NS = T // SW


def set_dims(t, sw=512):
    global T, NT, NS, SW
    T, SW = t, sw
    NT = T // 128
    NS = T // SW



_UID = [0]


def uname(base):
    _UID[0] += 1
    return f"{base}_{_UID[0]}"



class Dma:
    def __init__(self, nc, ctx, name):
        self.sem = ctx.enter_context(nc.semaphore(name))
        self.n = 0

    def start(self, eng, out, in_, **kw):
        eng.dma_start(out, in_, **kw).then_inc(self.sem, 16)
        self.n += 16
        return self.n

    def wait_all(self, eng):
        eng.wait_ge(self.sem, self.n)

    def wait(self, eng, upto):
        eng.wait_ge(self.sem, upto)


class Prog:
    def __init__(self, nc):
        self.nc = nc
        self.ctx = contextlib.ExitStack()
        self.ld = Dma(nc, self.ctx, "d_ld")
        self.gl = Dma(nc, self.ctx, "d_gl")
        self.st = Dma(nc, self.ctx, "d_st")
        self.sem = {}
        self.cnt = {}
        for nm in ("t", "v", "s", "g"):
            self.sem[nm] = self.ctx.enter_context(nc.semaphore("e_" + nm))
            self.cnt[nm] = 0

    def inc(self, key, ins):
        ins.then_inc(self.sem[key], 1)
        self.cnt[key] += 1
        return self.cnt[key]

    def wait(self, eng, key, val=None):
        eng.wait_ge(self.sem[key], self.cnt[key] if val is None else val)

    def sbuf(self, name, shape, dtype):
        return self.ctx.enter_context(self.nc.sbuf_tensor(name, shape, dtype))

    def close(self):
        self.ctx.close()


def dap(h, off, pattern):
    return bass.AP(h, off, pattern)



def emit_prelude(p):
    nc = p.nc
    p.onesB = p.sbuf("onesB", [128, 1], BF16)
    p.onesBr = p.sbuf("onesBr", [1, 128], BF16)
    p.identF = p.sbuf("identF", [128, 128], F32)
    p.identB = p.sbuf("identB", [128, 128], BF16)
    p.r_m10k = p.ctx.enter_context(nc.gpsimd.register("r_m10k"))
    with nc.Block() as b:
        @b.gpsimd
        def _(g):
            g.reg_mov(p.r_m10k, -10000.0)
            g.memset(p.onesB[:, :], 1.0)
            g.memset(p.onesBr[:, :], 1.0)
            g.memset(p.identF[:, :], 1.0)
            g.memset(p.identB[:, :], 1.0)
            g.affine_select(p.identF[:, :], p.identF[:, :], [[-1, 128]],
                            ALU.is_equal, 0.0, base=0, channel_multiplier=1)
            g.affine_select(p.identB[:, :], p.identB[:, :], [[-1, 128]],
                            ALU.is_equal, 0.0, base=0, channel_multiplier=1)


def register_consts(p, vals):
    vals = sorted(set(float(v) for v in vals) - {0.0, 1.0})
    if not vals:
        return
    p.cpool = p.sbuf("cpool", [128, len(vals)], F32)
    for i, v in enumerate(vals):
        p.nc.const_aps.register(F32, v, p.cpool[:, i:i + 1])
    with p.nc.Block() as b:
        @b.gpsimd
        def _(g):
            for i, v in enumerate(vals):
                g.memset(p.cpool[:, i:i + 1], v)



def emit_vec_cols(p, es, handle, n, src_off=0):
    nc = p.nc
    ncol = n // 128
    col_sb = es.enter_context(nc.sbuf_tensor(uname("veccol"), [128, ncol],
                                             F32))
    with contextlib.ExitStack() as e2:
        row_sb = e2.enter_context(nc.sbuf_tensor(uname("vecrow"), [1, n],
                                                 F32))
        with nc.Block() as b:
            @b.sync
            def _(sy):
                p.ld.start(sy, row_sb[0:1, :],
                           dap(handle, src_off, [[n, 1], [1, n]]))
        ldn = p.ld.n
        with contextlib.ExitStack() as e3:
            pc = e3.enter_context(nc.psum_tensor(uname("veccol_ps"),
                                                 [128, ncol], F32))
            with nc.Block() as b:
                @b.tensor
                def _(t):
                    p.ld.wait(t, ldn)
                    last = None
                    for c in range(ncol):
                        last = t.matmul(pc[:, c:c + 1],
                                        row_sb[0:1, c * 128:(c + 1) * 128],
                                        p.identF[0:1, 0:1], start=True,
                                        stop=True)
                    p.inc("t", last)
            with nc.Block() as b:
                @b.vector
                def _(v):
                    p.inc("v", v.tensor_copy(col_sb[:, :], pc[:, :]))
    return col_sb, None


def emit_vec_bcast(p, dst_sb, handle, n, src_off=0):
    nc = p.nc
    with contextlib.ExitStack() as e2:
        row_sb = e2.enter_context(nc.sbuf_tensor(uname("vbrow"), [1, n],
                                                 F32))
        with nc.Block() as b:
            @b.sync
            def _(sy):
                p.ld.start(sy, row_sb[0:1, :],
                           dap(handle, src_off, [[n, 1], [1, n]]))
        ldn = p.ld.n
        with nc.Block() as b:
            @b.vector
            def _(v):
                p.ld.wait(v, ldn)
        emit_bcast_row(p, dst_sb, row_sb, 0, n)


def emit_copy_state(p, dst, src, nelem):
    with p.nc.Block() as b:
        @b.sync
        def _(sy):
            step = nelem // 4
            for i in range(4):
                p.ld.start(sy, dap(dst, i * step, [[step, 1], [1, step]]),
                           dap(src, i * step, [[step, 1], [1, step]]))


def load_w_bf16(p, g, dst_sb, w_dram, rows, cols, dst_row_width=None,
                src_off=0, src_row_stride=None, dst_off=0,
                dst_chunk_stride=None):
    nfc = rows // 128
    rw = dst_row_width if dst_row_width is not None else nfc * cols
    dcs = dst_chunk_stride if dst_chunk_stride is not None else cols
    srs = src_row_stride if src_row_stride is not None else cols
    dst = dap(dst_sb, dst_off, [[rw, 128], [dcs, nfc], [1, cols]])
    src = dap(w_dram, src_off, [[srs, 128], [128 * srs, nfc], [1, cols]])
    p.gl.start(g, dst, src)



def emit_ln_fm(p, xb, src_dram_off=None, src_dram=None, mode="ln",
               gamma=None, beta=None, src_f32_sb=None, out_f32_sb=None,
               out_dram_fn=None, src_fn=None, out_accum=False, eps=LN_EPS):
    nc = p.nc
    if src_fn is not None:
        src_f32_sb = [src_fn(fc) for fc in range(NF)]
    fp32src = src_f32_sb is not None
    with contextlib.ExitStack() as es:
        xq = [es.enter_context(nc.sbuf_tensor(uname(f"lnq{i}"), [128, T], BF16))
              for i in range(2)]
        xc = ([es.enter_context(nc.sbuf_tensor(uname(f"lnc{i}"), [128, T],
                                               BF16)) for i in range(2)]
              if (fp32src and mode == "ln") else None)
        srow_mu = es.enter_context(nc.sbuf_tensor(uname("ln_smu"), [1, T], F32))
        srow_va = es.enter_context(nc.sbuf_tensor(uname("ln_sva"), [1, T], F32))
        brow_mu = es.enter_context(nc.sbuf_tensor(uname("ln_bmu"), [1, T], BF16))
        brow_va = es.enter_context(nc.sbuf_tensor(uname("ln_bva"), [1, T], BF16))
        rstd = es.enter_context(nc.sbuf_tensor(uname("ln_rstd"), [128, T], F32))
        gl0 = p.gl.n
        if src_dram is not None:
            with nc.Block() as b:
                @b.gpsimd
                def _(g):
                    p.gl.start(g, dap(xb, 0,
                                      [[NF * T, 128], [T, NF], [1, T]]),
                               dap(src_dram, src_dram_off,
                                   [[T, 128], [T * 128, NF], [1, T]]))

        def chunk(fc):
            if fp32src:
                c = src_f32_sb[fc]
                return c if src_fn is not None else c[:, :]
            return xb[:, fc * T:(fc + 1) * T]

        if xb is None:
            assert fp32src

        with contextlib.ExitStack() as es2:
            stat = es2.enter_context(nc.psum_tensor(uname("ln_stat"), [33, T], F32))
            with nc.Block() as b:
                s0 = p.cnt["s"]
                t0 = p.cnt["t"]

                g0 = p.cnt["g"]

                @b.scalar
                def _(s):
                    for fc in range(NF):
                        if src_dram is not None:
                            p.gl.wait(s, gl0 + 16)
                        if fc >= 2:
                            p.wait(s, "t", t0 + fc - 1)
                        p.inc("s", s.activation(xq[fc % 2][:, :], chunk(fc),
                                                AF.Square))

                if fp32src and mode == "ln":
                    @b.gpsimd
                    def _(g):
                        for fc in range(NF):
                            if fc >= 2:
                                p.wait(g, "t", t0 + fc - 1)
                            p.inc("g", g.tensor_copy(xc[fc % 2][:, :],
                                                     chunk(fc)))

                @b.tensor
                def _(t):
                    for fc in range(NF):
                        p.wait(t, "s", s0 + fc + 1)
                        if fp32src and mode == "ln":
                            p.wait(t, "g", g0 + fc + 1)
                        last = None
                        for ts in range(NS):
                            sl = slice(ts * SW, (ts + 1) * SW)
                            if mode == "ln":
                                src_sum = (xc[fc % 2][:, sl] if fp32src else
                                           xb[:, fc * T + ts * SW:
                                              fc * T + (ts + 1) * SW])
                                last = t.matmul(
                                    stat[0:1, sl], p.onesB[:, :], src_sum,
                                    start=(fc == 0), stop=(fc == NF - 1))
                            last = t.matmul(
                                stat[32:33, sl], p.onesB[:, :],
                                xq[fc % 2][:, sl], start=(fc == 0),
                                stop=(fc == NF - 1))
                        p.inc("t", last)

            assert not (fp32src and mode == "ln") or True

            with nc.Block() as b:
                @b.vector
                def _(v):
                    if mode == "ln":
                        p.inc("v", v.tensor_scalar(
                            srow_mu[0:1, :], stat[0:1, :], 1.0 / D, None,
                            ALU.mult))
                        p.wait(v, "v")
                        p.inc("v", v.tensor_mul(srow_va[0:1, :],
                                                srow_mu[0:1, :],
                                                srow_mu[0:1, :]))
                        p.wait(v, "v")
                        p.inc("v", v.scalar_tensor_tensor(
                            srow_va[0:1, :], stat[32:33, :], 1.0 / D,
                            srow_va[0:1, :], ALU.mult, ALU.subtract))
                        p.wait(v, "v")
                        p.inc("v", v.tensor_scalar(
                            srow_va[0:1, :], srow_va[0:1, :], eps, None,
                            ALU.add))
                        p.wait(v, "v")
                        p.inc("v", v.tensor_copy(brow_mu[0:1, :], srow_mu[0:1, :]))
                    else:
                        p.inc("v", v.tensor_scalar(
                            srow_va[0:1, :], stat[32:33, :], 1.0 / D, eps,
                            ALU.mult, ALU.add))
                        p.wait(v, "v")
                    p.inc("v", v.tensor_copy(brow_va[0:1, :], srow_va[0:1, :]))

        with contextlib.ExitStack() as es2:
            bc = es2.enter_context(nc.psum_tensor(uname("ln_bc"), [128, 2 * T], F32))
            with nc.Block() as b:
                @b.tensor
                def _(t):
                    last = None
                    for ts in range(NS):
                        sl = slice(ts * SW, (ts + 1) * SW)
                        sl2 = slice(T + ts * SW, T + (ts + 1) * SW)
                        if mode == "ln":
                            last = t.matmul(bc[:, sl], p.onesBr[:, :],
                                            brow_mu[0:1, sl], start=True,
                                            stop=True)
                        last = t.matmul(bc[:, sl2], p.onesBr[:, :],
                                        brow_va[0:1, sl], start=True,
                                        stop=True)
                    p.inc("t", last)

            with nc.Block() as b:
                @b.scalar
                def _(s):
                    p.inc("s", s.activation(rstd[:, :], bc[:, T:2 * T],
                                            AF.Ln))
                    p.wait(s, "s")
                    p.inc("s", s.activation(rstd[:, :], rstd[:, :], AF.Exp,
                                            scale=-0.5))

            with nc.Block() as b:
                @b.vector
                def _(v):
                    for fc in range(NF):
                        src = chunk(fc)
                        dst = (xb[:, fc * T:(fc + 1) * T]
                               if xb is not None else None)
                        if mode == "ln":
                            p.inc("v", v.tensor_sub(src, src, bc[:, 0:T]))
                            p.wait(v, "v")
                        p.inc("v", v.tensor_mul(src, src, rstd[:, :]))
                        p.wait(v, "v")
                        if gamma is not None:
                            p.inc("v", v.tensor_scalar(
                                src, src, gamma[:, fc:fc + 1],
                                beta[:, fc:fc + 1], ALU.mult, ALU.add))
                            p.wait(v, "v")
                        if fp32src and xb is not None:
                            p.inc("v", v.tensor_copy(dst, src))
                        if out_f32_sb is not None and \
                                out_f32_sb is not src_f32_sb:
                            p.inc("v", v.tensor_copy(out_f32_sb[fc][:, :],
                                                     src))
            if out_dram_fn is not None:
                with nc.Block() as b:
                    @b.gpsimd
                    def _(g):
                        for fc in range(NF):
                            kw = {"accum_op": ALU.add} if out_accum else {}
                            p.st.start(g, out_dram_fn(fc),
                                       chunk(fc) if fp32src
                                       else xb[:, fc * T:(fc + 1) * T], **kw)
                with nc.Block() as b:
                    @b.gpsimd
                    def _(g):
                        p.st.wait_all(g)



def attn_headloop(p, *, q_all, k_all, v_sl, oT, causal):
    nc = p.nc
    with contextlib.ExitStack() as es:
        sb = lambda n, sh, dt: es.enter_context(nc.sbuf_tensor(uname(n), sh, dt))
        qaug = sb("qaug", [65, T], BF16)
        kaug = sb("kaug", [65, T], BF16)
        mcols = sb("mcols", [128, NT], F32)
        probsT = sb("probsT", [128, NT * SW], BF16)
        rrow = sb("rrow", [1, SW], BF16)
        otmp = sb("otmp", [64, SW], F32)

        with nc.Block() as b:
            @b.gpsimd
            def _(g):
                g.memset(kaug[64:65, :], 1.0)

        for h in range(H):
            rbase = (h % 2) * 64
            grp = h // 2
            with nc.Block() as b:
                @b.vector
                def _(v):
                    p.inc("v", v.tensor_copy(
                        qaug[0:64, :],
                        q_all[rbase:rbase + 64, grp * T:(grp + 1) * T]))
                    p.inc("v", v.tensor_copy(
                        kaug[0:64, :],
                        k_all[rbase:rbase + 64, grp * T:(grp + 1) * T]))

            with contextlib.ExitStack() as esA:
                psA = esA.enter_context(
                    nc.psum_tensor(uname("psA"), [128, 2 * T], F32))
                with nc.Block() as b:
                    tA0 = p.cnt["t"]
                    vA0 = p.cnt["v"]

                    @b.tensor
                    def _(t):
                        for ti in range(NT):
                            nk = min((ti // (SW // 128) + 1) * SW, T) \
                                if causal else T
                            if ti >= 2:
                                p.wait(t, "v", vA0 + ti - 1)
                            dst = psA[:, (ti % 2) * T:(ti % 2) * T + T]
                            last = None
                            for ts in range(nk // SW):
                                sl = slice(ts * SW, (ts + 1) * SW)
                                last = t.matmul(
                                    dst[:, sl],
                                    qaug[0:64, ti * 128:(ti + 1) * 128],
                                    kaug[0:64, sl], start=True, stop=True)
                            p.inc("t", last)

                    @b.vector
                    def _(v):
                        for ti in range(NT):
                            nk = min((ti // (SW // 128) + 1) * SW, T) \
                                if causal else T
                            p.wait(v, "t", tA0 + ti + 1)
                            src = psA[:, (ti % 2) * T:(ti % 2) * T + nk]
                            p.inc("v", v.tensor_reduce(
                                mcols[:, ti:ti + 1], src, AX.X, ALU.max,
                                negate=True))

            with contextlib.ExitStack() as esM:
                pmr = esM.enter_context(nc.psum_tensor(uname("pmr"), [1, T], F32))
                with nc.Block() as b:
                    @b.tensor
                    def _(t):
                        last = None
                        for ti in range(NT):
                            last = t.transpose(
                                pmr[0:1, ti * 128:(ti + 1) * 128],
                                mcols[:, ti:ti + 1], p.identF[:, :])
                        p.inc("t", last)
                with nc.Block() as b:
                    @b.vector
                    def _(v):
                        p.inc("v", v.tensor_copy(qaug[64:65, :], pmr[0:1, :]))

            for st in range(NS):
                nk_t = min((st + 1) * (SW // 128), NT) if causal else NT
                with contextlib.ExitStack() as esB:
                    psB = esB.enter_context(
                        nc.psum_tensor(uname("psB"), [128, 2 * SW], F32))
                    psPV = esB.enter_context(
                        nc.psum_tensor(uname("psPV"), [65, SW], F32))
                    pbc = esB.enter_context(
                        nc.psum_tensor(uname("pbc"), [64, SW], F32))

                    with nc.Block() as b:
                        tB0 = p.cnt["t"]
                        sB0 = p.cnt["s"]

                        @b.tensor
                        def _(t):
                            for c in range(nk_t):
                                if c >= 2:
                                    p.wait(t, "s", sB0 + c - 1)
                                p.inc("t", t.matmul(
                                    psB[:, (c % 2) * SW:(c % 2 + 1) * SW],
                                    kaug[:, c * 128:(c + 1) * 128],
                                    qaug[:, st * SW:(st + 1) * SW],
                                    start=True, stop=True))

                        @b.scalar
                        def _(s):
                            for c in range(nk_t):
                                p.wait(s, "t", tB0 + c + 1)
                                p.inc("s", s.activation(
                                    probsT[:, c * SW:(c + 1) * SW],
                                    psB[:, (c % 2) * SW:(c % 2 + 1) * SW],
                                    AF.Exp))

                        if causal:
                            @b.gpsimd
                            def _(g):
                                for c in range(nk_t):
                                    if c * 128 + 127 <= st * SW:
                                        continue
                                    p.wait(g, "s", sB0 + c + 1)
                                    p.inc("g", g.affine_select(
                                        probsT[:, c * SW:(c + 1) * SW],
                                        probsT[:, c * SW:(c + 1) * SW],
                                        [[1, SW]], ALU.is_ge, 0.0,
                                        base=st * SW - c * 128,
                                        channel_multiplier=-1))

                    with nc.Block() as b:
                        @b.tensor
                        def _(t):
                            for c in range(nk_t):
                                p.inc("t", t.matmul(
                                    psPV[:, :],
                                    v_sl[:, c * (H * 65) + h * 65:
                                         c * (H * 65) + h * 65 + 65],
                                    probsT[:, c * SW:(c + 1) * SW],
                                    start=(c == 0), stop=(c == nk_t - 1)))

                    with nc.Block() as b:
                        @b.scalar
                        def _(s):
                            p.inc("s", s.activation(rrow[0:1, :],
                                                    psPV[64:65, :], AF.Ln))
                            p.wait(s, "s")
                            p.inc("s", s.activation(rrow[0:1, :],
                                                    rrow[0:1, :], AF.Exp,
                                                    scale=-1.0))
                            p.wait(s, "s")
                            p.inc("s", s.activation(otmp[:, :], psPV[0:64, :],
                                                    AF.Copy))

                    with nc.Block() as b:
                        @b.tensor
                        def _(t):
                            p.inc("t", t.matmul(
                                pbc[:, :], p.onesBr[0:1, 0:64], rrow[0:1, :],
                                start=True, stop=True))

                    with nc.Block() as b:
                        @b.vector
                        def _(v):
                            dst = oT[rbase:rbase + 64,
                                     grp * T + st * SW:
                                     grp * T + (st + 1) * SW]
                            p.inc("v", v.tensor_mul(dst, otmp[:, :],
                                                    pbc[:, :]))



def emit_rope_tables(p, es, theta=500.0, half=32):
    nc = p.nc
    sb = lambda n, sh, dt: es.enter_context(nc.sbuf_tensor(uname(n), sh, dt))
    cosT = sb("cosT", [128, T], BF16)
    sinT = sb("sinT", [128, T], BF16)
    with contextlib.ExitStack() as e2:
        s2 = lambda n, sh, dt: e2.enter_context(nc.sbuf_tensor(uname(n), sh, dt))
        frow = s2("frow", [1, 64], F32)
        trow = s2("trow", [1, T], F32)
        ang = s2("ang", [64, T], F32)
        u = s2("rr_u", [64, T], F32)
        ui = s2("rr_i", [64, T], I32)
        pang = e2.enter_context(nc.psum_tensor(uname("pang"), [64, T], F32))

        with nc.Block() as b:
            @b.gpsimd
            def _(g):
                g.iota(frow[:, :], [[0, 2], [1, 32]], base=0,
                       channel_multiplier=0,
                       allow_small_or_imprecise_dtypes=True)
                g.iota(trow[:, :], [[1, T]], base=0, channel_multiplier=0,
                       allow_small_or_imprecise_dtypes=True)

        with nc.Block() as b:
            @b.scalar
            def _(s):
                p.inc("s", s.activation(frow[:, :], frow[:, :], AF.Exp,
                                        scale=-math.log(theta) / half))

        with nc.Block() as b:
            @b.tensor
            def _(t):
                for ts in range(NS):
                    sl = slice(ts * SW, (ts + 1) * SW)
                    p.inc("t", t.matmul(pang[:, sl], frow[0:1, :],
                                        trow[0:1, sl], start=True, stop=True))

        for which, tab, shift in (("s", sinT, 0.0), ("c", cosT, PI / 2)):
            with nc.Block() as b:
                @b.vector
                def _(v):
                    p.inc("v", v.tensor_scalar(ang[:, :], pang[:, :], 1.0,
                                               shift, ALU.mult, ALU.add))
                    p.wait(v, "v")
                    p.inc("v", v.tensor_scalar(u[:, :], ang[:, :],
                                               1.0 / PI2, None, ALU.mult))
                    p.wait(v, "v")
                    p.inc("v", v.tensor_copy(ui[:, :], u[:, :]))
                    p.wait(v, "v")
                    p.inc("v", v.tensor_copy(u[:, :], ui[:, :]))
                    p.wait(v, "v")
                    p.inc("v", v.scalar_tensor_tensor(
                        ang[:, :], u[:, :], -PI2, ang[:, :], ALU.mult,
                        ALU.add))

                @b.scalar
                def _(s):
                    p.wait(s, "v")
                    p.inc("s", s.activation(tab[0:64, :], ang[:, :], AF.Sin))
                    p.inc("s", s.activation(tab[64:128, :], ang[:, :],
                                            AF.Sin))
    return cosT, sinT


def emit_rope_apply(p, A, cosT, sinT, es):
    nc = p.nc
    with contextlib.ExitStack() as e2:
        tmp = e2.enter_context(nc.sbuf_tensor(uname("rp_tmp"), [128, T],
                                              BF16))
        for grp in range(NF):
            sl = slice(grp * T, (grp + 1) * T)
            with nc.Block() as b:
                @b.vector
                def _(v):
                    p.inc("v", v.tensor_scalar(
                        tmp[0:32, :], A[32:64, sl], -1.0, None, ALU.mult))
                    p.inc("v", v.tensor_copy(tmp[32:64, :], A[0:32, sl]))
                    p.inc("v", v.tensor_scalar(
                        tmp[64:96, :], A[96:128, sl], -1.0, None, ALU.mult))
                    p.inc("v", v.tensor_copy(tmp[96:128, :], A[64:96, sl]))
                    p.wait(v, "v")
                    p.inc("v", v.tensor_mul(A[:, sl], A[:, sl], cosT[:, :]))
                    p.inc("v", v.tensor_mul(tmp[:, :], tmp[:, :],
                                            sinT[:, :]))
                    p.wait(v, "v")
                    p.inc("v", v.tensor_add(A[:, sl], A[:, sl], tmp[:, :]))



def emit_oproj(p, w_dram, bias_col, oT, xout, b, accum=True, w_sb=None):
    nc = p.nc
    with contextlib.ExitStack() as es:
        if w_sb is None:
            wsb = es.enter_context(
                nc.sbuf_tensor(uname("wo_sb"), [128, NF * D], BF16))
            with nc.Block() as blk:
                @blk.gpsimd
                def _(g):
                    load_w_bf16(p, g, wsb, w_dram, D, D)
        else:
            wsb = w_sb
        osb = [es.enter_context(nc.sbuf_tensor(uname(f"osb{i}"), [128, SW], F32))
               for i in range(2)]
        pso = es.enter_context(nc.psum_tensor(uname("pso"), [128, 2 * SW], F32))

        jobs = [(dc, ts) for dc in range(NF) for ts in range(NS)]
        with nc.Block() as blk:
            t0 = p.cnt["t"]
            s0 = p.cnt["s"]
            st0 = p.st.n
            gl_n = p.gl.n

            @blk.tensor
            def _(t):
                p.gl.wait(t, gl_n)
                for i, (dc, ts) in enumerate(jobs):
                    if i >= 2:
                        p.wait(t, "s", s0 + i - 1)
                    last = None
                    for fc in range(NF):
                        last = t.matmul(
                            pso[:, (i % 2) * SW:(i % 2 + 1) * SW],
                            wsb[:, fc * D + dc * 128:fc * D + (dc + 1) * 128],
                            oT[:, fc * T + ts * SW:fc * T + (ts + 1) * SW],
                            start=(fc == 0), stop=(fc == NF - 1))
                    p.inc("t", last)

            @blk.scalar
            def _(s):
                for i, (dc, ts) in enumerate(jobs):
                    p.wait(s, "t", t0 + i + 1)
                    if i >= 2:
                        p.st.wait(s, st0 + 16 * (i - 1))
                    p.inc("s", s.activation(
                        osb[i % 2][:, :],
                        pso[:, (i % 2) * SW:(i % 2 + 1) * SW],
                        AF.Identity if bias_col is not None else AF.Copy,
                        bias=bias_col[:, dc:dc + 1] if bias_col is not None
                        else 0.0))

            @blk.gpsimd
            def _(g):
                for i, (dc, ts) in enumerate(jobs):
                    p.wait(g, "s", s0 + i + 1)
                    p.st.start(g, dap(
                        xout, (b * D + dc * 128) * T + ts * SW,
                        [[T, 128], [1, SW]]),
                        osb[i % 2][:, :], accum_op=ALU.add)

        with nc.Block() as blk:
            @blk.gpsimd
            def _(g):
                p.st.wait_all(g)



def emit_qkv_heads(p, xnorm, wq, rw, qoff, koff, voff, q_all, k_all, v_sl,
                   qscale=1.0, qbias=None, kbias=None, bvb=None,
                   do_qk=True, do_v=True):
    nc = p.nc
    gl_n = p.gl.n
    with contextlib.ExitStack() as e4:
        if not do_qk:
            psq = None
        else:
            psq = e4.enter_context(nc.psum_tensor(uname("psq"),
                                                  [64, 2 * T], F32))
        for h in range(H if do_qk else 0):
            rbase = (h % 2) * 64
            grp = h // 2
            with nc.Block() as blk:
                @blk.tensor
                def _(t):
                    p.gl.wait(t, gl_n)
                    for which, woff0 in ((0, qoff), (1, koff)):
                        woff = woff0 + h * 64
                        dst = psq[:, which * T:(which + 1) * T]
                        last = None
                        for fc in range(NF):
                            lhsT = wq[:, fc * rw + woff:fc * rw + woff + 64]
                            for ts in range(NS):
                                sl = slice(ts * SW, (ts + 1) * SW)
                                last = t.matmul(
                                    dst[:, sl], lhsT,
                                    xnorm[:, fc * T + ts * SW:
                                          fc * T + (ts + 1) * SW],
                                    start=(fc == 0), stop=(fc == NF - 1))
                        p.inc("t", last)

            with nc.Block() as blk:
                @blk.scalar
                def _(s):
                    qdst = q_all[rbase:rbase + 64, grp * T:(grp + 1) * T]
                    kdst = k_all[rbase:rbase + 64, grp * T:(grp + 1) * T]
                    if qbias is not None:
                        p.inc("s", s.activation(qdst, psq[:, 0:T],
                                                AF.Identity, bias=qbias(h),
                                                scale=qscale))
                    else:
                        p.inc("s", s.activation(qdst, psq[:, 0:T], AF.Copy,
                                                scale=qscale))
                    if kbias is not None:
                        p.inc("s", s.activation(kdst, psq[:, T:2 * T],
                                                AF.Identity, bias=kbias(h)))
                    else:
                        p.inc("s", s.activation(kdst, psq[:, T:2 * T],
                                                AF.Copy))

    with contextlib.ExitStack() as e4:
        if not do_v:
            return
        psv = e4.enter_context(nc.psum_tensor(uname("psv"), [128, 2 * SW],
                                              F32))
        for tc in range(NT):
            for hf in range(D // SW):
                slot = tc * (D // SW) + hf
                with nc.Block() as blk:
                    @blk.tensor
                    def _(t):
                        dst = psv[:, (slot % 2) * SW:(slot % 2 + 1) * SW]
                        last = None
                        for fc in range(NF):
                            last = t.matmul(
                                dst,
                                xnorm[:, fc * T + tc * 128:
                                      fc * T + (tc + 1) * 128],
                                wq[:, fc * rw + voff + hf * SW:
                                   fc * rw + voff + (hf + 1) * SW],
                                start=(fc == 0), stop=(fc == NF - 1))
                        p.inc("t", last)
                with nc.Block() as blk:
                    @blk.vector
                    def _(v):
                        src = psv[:, (slot % 2) * SW:(slot % 2 + 1) * SW]
                        for hh in range(SW // 64):
                            h2 = hf * (SW // 64) + hh
                            doff = tc * H * 65 + h2 * 65
                            if bvb is not None:
                                p.inc("v", v.scalar_tensor_tensor(
                                    v_sl[:, doff:doff + 64],
                                    src[:, hh * 64:(hh + 1) * 64], 1.0,
                                    bvb[:, h2 * 64:(h2 + 1) * 64],
                                    ALU.mult, ALU.add))
                            else:
                                p.inc("v", v.tensor_copy(
                                    v_sl[:, doff:doff + 64],
                                    src[:, hh * 64:(hh + 1) * 64]))


def emit_vsl_ones(p, v_sl):
    with p.nc.Block() as blk:
        @blk.gpsimd
        def _(g):
            for tc in range(NT):
                for h in range(H):
                    off = tc * H * 65 + h * 65 + 64
                    g.memset(v_sl[:, off:off + 1], 1.0)


def emit_attn_l0a(p, io, b):
    nc = p.nc
    scale = 1.0 / math.sqrt(DH)
    with contextlib.ExitStack() as es:
        sb = lambda n, sh, dt: es.enter_context(nc.sbuf_tensor(uname(n), sh, dt))
        q_all = sb("q_all", [128, NF * T], BF16)
        k_all = sb("k_all", [128, NF * T], BF16)
        v_sl = sb("v_sl", [128, NT * H * 65], BF16)
        bvb = sb("bvb", [128, D], F32)
        bqs_c = sb("bqs_c", [128, 8], F32)
        bqkv_c, _ = emit_vec_cols(p, es, io["b_qkv"], 3 * D)
        bout_c, _ = emit_vec_cols(p, es, io["b_out"], D)
        emit_vsl_ones(p, v_sl)

        with nc.Block() as blk:
            @blk.vector
            def _(v):
                p.inc("v", v.tensor_scalar(bqs_c[:, :], bqkv_c[:, 0:8],
                                           scale, None, ALU.mult))

        emit_vec_bcast(p, bvb, io["b_qkv"], D, src_off=2 * D)

        with contextlib.ExitStack() as e2:
            xnorm = e2.enter_context(
                nc.sbuf_tensor(uname("xnorm"), [128, NF * T], BF16))
            emit_ln_fm(p, xnorm, src_dram=io["xst"], src_dram_off=b * D * T)
            def qbias(h):
                r, c = (h * 64) % 128, (h * 64) // 128
                return bqs_c[r:r + 64, c:c + 1]

            def kbias(h):
                r, c = (D + h * 64) % 128, (D + h * 64) // 128
                return bqkv_c[r:r + 64, c:c + 1]

            with contextlib.ExitStack() as e3:
                wq = e3.enter_context(
                    nc.sbuf_tensor(uname("wqk_sb"), [128, NF * 2 * D],
                                   BF16))
                with nc.Block() as blk:
                    @blk.gpsimd
                    def _(g):
                        load_w_bf16(p, g, wq, io["w_qkv"], D, 2 * D,
                                    src_row_stride=3 * D)
                emit_qkv_heads(p, xnorm, wq, 2 * D, 0, D, None,
                               q_all, k_all, v_sl, qscale=scale,
                               qbias=qbias, kbias=kbias, do_v=False)
            with contextlib.ExitStack() as e3:
                wv = e3.enter_context(
                    nc.sbuf_tensor(uname("wv_sb"), [128, NF * D], BF16))
                with nc.Block() as blk:
                    @blk.gpsimd
                    def _(g):
                        load_w_bf16(p, g, wv, io["w_qkv"], D, D,
                                    src_row_stride=3 * D, src_off=2 * D)
                emit_qkv_heads(p, xnorm, wv, D, None, None, 0,
                               q_all, k_all, v_sl, bvb=bvb, do_qk=False)

        with contextlib.ExitStack() as e2:
            cosT, sinT = emit_rope_tables(p, e2)
            emit_rope_apply(p, q_all, cosT, sinT, e2)
            emit_rope_apply(p, k_all, cosT, sinT, e2)

        with contextlib.ExitStack() as e2:
            oT = e2.enter_context(nc.sbuf_tensor(uname("oT"),
                                                 [128, NF * T], BF16))
            attn_headloop(p, q_all=q_all, k_all=k_all, v_sl=v_sl, oT=oT,
                          causal=False)
            emit_oproj(p, io["w_out"], bout_c, oT, io["xout"], b)


def emit_bcast_row(p, dst_sb, row_sb, row_off, n):
    nc = p.nc
    with contextlib.ExitStack() as e2:
        bvp = e2.enter_context(nc.psum_tensor(uname("bvp"), [128, SW], F32))
        brb = e2.enter_context(nc.sbuf_tensor(uname("brb"), [1, n], BF16))
        with nc.Block() as blk:
            @blk.vector
            def _(v):
                p.inc("v", v.tensor_copy(brb[0:1, :],
                                         row_sb[0:1, row_off:row_off + n]))
        for j in range(n // SW):
            with nc.Block() as blk:
                @blk.tensor
                def _(t):
                    p.inc("t", t.matmul(bvp[:, :], p.onesBr[:, :],
                                        brb[0:1, j * SW:(j + 1) * SW],
                                        start=True, stop=True))
            with nc.Block() as blk:
                @blk.vector
                def _(v):
                    p.inc("v", v.tensor_copy(dst_sb[:, j * SW:(j + 1) * SW],
                                             bvp[:, :]))



NE = 8
ED = 2048
NH = ED // 128


def emit_router(p, es, xnorm2, rw_handle, sqc_all):
    nc = p.nc
    with contextlib.ExitStack() as e2:
        s2 = lambda n, sh, dt: e2.enter_context(
            nc.sbuf_tensor(uname(n), sh, dt))
        rw_sb = s2("rw_sb", [128, NF * NE], BF16)
        probs = s2("probs", [128, NE], F32)
        nmx = s2("nmx", [128, 1], F32)
        rsum = s2("rsum", [128, 1], F32)
        t1 = s2("t1", [128, 1], F32)
        t2 = s2("t2", [128, 1], F32)
        eq = s2("eq", [128, NE], F32)
        pm = s2("pm", [128, NE], F32)
        cs = s2("cs", [128, 1], F32)
        psr = e2.enter_context(nc.psum_tensor(uname("psr"), [128, NE], F32))
        with nc.Block() as blk:
            @blk.gpsimd
            def _(g):
                load_w_bf16(p, g, rw_sb, rw_handle, D, NE)
        gl_n = p.gl.n
        for tc in range(NT):
            with nc.Block() as blk:
                @blk.tensor
                def _(t):
                    p.gl.wait(t, gl_n)
                    last = None
                    for fc in range(NF):
                        last = t.matmul(
                            psr[:, :],
                            xnorm2[:, fc * T + tc * 128:
                                   fc * T + (tc + 1) * 128],
                            rw_sb[:, fc * NE:(fc + 1) * NE],
                            start=(fc == 0), stop=(fc == NF - 1))
                    p.inc("t", last)
            with nc.Block() as blk:
                @blk.vector
                def _(v):
                    p.inc("v", v.tensor_reduce(nmx[:, :], psr[:, :], AX.X,
                                               ALU.max, negate=True))

                @blk.scalar
                def _(sc):
                    p.wait(sc, "v")
                    p.inc("s", sc.activation(probs[:, :], psr[:, :], AF.Exp,
                                             bias=nmx[:, :]))
            with nc.Block() as blk:
                @blk.vector
                def _(v):
                    p.inc("v", v.tensor_reduce(t1[:, :], probs[:, :], AX.X,
                                               ALU.max))
                    p.wait(v, "v")
                    p.inc("v", v.tensor_scalar(eq[:, :], probs[:, :],
                                               t1[:, :], None, ALU.is_ge))
                    p.wait(v, "v")
                    p.inc("v", v.tensor_scalar(eq[:, :], eq[:, :], 2.0,
                                               None, ALU.mult))
                    p.wait(v, "v")
                    p.inc("v", v.tensor_sub(pm[:, :], probs[:, :], eq[:, :]))
                    p.wait(v, "v")
                    p.inc("v", v.tensor_reduce(t2[:, :], pm[:, :], AX.X,
                                               ALU.max))
                    p.wait(v, "v")
                    p.inc("v", v.tensor_add(cs[:, :], t1[:, :], t2[:, :]))
                    p.wait(v, "v")
                    p.inc("v", v.reciprocal(cs[:, :], cs[:, :]))
                    p.wait(v, "v")
                    p.inc("v", v.tensor_scalar(eq[:, :], probs[:, :],
                                               t2[:, :], None, ALU.is_ge))
                    p.wait(v, "v")
                    p.inc("v", v.tensor_mul(pm[:, :], probs[:, :], eq[:, :]))
                    p.wait(v, "v")
                    p.inc("v", v.tensor_scalar(pm[:, :], pm[:, :], cs[:, :],
                                               None, ALU.mult))

                @blk.scalar
                def _(sc):
                    p.wait(sc, "v")
                    p.inc("s", sc.activation(
                        sqc_all[:, tc * NE:(tc + 1) * NE], pm[:, :],
                        AF.Sqrt))


def emit_expert(p, io, b, e, xnorm2, sqc_all):
    nc = p.nc
    with contextlib.ExitStack() as es:
        sb = lambda n, sh, dt: es.enter_context(
            nc.sbuf_tensor(uname(n), sh, dt))
        h1_sb = sb("h1_sb", [128, NH * T], BF16)
        srow = sb("sqc_row", [1, T], BF16)
        hr = [sb(f"hr{i}", [128, SW], BF16) for i in range(2)]

        with contextlib.ExitStack() as e2:
            s2 = lambda n, sh, dt: e2.enter_context(
                nc.sbuf_tensor(uname(n), sh, dt))
            xfe = s2("xfe", [128, NF * T], BF16)
            fc1_sb = s2("fc1_sb", [128, NF * ED], BF16)
            with nc.Block() as blk:
                @blk.gpsimd
                def _(g):
                    load_w_bf16(p, g, fc1_sb, io["exp_fc1"], D, ED,
                                src_off=e * D * ED)
            gl_n = p.gl.n

            with contextlib.ExitStack() as e3:
                psw = e3.enter_context(nc.psum_tensor(uname("psw"),
                                                      [1, T], F32))
                with nc.Block() as blk:
                    @blk.tensor
                    def _(t):
                        last = None
                        for tc in range(NT):
                            last = t.matmul(
                                psw[0:1, tc * 128:(tc + 1) * 128],
                                sqc_all[:, tc * NE + e:tc * NE + e + 1],
                                p.identF[:, :], start=True, stop=True)
                        p.inc("t", last)
                with nc.Block() as blk:
                    @blk.vector
                    def _(v):
                        p.inc("v", v.tensor_copy(srow[0:1, :], psw[0:1, :]))

            with contextlib.ExitStack() as e3:
                pcb = e3.enter_context(nc.psum_tensor(uname("pcb"),
                                                      [128, SW], F32))
                for ts in range(NS):
                    with nc.Block() as blk:
                        @blk.tensor
                        def _(t):
                            p.inc("t", t.matmul(
                                pcb[:, :], p.onesBr[:, :],
                                srow[0:1, ts * SW:(ts + 1) * SW],
                                start=True, stop=True))
                    with nc.Block() as blk:
                        @blk.vector
                        def _(v):
                            for fc in range(NF):
                                p.inc("v", v.tensor_mul(
                                    xfe[:, fc * T + ts * SW:
                                        fc * T + (ts + 1) * SW],
                                    xnorm2[:, fc * T + ts * SW:
                                           fc * T + (ts + 1) * SW],
                                    pcb[:, :]))

            with contextlib.ExitStack() as e3:
                ps1 = e3.enter_context(nc.psum_tensor(uname("ps1"),
                                                      [128, 2 * SW], F32))
                jobs = [(hc, ts) for hc in range(NH) for ts in range(NS)]
                with nc.Block() as blk:
                    t0 = p.cnt["t"]
                    s0 = p.cnt["s"]

                    @blk.tensor
                    def _(t):
                        p.gl.wait(t, gl_n)
                        for i, (hc, ts) in enumerate(jobs):
                            if i >= 2:
                                p.wait(t, "s", s0 + 2 * (i - 1))
                            last = None
                            for fc in range(NF):
                                last = t.matmul(
                                    ps1[:, (i % 2) * SW:(i % 2 + 1) * SW],
                                    fc1_sb[:, fc * ED + hc * 128:
                                           fc * ED + (hc + 1) * 128],
                                    xfe[:, fc * T + ts * SW:
                                        fc * T + (ts + 1) * SW],
                                    start=(fc == 0), stop=(fc == NF - 1))
                            p.inc("t", last)

                    @blk.scalar
                    def _(sc):
                        for i, (hc, ts) in enumerate(jobs):
                            p.wait(sc, "t", t0 + i + 1)
                            p.inc("s", sc.activation(
                                hr[i % 2][:, :],
                                ps1[:, (i % 2) * SW:(i % 2 + 1) * SW],
                                AF.Relu))
                            p.wait(sc, "s")
                            p.inc("s", sc.activation(
                                h1_sb[:, hc * T + ts * SW:
                                      hc * T + (ts + 1) * SW],
                                hr[i % 2][:, :], AF.Square))

        with contextlib.ExitStack() as e2:
            s2 = lambda n, sh, dt: e2.enter_context(
                nc.sbuf_tensor(uname(n), sh, dt))
            fc2_sb = s2("fc2_sb", [128, NH * D], BF16)
            osb = [s2(f"osb{i}", [128, SW], F32) for i in range(2)]
            ps2 = e2.enter_context(nc.psum_tensor(uname("ps2"),
                                                  [128, 2 * SW], F32))
            with nc.Block() as blk:
                @blk.gpsimd
                def _(g):
                    load_w_bf16(p, g, fc2_sb, io["exp_fc2"], ED, D,
                                src_off=e * ED * D)
            gl_n = p.gl.n
            jobs = [(dc, ts) for dc in range(NF) for ts in range(NS)]
            with nc.Block() as blk:
                t0 = p.cnt["t"]
                s0 = p.cnt["s"]
                st0 = p.st.n

                @blk.tensor
                def _(t):
                    p.gl.wait(t, gl_n)
                    for i, (dc, ts) in enumerate(jobs):
                        if i >= 2:
                            p.wait(t, "s", s0 + i - 1)
                        last = None
                        for hc in range(NH):
                            last = t.matmul(
                                ps2[:, (i % 2) * SW:(i % 2 + 1) * SW],
                                fc2_sb[:, hc * D + dc * 128:
                                       hc * D + (dc + 1) * 128],
                                h1_sb[:, hc * T + ts * SW:
                                      hc * T + (ts + 1) * SW],
                                start=(hc == 0), stop=(hc == NH - 1))
                        p.inc("t", last)

                @blk.scalar
                def _(sc):
                    for i, (dc, ts) in enumerate(jobs):
                        p.wait(sc, "t", t0 + i + 1)
                        if i >= 2:
                            p.st.wait(sc, st0 + 16 * (i - 1))
                        p.inc("s", sc.activation(
                            osb[i % 2][:, :],
                            ps2[:, (i % 2) * SW:(i % 2 + 1) * SW], AF.Copy))

                @blk.gpsimd
                def _(g):
                    for i, (dc, ts) in enumerate(jobs):
                        p.wait(g, "s", s0 + i + 1)
                        p.st.start(g, dap(
                            io["xf"], (b * D + dc * 128) * T + ts * SW,
                            [[T, 128], [1, SW]]),
                            osb[i % 2][:, :], accum_op=ALU.add)

            with nc.Block() as blk:
                @blk.gpsimd
                def _(g):
                    p.st.wait_all(g)


def emit_accum_state(p, dst, src, b):
    nc = p.nc
    with contextlib.ExitStack() as es:
        xs = [es.enter_context(nc.sbuf_tensor(uname(f"acc{i}"), [128, T],
                                              F32)) for i in range(2)]
        ld0 = p.ld.n
        st0 = p.st.n
        with nc.Block() as blk:
            @blk.sync
            def _(sy):
                p.st.wait(sy, st0)
                for fc in range(NF):
                    if fc >= 2:
                        p.st.wait(sy, st0 + 16 * (fc - 1))
                    p.ld.start(sy, xs[fc % 2][:, :],
                               dap(src, (b * D + fc * 128) * T,
                                   [[T, 128], [1, T]]))

            @blk.gpsimd
            def _(g):
                for fc in range(NF):
                    p.ld.wait(g, ld0 + 16 * (fc + 1))
                    p.st.start(g, dap(dst, (b * D + fc * 128) * T,
                                      [[T, 128], [1, T]]),
                               xs[fc % 2][:, :], accum_op=ALU.add)
        with nc.Block() as blk:
            @blk.gpsimd
            def _(g):
                p.st.wait_all(g)


def emit_l0_ffn(p, io, b):
    nc = p.nc
    scale = 1.0 / math.sqrt(DH)
    with contextlib.ExitStack() as es:
        xsrc = [es.enter_context(nc.sbuf_tensor(uname(f"xs{i}"), [128, T],
                                                F32)) for i in range(NF)]
        ld0 = p.ld.n
        with nc.Block() as blk:
            @blk.sync
            def _(sy):
                for fc in range(NF):
                    p.ld.start(sy, xsrc[fc][:, :],
                               dap(io["xst"], (b * D + fc * 128) * T,
                                   [[T, 128], [1, T]]))
        with nc.Block() as blk:
            @blk.vector
            def _(v):
                p.ld.wait(v, ld0 + 16 * NF)
        emit_ln_fm(p, None, mode="ln", src_f32_sb=xsrc,
                   out_dram_fn=lambda fc: dap(
                       io["xf"], (b * D + fc * 128) * T, [[T, 128], [1, T]]))

    with contextlib.ExitStack() as es:
        sb = lambda n, sh, dt: es.enter_context(
            nc.sbuf_tensor(uname(n), sh, dt))
        q_all = sb("q_all", [128, NF * T], BF16)
        k_all = sb("k_all", [128, NF * T], BF16)
        v_sl = sb("v_sl", [128, NT * H * 65], BF16)
        emit_vsl_ones(p, v_sl)
        with contextlib.ExitStack() as e2:
            xnorm = e2.enter_context(
                nc.sbuf_tensor(uname("xnorm"), [128, NF * T], BF16))
            with nc.Block() as blk:
                @blk.gpsimd
                def _(g):
                    p.st.wait_all(g)
            emit_ln_fm(p, xnorm, src_dram=io["xf"], src_dram_off=b * D * T,
                       mode="rms")
            with contextlib.ExitStack() as e3:
                wq = e3.enter_context(
                    nc.sbuf_tensor(uname("wqk_sb"), [128, NF * 2 * D],
                                   BF16))
                with nc.Block() as blk:
                    @blk.gpsimd
                    def _(g):
                        for m, hd in enumerate((io["wq"], io["wk"])):
                            load_w_bf16(p, g, wq, hd, D, D,
                                        dst_row_width=NF * 2 * D,
                                        dst_chunk_stride=2 * D,
                                        dst_off=m * D)
                emit_qkv_heads(p, xnorm, wq, 2 * D, 0, D, None,
                               q_all, k_all, v_sl, qscale=scale,
                               do_v=False)
            with contextlib.ExitStack() as e3:
                wv = e3.enter_context(
                    nc.sbuf_tensor(uname("wv_sb"), [128, NF * D], BF16))
                with nc.Block() as blk:
                    @blk.gpsimd
                    def _(g):
                        load_w_bf16(p, g, wv, io["wv"], D, D)
                emit_qkv_heads(p, xnorm, wv, D, None, None, 0,
                               q_all, k_all, v_sl, do_qk=False)
        with contextlib.ExitStack() as e2:
            oT = e2.enter_context(nc.sbuf_tensor(uname("oT"),
                                                 [128, NF * T], BF16))
            attn_headloop(p, q_all=q_all, k_all=k_all, v_sl=v_sl, oT=oT,
                          causal=True)
            emit_oproj(p, io["wo"], None, oT, io["xf"], b)

    with contextlib.ExitStack() as es:
        xnorm2 = es.enter_context(
            nc.sbuf_tensor(uname("xnorm2"), [128, NF * T], BF16))
        sqc_all = es.enter_context(
            nc.sbuf_tensor(uname("sqc_all"), [128, NT * NE], F32))
        with nc.Block() as blk:
            @blk.gpsimd
            def _(g):
                p.st.wait_all(g)
        emit_ln_fm(p, xnorm2, src_dram=io["xf"], src_dram_off=b * D * T,
                   mode="rms")
        emit_router(p, es, xnorm2, io["router_w"], sqc_all)
        for e in range(NE):
            emit_expert(p, io, b, e, xnorm2, sqc_all)

    emit_accum_state(p, io["xout"], io["xf"], b)


def build_l0_ffn_prog():
    nc = bass.Bass("TRN2", target_bir_lowering=False)
    xst = nc.dram_tensor("xst", [B_PER_CORE, D, T], F32,
                         kind="ExternalInput")
    xout = nc.dram_tensor("xout", [B_PER_CORE, D, T], F32,
                          kind="ExternalOutput")
    io = {"xst": xst, "xout": xout}
    for nm, shape in (("wq", [D, D]), ("wk", [D, D]), ("wv", [D, D]),
                      ("wo", [D, D]), ("router_w", [D, NE]),
                      ("exp_fc1", [NE, D, ED]), ("exp_fc2", [NE, ED, D])):
        io[nm] = nc.dram_tensor(nm, shape, F32, kind="ExternalInput")
    io["xf"] = nc.dram_tensor("xf", [B_PER_CORE, D, T], F32)

    p = Prog(nc)
    emit_prelude(p)
    emit_copy_state(p, xout, xst, B_PER_CORE * D * T)
    with nc.Block() as blk:
        @blk.gpsimd
        def _(g):
            p.ld.wait_all(g)
    for b in range(B_PER_CORE):
        emit_l0_ffn(p, io, b)
    with nc.Block() as blk:
        @blk.sync
        def _(sy):
            p.ld.wait_all(sy)
            p.st.wait_all(sy)
            p.gl.wait_all(sy)
    p.close()
    return nc



NC_COL = 16
NN = 256


def emit_l1_ffn(p, io, b):
    nc = p.nc
    with contextlib.ExitStack() as es:
        sb = lambda n, sh, dt: es.enter_context(
            nc.sbuf_tensor(uname(n), sh, dt))
        xs = sb("xs", [128, NF * T], BF16)
        wcol_all = sb("wcol_all", [128, NT * NC_COL], F32)
        rnorm = sb("rnorm", [128, NT], F32)
        cembT = sb("cembT", [128, NF * NC_COL], BF16)
        gb1_c, _ = emit_vec_cols(p, es, io["gate_b1"], 512)
        bcols_c, _ = emit_vec_cols(p, es, io["b_cols"], NC_COL * NN)
        pb_c, _ = emit_vec_cols(p, es, io["proj_b"], D)
        lng_c, _ = emit_vec_cols(p, es, io["ln_g"], D)
        lnb_c, _ = emit_vec_cols(p, es, io["ln_b"], D)

        emit_ln_fm(p, xs, src_dram=io["xst"], src_dram_off=b * D * T)

        with contextlib.ExitStack() as e2:
            s2 = lambda n, sh, dt: e2.enter_context(
                nc.sbuf_tensor(uname(n), sh, dt))
            xq = [s2(f"xq{i}", [128, T], BF16) for i in range(2)]
            ssr = s2("ssr", [1, T], F32)
            with contextlib.ExitStack() as e3:
                stat = e3.enter_context(nc.psum_tensor(uname("l2st"),
                                                       [1, T], F32))
                with nc.Block() as blk:
                    s0 = p.cnt["s"]
                    t0 = p.cnt["t"]

                    @blk.scalar
                    def _(sc):
                        for fc in range(NF):
                            if fc >= 2:
                                p.wait(sc, "t", t0 + fc - 1)
                            p.inc("s", sc.activation(
                                xq[fc % 2][:, :], xs[:, fc * T:(fc + 1) * T],
                                AF.Square))

                    @blk.tensor
                    def _(t):
                        for fc in range(NF):
                            p.wait(t, "s", s0 + fc + 1)
                            last = None
                            for ts in range(NS):
                                sl = slice(ts * SW, (ts + 1) * SW)
                                last = t.matmul(stat[0:1, sl], p.onesB[:, :],
                                                xq[fc % 2][:, sl],
                                                start=(fc == 0),
                                                stop=(fc == NF - 1))
                            p.inc("t", last)

                with nc.Block() as blk:
                    @blk.vector
                    def _(v):
                        p.inc("v", v.tensor_scalar(ssr[0:1, :], stat[0:1, :],
                                                   1e-24, None, ALU.max))

                    @blk.scalar
                    def _(sc):
                        p.wait(sc, "v")
                        p.inc("s", sc.activation(ssr[0:1, :], ssr[0:1, :],
                                                 AF.Ln))
                        p.wait(sc, "s")
                        p.inc("s", sc.activation(ssr[0:1, :], ssr[0:1, :],
                                                 AF.Exp, scale=-0.5))

            with contextlib.ExitStack() as e3:
                prn = e3.enter_context(nc.psum_tensor(uname("prn"),
                                                      [128, NT], F32))
                with nc.Block() as blk:
                    @blk.tensor
                    def _(t):
                        last = None
                        for tc in range(NT):
                            last = t.matmul(
                                prn[:, tc:tc + 1],
                                ssr[0:1, tc * 128:(tc + 1) * 128],
                                p.identF[0:1, 0:1], start=True, stop=True)
                        p.inc("t", last)
                with nc.Block() as blk:
                    @blk.vector
                    def _(v):
                        p.inc("v", v.tensor_copy(rnorm[:, :], prn[:, :]))

        with contextlib.ExitStack() as e2:
            s2 = lambda n, sh, dt: e2.enter_context(
                nc.sbuf_tensor(uname(n), sh, dt))
            ce = s2("ce", [16, D], F32)
            ce2 = s2("ce2", [16, D], F32)
            cs2 = s2("cs2", [16, 1], F32)
            cen = s2("cen", [16, D], BF16)
            ld0 = p.ld.n
            with nc.Block() as blk:
                @blk.sync
                def _(sy):
                    p.ld.start(sy, ce[:, :], dap(io["col_emb"], 0,
                                                 [[D, 16], [1, D]]))
            with nc.Block() as blk:
                @blk.vector
                def _(v):
                    p.ld.wait(v, ld0 + 16)
                    p.inc("v", v.tensor_mul(ce2[:, :], ce[:, :], ce[:, :]))
                    p.wait(v, "v")
                    p.inc("v", v.tensor_reduce(cs2[:, :], ce2[:, :], AX.X,
                                               ALU.add))
                    p.wait(v, "v")
                    p.inc("v", v.tensor_scalar(cs2[:, :], cs2[:, :], 1e-24,
                                               None, ALU.max))

                @blk.scalar
                def _(sc):
                    p.wait(sc, "v")
                    p.inc("s", sc.activation(cs2[:, :], cs2[:, :], AF.Ln))
                    p.wait(sc, "s")
                    p.inc("s", sc.activation(cs2[:, :], cs2[:, :], AF.Exp,
                                             scale=-0.5))
            with nc.Block() as blk:
                @blk.vector
                def _(v):
                    p.inc("v", v.tensor_scalar(cen[:, :], ce[:, :],
                                               cs2[:, :], None, ALU.mult))
            with contextlib.ExitStack() as e3:
                pce = e3.enter_context(nc.psum_tensor(uname("pce"),
                                                      [128, NC_COL], BF16))
                for fc in range(NF):
                    with nc.Block() as blk:
                        @blk.tensor
                        def _(t):
                            p.inc("t", t.transpose(
                                pce[:, :], cen[:, fc * 128:(fc + 1) * 128],
                                p.identB[0:16, 0:16]))
                    with nc.Block() as blk:
                        @blk.vector
                        def _(v):
                            p.inc("v", v.tensor_copy(
                                cembT[:, fc * NC_COL:(fc + 1) * NC_COL],
                                pce[:, :]))

        with contextlib.ExitStack() as e2:
            s2 = lambda n, sh, dt: e2.enter_context(
                nc.sbuf_tensor(uname(n), sh, dt))
            hgel = s2("hgel", [128, 4 * T], BF16)
            w1sb = s2("w1sb", [128, NF * 512], BF16)
            with nc.Block() as blk:
                @blk.gpsimd
                def _(g):
                    load_w_bf16(p, g, w1sb, io["gate_w1"], D, 512)
            gl_n = p.gl.n
            with contextlib.ExitStack() as e3:
                ps1 = e3.enter_context(nc.psum_tensor(uname("psg1"),
                                                      [128, 2 * SW], F32))
                jobs = [(gc, ts) for gc in range(4) for ts in range(NS)]
                with nc.Block() as blk:
                    t0 = p.cnt["t"]
                    s0 = p.cnt["s"]

                    @blk.tensor
                    def _(t):
                        p.gl.wait(t, gl_n)
                        for i, (gc, ts) in enumerate(jobs):
                            if i >= 2:
                                p.wait(t, "s", s0 + i - 1)
                            last = None
                            for fc in range(NF):
                                last = t.matmul(
                                    ps1[:, (i % 2) * SW:(i % 2 + 1) * SW],
                                    w1sb[:, fc * 512 + gc * 128:
                                         fc * 512 + (gc + 1) * 128],
                                    xs[:, fc * T + ts * SW:
                                       fc * T + (ts + 1) * SW],
                                    start=(fc == 0), stop=(fc == NF - 1))
                            p.inc("t", last)

                    @blk.scalar
                    def _(sc):
                        for i, (gc, ts) in enumerate(jobs):
                            p.wait(sc, "t", t0 + i + 1)
                            p.inc("s", sc.activation(
                                hgel[:, gc * T + ts * SW:
                                     gc * T + (ts + 1) * SW],
                                ps1[:, (i % 2) * SW:(i % 2 + 1) * SW],
                                AF.Gelu, bias=gb1_c[:, gc:gc + 1]))

            w2sb = s2("w2sb", [128, 4 * NC_COL], BF16)
            b2b = s2("b2b", [128, NC_COL], F32)
            b2r = s2("b2r", [1, NC_COL], BF16)
            with nc.Block() as blk:
                @blk.gpsimd
                def _(g):
                    load_w_bf16(p, g, w2sb, io["gate_w2"], 512, NC_COL)
                    p.gl.start(g, b2r[0:1, :],
                               dap(io["gate_b2"], 0,
                                   [[NC_COL, 1], [1, NC_COL]]))
            gl_n = p.gl.n
            with contextlib.ExitStack() as e3:
                pb2 = e3.enter_context(nc.psum_tensor(uname("pb2"),
                                                      [128, NC_COL], F32))
                with nc.Block() as blk:
                    @blk.tensor
                    def _(t):
                        p.gl.wait(t, gl_n)
                        p.inc("t", t.matmul(pb2[:, :], p.onesBr[:, :],
                                            b2r[0:1, :], start=True,
                                            stop=True))
                with nc.Block() as blk:
                    @blk.vector
                    def _(v):
                        p.inc("v", v.tensor_copy(b2b[:, :], pb2[:, :]))

            with contextlib.ExitStack() as e3:
                s3 = lambda n, sh, dt: e3.enter_context(
                    nc.sbuf_tensor(uname(n), sh, dt))
                logit = s3("logit", [128, NC_COL], F32)
                gv = s3("gv", [128, NC_COL], F32)
                nmx = s3("nmx", [128, 1], F32)
                rsum = s3("rsum", [128, 1], F32)
                t1c = s3("t1c", [128, 1], F32)
                msk = s3("msk", [128, NC_COL], F32)
                psim = e3.enter_context(nc.psum_tensor(uname("psim"),
                                                       [128, NC_COL], F32))
                pg2 = e3.enter_context(nc.psum_tensor(uname("pg2"),
                                                      [128, NC_COL], F32))
                for tc in range(NT):
                    with nc.Block() as blk:
                        @blk.tensor
                        def _(t):
                            last = None
                            for fc in range(NF):
                                last = t.matmul(
                                    psim[:, :],
                                    xs[:, fc * T + tc * 128:
                                       fc * T + (tc + 1) * 128],
                                    cembT[:, fc * NC_COL:(fc + 1) * NC_COL],
                                    start=(fc == 0), stop=(fc == NF - 1))
                            p.inc("t", last)
                            for gc in range(4):
                                last = t.matmul(
                                    pg2[:, :],
                                    hgel[:, gc * T + tc * 128:
                                         gc * T + (tc + 1) * 128],
                                    w2sb[:, gc * NC_COL:(gc + 1) * NC_COL],
                                    start=(gc == 0), stop=(gc == 3))
                            p.inc("t", last)

                    with nc.Block() as blk:
                        @blk.vector
                        def _(v):
                            p.inc("v", v.tensor_add(gv[:, :], pg2[:, :],
                                                    b2b[:, :]))

                        @blk.scalar
                        def _(sc):
                            p.wait(sc, "v")
                            p.inc("s", sc.activation(gv[:, :], gv[:, :],
                                                     AF.Sigmoid))

                    with nc.Block() as blk:
                        @blk.vector
                        def _(v):
                            p.inc("v", v.tensor_scalar(
                                logit[:, :], psim[:, :],
                                rnorm[:, tc:tc + 1], None, ALU.mult))
                            p.wait(v, "v")
                            p.inc("v", v.tensor_add(logit[:, :], logit[:, :],
                                                    gv[:, :]))
                            p.wait(v, "v")
                            p.inc("v", v.tensor_reduce(nmx[:, :],
                                                       logit[:, :], AX.X,
                                                       ALU.max, negate=True))

                        @blk.scalar
                        def _(sc):
                            p.wait(sc, "v")
                            p.inc("s", sc.activation(
                                msk[:, :], logit[:, :], AF.Exp,
                                bias=nmx[:, :], accum_out=rsum[:, :]))

                    with nc.Block() as blk:
                        @blk.vector
                        def _(v):
                            p.inc("v", v.reciprocal(rsum[:, :], rsum[:, :]))
                            p.inc("v", v.tensor_reduce(t1c[:, :], logit[:, :],
                                                       AX.X, ALU.max))
                            p.wait(v, "v")
                            p.inc("v", v.tensor_scalar(
                                wcol_all[:, tc * NC_COL:(tc + 1) * NC_COL],
                                logit[:, :], t1c[:, :], None, ALU.is_ge))
                            p.wait(v, "v")
                            p.inc("v", v.tensor_mul(
                                wcol_all[:, tc * NC_COL:(tc + 1) * NC_COL],
                                wcol_all[:, tc * NC_COL:(tc + 1) * NC_COL],
                                msk[:, :]))
                            p.wait(v, "v")
                            p.inc("v", v.tensor_scalar(
                                wcol_all[:, tc * NC_COL:(tc + 1) * NC_COL],
                                wcol_all[:, tc * NC_COL:(tc + 1) * NC_COL],
                                rsum[:, :], None, ALU.mult))

        with contextlib.ExitStack() as e2:
            s2 = lambda n, sh, dt: e2.enter_context(
                nc.sbuf_tensor(uname(n), sh, dt))
            acc = s2("acc", [128, NF * T], F32)
            with nc.Block() as blk:
                @blk.gpsimd
                def _(g):
                    p.inc("g", g.memset(acc[:, :], 0.0))

            for grp in range(4):
                with contextlib.ExitStack() as e3:
                    s3 = lambda n, sh, dt: e3.enter_context(
                        nc.sbuf_tensor(uname(n), sh, dt))
                    cgel = s3("cgel", [128, 8 * T], BF16)
                    wcsb = s3("wcsb", [128, NF * 1024], BF16)
                    pjsb = s3("pjsb", [128, 8 * D], BF16)
                    wrow = s3("wrow", [1, T], BF16)
                    with nc.Block() as blk:
                        @blk.gpsimd
                        def _(g):
                            for cl in range(4):
                                load_w_bf16(
                                    p, g, wcsb, io["W_cols"], D, NN,
                                    dst_row_width=NF * 1024,
                                    dst_chunk_stride=1024,
                                    dst_off=cl * NN,
                                    src_off=(grp * 4 + cl) * D * NN)
                            load_w_bf16(p, g, pjsb, io["proj_w"], 1024, D,
                                        src_off=grp * 1024 * D)
                    gl_n = p.gl.n

                    with contextlib.ExitStack() as e4:
                        psc = e4.enter_context(nc.psum_tensor(
                            uname("psc"), [128, 2 * SW], F32))
                        jobs = [(cl, cc, ts) for cl in range(4)
                                for cc in range(2) for ts in range(NS)]
                        with nc.Block() as blk:
                            t0 = p.cnt["t"]
                            s0 = p.cnt["s"]

                            @blk.tensor
                            def _(t):
                                p.gl.wait(t, gl_n)
                                for i, (cl, cc, ts) in enumerate(jobs):
                                    if i >= 2:
                                        p.wait(t, "s", s0 + i - 1)
                                    last = None
                                    for fc in range(NF):
                                        last = t.matmul(
                                            psc[:, (i % 2) * SW:
                                                (i % 2 + 1) * SW],
                                            wcsb[:, fc * 1024 + cl * NN +
                                                 cc * 128:
                                                 fc * 1024 + cl * NN +
                                                 (cc + 1) * 128],
                                            xs[:, fc * T + ts * SW:
                                               fc * T + (ts + 1) * SW],
                                            start=(fc == 0),
                                            stop=(fc == NF - 1))
                                    p.inc("t", last)

                            @blk.scalar
                            def _(sc):
                                for i, (cl, cc, ts) in enumerate(jobs):
                                    p.wait(sc, "t", t0 + i + 1)
                                    c_abs = grp * 4 + cl
                                    bidx = c_abs * 2 + cc
                                    p.inc("s", sc.activation(
                                        cgel[:, (cl * 2 + cc) * T + ts * SW:
                                             (cl * 2 + cc) * T +
                                             (ts + 1) * SW],
                                        psc[:, (i % 2) * SW:
                                            (i % 2 + 1) * SW],
                                        AF.Gelu,
                                        bias=bcols_c[:, bidx:bidx + 1]))

                    for cl in range(4):
                        c_abs = grp * 4 + cl
                        with contextlib.ExitStack() as e4:
                            pwr = e4.enter_context(nc.psum_tensor(
                                uname("pwr"), [1, T], F32))
                            with nc.Block() as blk:
                                @blk.tensor
                                def _(t):
                                    last = None
                                    for tc in range(NT):
                                        last = t.matmul(
                                            pwr[0:1, tc * 128:(tc + 1) * 128],
                                            wcol_all[:, tc * NC_COL + c_abs:
                                                     tc * NC_COL + c_abs + 1],
                                            p.identF[:, :], start=True,
                                            stop=True)
                                    p.inc("t", last)
                            with nc.Block() as blk:
                                @blk.vector
                                def _(v):
                                    p.inc("v", v.tensor_copy(wrow[0:1, :],
                                                             pwr[0:1, :]))
                        with contextlib.ExitStack() as e4:
                            pwb = e4.enter_context(nc.psum_tensor(
                                uname("pwb"), [128, SW], F32))
                            for ts in range(NS):
                                with nc.Block() as blk:
                                    @blk.tensor
                                    def _(t):
                                        p.inc("t", t.matmul(
                                            pwb[:, :], p.onesBr[:, :],
                                            wrow[0:1, ts * SW:(ts + 1) * SW],
                                            start=True, stop=True))
                                with nc.Block() as blk:
                                    @blk.vector
                                    def _(v):
                                        for cc in range(2):
                                            sl = slice(
                                                (cl * 2 + cc) * T + ts * SW,
                                                (cl * 2 + cc) * T +
                                                (ts + 1) * SW)
                                            p.inc("v", v.tensor_mul(
                                                cgel[:, sl], cgel[:, sl],
                                                pwb[:, :]))

                    with contextlib.ExitStack() as e4:
                        psp = e4.enter_context(nc.psum_tensor(
                            uname("psp"), [128, 2 * SW], F32))
                        jobs = [(dc, ts) for dc in range(NF)
                                for ts in range(NS)]
                        with nc.Block() as blk:
                            t0 = p.cnt["t"]
                            v0 = p.cnt["v"]

                            @blk.tensor
                            def _(t):
                                for i, (dc, ts) in enumerate(jobs):
                                    if i >= 2:
                                        p.wait(t, "v", v0 + i - 1)
                                    last = None
                                    for hc in range(8):
                                        last = t.matmul(
                                            psp[:, (i % 2) * SW:
                                                (i % 2 + 1) * SW],
                                            pjsb[:, hc * D + dc * 128:
                                                 hc * D + (dc + 1) * 128],
                                            cgel[:, hc * T + ts * SW:
                                                 hc * T + (ts + 1) * SW],
                                            start=(hc == 0), stop=(hc == 7))
                                    p.inc("t", last)

                            @blk.vector
                            def _(v):
                                for i, (dc, ts) in enumerate(jobs):
                                    p.wait(v, "t", t0 + i + 1)
                                    sl = slice(dc * T + ts * SW,
                                               dc * T + (ts + 1) * SW)
                                    p.inc("v", v.tensor_add(
                                        acc[:, sl], acc[:, sl],
                                        psp[:, (i % 2) * SW:
                                            (i % 2 + 1) * SW]))

            with nc.Block() as blk:
                @blk.vector
                def _(v):
                    for fc in range(NF):
                        sl = slice(fc * T, (fc + 1) * T)
                        p.inc("v", v.tensor_scalar(
                            acc[:, sl], acc[:, sl], pb_c[:, fc:fc + 1],
                            None, ALU.add))
                        p.wait(v, "v")
                        p.inc("v", v.tensor_add(acc[:, sl], acc[:, sl],
                                                xs[:, sl]))

            emit_ln_fm(p, None, mode="ln",
                       src_fn=lambda fc: acc[:, fc * T:(fc + 1) * T],
                       gamma=lng_c, beta=lnb_c,
                       out_dram_fn=lambda fc: dap(
                           io["xout"], (b * D + fc * 128) * T,
                           [[T, 128], [1, T]]),
                       out_accum=True)


def build_l1_ffn_prog():
    nc = bass.Bass("TRN2", target_bir_lowering=False)
    xst = nc.dram_tensor("xst", [B_PER_CORE, D, T], F32,
                         kind="ExternalInput")
    xout = nc.dram_tensor("xout", [B_PER_CORE, D, T], F32,
                          kind="ExternalOutput")
    io = {"xst": xst, "xout": xout}
    for nm, shape in (("col_emb", [NC_COL, D]), ("gate_w1", [D, 512]),
                      ("gate_b1", [512]), ("gate_w2", [512, NC_COL]),
                      ("gate_b2", [NC_COL]), ("W_cols", [NC_COL, D, NN]),
                      ("b_cols", [NC_COL, NN]), ("proj_w", [4096, D]),
                      ("proj_b", [D]), ("ln_g", [D]), ("ln_b", [D])):
        io[nm] = nc.dram_tensor(nm, shape, F32, kind="ExternalInput")

    p = Prog(nc)
    emit_prelude(p)
    emit_copy_state(p, xout, xst, B_PER_CORE * D * T)
    with nc.Block() as blk:
        @blk.gpsimd
        def _(g):
            p.ld.wait_all(g)
    for b in range(B_PER_CORE):
        emit_l1_ffn(p, io, b)
    with nc.Block() as blk:
        @blk.sync
        def _(sy):
            p.ld.wait_all(sy)
            p.st.wait_all(sy)
            p.gl.wait_all(sy)
    p.close()
    return nc



def emit_wT_load(p, es, handles, dst_sb, n_mats):
    nc = p.nc
    rw = NF * n_mats * D
    with contextlib.ExitStack() as e2:
        wnat = e2.enter_context(nc.sbuf_tensor(uname("wnat"),
                                               [128, NF * D], BF16))
        pst = e2.enter_context(nc.psum_tensor(uname("pst"), [128, 128],
                                              BF16))
        for m, hd in enumerate(handles):
            with nc.Block() as blk:
                @blk.gpsimd
                def _(g):
                    load_w_bf16(p, g, wnat, hd, D, D)
            gl_n = p.gl.n
            for oc in range(NF):
                for fc in range(NF):
                    with nc.Block() as blk:
                        @blk.tensor
                        def _(t):
                            p.gl.wait(t, gl_n)
                            p.inc("t", t.transpose(
                                pst[:, :],
                                wnat[:, oc * D + fc * 128:
                                     oc * D + (fc + 1) * 128],
                                p.identB[:, :]))
                    with nc.Block() as blk:
                        @blk.vector
                        def _(v):
                            dst = dst_sb[:, fc * (n_mats * D) + m * D +
                                         oc * 128:
                                         fc * (n_mats * D) + m * D +
                                         (oc + 1) * 128]
                            p.inc("v", v.tensor_copy(dst, pst[:, :]))


def emit_l1_attn_b(p, io, b):
    nc = p.nc
    with contextlib.ExitStack() as es:
        sb = lambda n, sh, dt: es.enter_context(
            nc.sbuf_tensor(uname(n), sh, dt))
        q_all = sb("q_all", [128, NF * T], BF16)
        k_all = sb("k_all", [128, NF * T], BF16)
        v_sl = sb("v_sl", [128, NT * H * 65], BF16)
        oT = sb("xno_oT_b", [128, NF * T], BF16)
        slope_cols = sb("slope_cols", [128, H], F32)
        power_col = sb("power_col", [128, 1], F32)
        pfut_tile = sb("pfut_tile", [128, SW], BF16)
        woT = sb("woT", [128, NF * D], BF16)
        emit_vsl_ones(p, v_sl)

        with contextlib.ExitStack() as e2:
            s2 = lambda n, sh, dt: e2.enter_context(
                nc.sbuf_tensor(uname(n), sh, dt))
            srow = s2("srow", [1, H], BF16)
            prow = s2("prow", [1, 1], BF16)
            tmpr = s2("tmpr", [1, H], F32)
            tmpp = s2("tmpp", [1, 1], F32)
            ld0 = p.ld.n
            with nc.Block() as blk:
                @blk.sync
                def _(sy):
                    p.ld.start(sy, tmpr[0:1, :],
                               dap(io["alibi_slopes"], 0, [[H, 1], [1, H]]))
                    p.ld.start(sy, tmpp[0:1, :],
                               dap(io["power"], 0, [[1, 1], [1, 1]]))
            with nc.Block() as blk:
                @blk.vector
                def _(v):
                    p.ld.wait(v, ld0 + 32)
                    p.inc("v", v.tensor_copy(srow[0:1, :], tmpr[0:1, :]))
                    p.inc("v", v.tensor_copy(prow[0:1, :], tmpp[0:1, :]))
            with contextlib.ExitStack() as e3:
                pbs = e3.enter_context(nc.psum_tensor(uname("pbs"),
                                                      [128, H + 1], F32))
                with nc.Block() as blk:
                    @blk.tensor
                    def _(t):
                        p.inc("t", t.matmul(pbs[:, 0:H], p.onesBr[:, :],
                                            srow[0:1, :], start=True,
                                            stop=True))
                        p.inc("t", t.matmul(pbs[:, H:H + 1], p.onesBr[:, :],
                                            prow[0:1, :], start=True,
                                            stop=True))
                with nc.Block() as blk:
                    @blk.vector
                    def _(v):
                        p.inc("v", v.tensor_scalar(slope_cols[:, :],
                                                   pbs[:, 0:H], -1.0, None,
                                                   ALU.mult))
                        p.inc("v", v.tensor_copy(power_col[:, :],
                                                 pbs[:, H:H + 1]))

        with contextlib.ExitStack() as e2:
            s2 = lambda n, sh, dt: e2.enter_context(
                nc.sbuf_tensor(uname(n), sh, dt))
            prow2 = s2("prow2", [1, SW], F32)
            prow2b = s2("prow2b", [1, SW], BF16)
            pfp = e2.enter_context(nc.psum_tensor(uname("pfp"),
                                                  [128, SW], F32))
            with nc.Block() as blk:
                @blk.scalar
                def _(sc):
                    p.inc("s", sc.activation(
                        prow2[0:1, 0:1], power_col[0:1, 0:1], AF.Exp,
                        scale=-math.log(20002.0)))
            with nc.Block() as blk:
                @blk.gpsimd
                def _(g):
                    p.inc("g", g.memset(prow2b[0:1, :], 1.0))
            with nc.Block() as blk:
                @blk.vector
                def _(v):
                    p.inc("v", v.tensor_scalar(prow2b[0:1, :],
                                               prow2b[0:1, :],
                                               prow2[0:1, 0:1], None,
                                               ALU.mult))
            with nc.Block() as blk:
                @blk.tensor
                def _(t):
                    p.inc("t", t.matmul(pfp[:, :], p.onesBr[:, :],
                                        prow2b[0:1, :], start=True,
                                        stop=True))
            with nc.Block() as blk:
                @blk.vector
                def _(v):
                    p.inc("v", v.tensor_copy(pfut_tile[:, :], pfp[:, :]))

        with contextlib.ExitStack() as e2:
            xnorm = oT
            emit_ln_fm(p, xnorm, src_dram=io["xst"], src_dram_off=b * D * T)
            with contextlib.ExitStack() as e3:
                wqkT = e3.enter_context(
                    nc.sbuf_tensor(uname("wqkT"), [128, NF * 2 * D], BF16))
                emit_wT_load(p, e3, (io["Wq"], io["Wk"]), wqkT, 2)
                emit_qkv_heads(p, xnorm, wqkT, 2 * D, 0, D, None,
                               q_all, k_all, v_sl, qscale=2.0, do_v=False)
            with contextlib.ExitStack() as e3:
                wvT = e3.enter_context(
                    nc.sbuf_tensor(uname("wvT"), [128, NF * D], BF16))
                emit_wT_load(p, e3, (io["Wv"],), wvT, 1)
                emit_qkv_heads(p, xnorm, wvT, D, None, None, 0,
                               q_all, k_all, v_sl, do_qk=False)
        emit_wT_load(p, es, (io["Wo"],), woT, 1)

        with contextlib.ExitStack() as e2:
            s2 = lambda n, sh, dt: e2.enter_context(
                nc.sbuf_tensor(uname(n), sh, dt))
            aslab = s2("aslab", [128, NT * SW], F32)
            probsT = [s2(f"probsT{i}", [128, SW], BF16) for i in range(2)]
            sc_sb = [s2(f"scsb{i}", [128, SW], F32) for i in range(2)]
            r_sb = [s2(f"rsb{i}", [128, SW], F32) for i in range(2)]
            a_sb = [s2(f"asb{i}", [128, SW], F32) for i in range(2)]
            d_sb = [s2(f"dsb{i}", [128, SW], F32) for i in range(2)]
            rrow = s2("rrow", [1, SW], BF16)
            otmp = s2("otmp", [64, SW], F32)

            for st in range(NS):
                ncz = min((st + 1) * (SW // 128), NT)
                with nc.Block() as blk:
                    @blk.gpsimd
                    def _(g):
                        for c in range(ncz):
                            p.inc("g", g.iota(
                                aslab[:, c * SW:(c + 1) * SW], [[-1, SW]],
                                base=c * 128 - st * SW,
                                channel_multiplier=1,
                                allow_small_or_imprecise_dtypes=True))
                with nc.Block() as blk:
                    @blk.scalar
                    def _(sc):
                        for c in range(ncz):
                            p.inc("s", sc.activation(
                                aslab[:, c * SW:(c + 1) * SW],
                                aslab[:, c * SW:(c + 1) * SW], AF.Abs))

                for h in range(H):
                    rbase = (h % 2) * 64
                    grp = h // 2
                    qsl = q_all[rbase:rbase + 64,
                                grp * T + st * SW:grp * T + (st + 1) * SW]
                    with contextlib.ExitStack() as e3:
                        psB = e3.enter_context(nc.psum_tensor(
                            uname("psB"), [128, 2 * SW], F32))
                        psPV = e3.enter_context(nc.psum_tensor(
                            uname("psPV"), [65, SW], F32))
                        pbc = e3.enter_context(nc.psum_tensor(
                            uname("pbc"), [64, SW], F32))

                        with nc.Block() as blk:
                            t0 = p.cnt["t"]
                            v0 = p.cnt["v"]
                            g0 = p.cnt["g"]
                            s0 = p.cnt["s"]

                            seq = [("mmB", 0), ("mmB", 1)][:min(2, ncz)]
                            for c in range(2, ncz):
                                seq.append(("PV", c - 2))
                                seq.append(("mmB", c))
                            for j in range(max(0, ncz - 2), NT):
                                seq.append(("PV", j))
                            t_at = {}

                            @blk.tensor
                            def _(t):
                                for kind, c in seq:
                                    if kind == "mmB":
                                        if c >= 2:
                                            p.wait(t, "v",
                                                   v0 + (c - 2) * 4 + 1)
                                        p.inc("t", t.matmul(
                                            psB[:, (c % 2) * SW:
                                                (c % 2 + 1) * SW],
                                            k_all[rbase:rbase + 64,
                                                  grp * T + c * 128:
                                                  grp * T + (c + 1) * 128],
                                            qsl, start=True, stop=True))
                                    else:
                                        if c < ncz:
                                            p.wait(t, "s", s0 + c * 3 + 3)
                                            rhs = probsT[c % 2][:, :]
                                        else:
                                            rhs = pfut_tile[:, :]
                                        p.inc("t", t.matmul(
                                            psPV[:, :],
                                            v_sl[:, c * (H * 65) + h * 65:
                                                 c * (H * 65) + h * 65 + 65],
                                            rhs, start=(c == 0),
                                            stop=(c == NT - 1)))
                                    t_at[(kind, c)] = p.cnt["t"]

                            @blk.vector
                            def _(v):
                                for c in range(ncz):
                                    p.wait(v, "t", t_at[("mmB", c)])
                                    if c >= 2:
                                        p.wait(v, "g", g0 + c - 1)
                                    p.inc("v", v.scalar_tensor_tensor(
                                        sc_sb[c % 2][:, :],
                                        aslab[:, c * SW:(c + 1) * SW],
                                        slope_cols[:, h:h + 1],
                                        psB[:, (c % 2) * SW:
                                            (c % 2 + 1) * SW],
                                        ALU.mult, ALU.add))
                                    p.wait(v, "g", g0 + c + 1)
                                    p.inc("v", v.tensor_scalar(
                                        r_sb[c % 2][:, :],
                                        sc_sb[c % 2][:, :], 0.0, None,
                                        ALU.max))
                                    p.wait(v, "v")
                                    p.inc("v", v.scalar_tensor_tensor(
                                        a_sb[c % 2][:, :],
                                        r_sb[c % 2][:, :], 2.0,
                                        sc_sb[c % 2][:, :], ALU.mult,
                                        ALU.subtract))
                                    p.wait(v, "s", s0 + c * 3 + 2)
                                    p.inc("v", v.tensor_sub(
                                        d_sb[c % 2][:, :],
                                        r_sb[c % 2][:, :],
                                        a_sb[c % 2][:, :]))

                            @blk.gpsimd
                            def _(g):
                                for c in range(ncz):
                                    p.wait(g, "v", v0 + c * 4 + 1)
                                    p.inc("g", g.affine_select(
                                        sc_sb[c % 2][:, :],
                                        sc_sb[c % 2][:, :], [[1, SW]],
                                        ALU.is_ge, p.r_m10k,
                                        base=st * SW - c * 128,
                                        channel_multiplier=-1))

                            @blk.scalar
                            def _(sc):
                                for c in range(ncz):
                                    p.wait(sc, "v", v0 + c * 4 + 3)
                                    p.inc("s", sc.activation(
                                        r_sb[c % 2][:, :],
                                        r_sb[c % 2][:, :], AF.Ln,
                                        scale=2.0, bias=1.0))
                                    p.inc("s", sc.activation(
                                        a_sb[c % 2][:, :],
                                        a_sb[c % 2][:, :], AF.Ln,
                                        scale=2.0, bias=2.0))
                                    p.wait(sc, "v", v0 + c * 4 + 4)
                                    if c >= 2:
                                        p.wait(sc, "t",
                                               t_at[("PV", c - 2)])
                                    p.inc("s", sc.activation(
                                        probsT[c % 2][:, :],
                                        d_sb[c % 2][:, :], AF.Exp,
                                        scale=power_col[:, :]))

                        with nc.Block() as blk:
                            @blk.scalar
                            def _(sc):
                                p.inc("s", sc.activation(
                                    rrow[0:1, :], psPV[64:65, :], AF.Ln,
                                    bias=1e-6))
                                p.wait(sc, "s")
                                p.inc("s", sc.activation(
                                    rrow[0:1, :], rrow[0:1, :], AF.Exp,
                                    scale=-1.0))
                                p.wait(sc, "s")
                                p.inc("s", sc.activation(
                                    otmp[:, :], psPV[0:64, :], AF.Copy))

                        with nc.Block() as blk:
                            @blk.tensor
                            def _(t):
                                p.inc("t", t.matmul(
                                    pbc[:, :], p.onesBr[0:1, 0:64],
                                    rrow[0:1, :], start=True, stop=True))

                        with nc.Block() as blk:
                            @blk.vector
                            def _(v):
                                dst = oT[rbase:rbase + 64,
                                         grp * T + st * SW:
                                         grp * T + (st + 1) * SW]
                                p.inc("v", v.tensor_mul(dst, otmp[:, :],
                                                        pbc[:, :]))

        emit_oproj(p, None, None, oT, io["xout"], b, w_sb=woT)


def build_l1_attn_b_prog():
    nc = bass.Bass("TRN2", target_bir_lowering=False)
    xst = nc.dram_tensor("xst", [B_PER_CORE, D, T], F32,
                         kind="ExternalInput")
    xout = nc.dram_tensor("xout", [B_PER_CORE, D, T], F32,
                          kind="ExternalOutput")
    io = {"xst": xst, "xout": xout}
    for nm, shape in (("Wq", [D, D]), ("Wk", [D, D]), ("Wv", [D, D]),
                      ("Wo", [D, D]), ("alibi_slopes", [H]),
                      ("power", [1])):
        io[nm] = nc.dram_tensor(nm, shape, F32, kind="ExternalInput")
    p = Prog(nc)
    emit_prelude(p)
    register_consts(p, [2.0, 1e-6])
    emit_copy_state(p, xout, xst, B_PER_CORE * D * T)
    with nc.Block() as blk:
        @blk.gpsimd
        def _(g):
            p.ld.wait_all(g)
    for b in range(B_PER_CORE):
        emit_l1_attn_b(p, io, b)
    with nc.Block() as blk:
        @blk.sync
        def _(sy):
            p.ld.wait_all(sy)
            p.st.wait_all(sy)
            p.gl.wait_all(sy)
    p.close()
    return nc



PD = 4
_PI_IDX = [0, 0, 0, 1, 1, 2]
_PJ_IDX = [1, 2, 3, 2, 3, 3]
_J6_SIGN = [1.0, -1.0, 1.0, 1.0, -1.0, 1.0]


def emit_l1_attn_a(p, io, b):
    nc = p.nc
    scale = 1.0 / math.sqrt(DH)
    with contextlib.ExitStack() as es:
        sb = lambda n, sh, dt: es.enter_context(
            nc.sbuf_tensor(uname(n), sh, dt))
        q_all = sb("q_all", [128, NF * T], BF16)
        k_all = sb("k_all", [128, NF * T], BF16)
        v_sl = sb("v_sl", [128, NT * H * 65], BF16)
        gmul = sb("gmul", [128, NT * H], F32)
        bq_c, _ = emit_vec_cols(p, es, io["qkv_b"], 3 * D)
        bo_c, _ = emit_vec_cols(p, es, io["out_b"], D)
        bqs_c = sb("bqs_c", [128, 8], F32)
        kvbb = sb("kvbb", [128, 2 * D], F32)
        emit_vsl_ones(p, v_sl)

        with nc.Block() as blk:
            @blk.vector
            def _(v):
                p.inc("v", v.tensor_scalar(bqs_c[:, :], bq_c[:, 0:8],
                                           scale, None, ALU.mult))
        emit_vec_bcast(p, kvbb, io["qkv_b"], 2 * D, src_off=D)

        with contextlib.ExitStack() as e2:
            xnorm = e2.enter_context(
                nc.sbuf_tensor(uname("xnorm"), [128, NF * T], BF16))
            emit_ln_fm(p, xnorm, src_dram=io["xst"], src_dram_off=b * D * T)
            es_mem = e2.enter_context(contextlib.ExitStack())
            sbm = lambda n, sh, dt: es_mem.enter_context(
                nc.sbuf_tensor(uname(n), sh, dt))
            lndc = sbm("lndc", [128, H], F32)
            lnd2c = sbm("lnd2c", [128, H], F32)
            acol = sbm("acol", [128, H], F32)
            msc = sbm("msc", [128, H], F32)
            kvgc = sbm("kvgc", [128, 1], F32)
            dcols = sbm("dcols", [128, H * 4], F32)
            doffb = sbm("doffb", [128, H * 16], F32)
            pcol = sbm("pcol", [128, 1], F32)
            with contextlib.ExitStack() as e2:
                s2 = lambda n, sh, dt: e2.enter_context(
                    nc.sbuf_tensor(uname(n), sh, dt))
                dr = s2("dr", [1, H], F32)
                msr = s2("msr", [1, H], F32)
                kvr = s2("kvr", [1, 1], F32)
                drb = s2("drb", [1, H], BF16)
                msrb = s2("msrb", [1, H], BF16)
                kvrb = s2("kvrb", [1, 1], BF16)
                offr = s2("offr", [1, 16], F32)
                pbs = e2.enter_context(nc.psum_tensor(uname("pbs"),
                                                      [128, 2 * H + 1], F32))
                ld0 = p.ld.n
                with nc.Block() as blk:
                    @blk.sync
                    def _(sy):
                        p.ld.start(sy, dr[0:1, :],
                                   dap(io["decay_logits"], 0, [[H, 1], [1, H]]))
                        p.ld.start(sy, msr[0:1, :],
                                   dap(io["mem_scale"], 0, [[H, 1], [1, H]]))
                        p.ld.start(sy, kvr[0:1, :],
                                   dap(io["kv_gate"], 0, [[1, 1], [1, 1]]))

                    @blk.gpsimd
                    def _(g):
                        g.iota(offr[0:1, :], [[128, 16]], base=1,
                               channel_multiplier=0,
                               allow_small_or_imprecise_dtypes=True)
                        g.iota(pcol[:, :], [[0, 1]], base=0,
                               channel_multiplier=1,
                               allow_small_or_imprecise_dtypes=True)

                with nc.Block() as blk:
                    @blk.scalar
                    def _(sc):
                        p.ld.wait(sc, ld0 + 48)
                        p.inc("s", sc.activation(dr[0:1, :], dr[0:1, :],
                                                 AF.Sigmoid))
                        p.wait(sc, "s")
                        p.inc("s", sc.activation(dr[0:1, :], dr[0:1, :],
                                                 AF.Ln))

                with nc.Block() as blk:
                    @blk.vector
                    def _(v):
                        p.inc("v", v.tensor_copy(drb[0:1, :], dr[0:1, :]))
                        p.inc("v", v.tensor_copy(msrb[0:1, :], msr[0:1, :]))
                        p.inc("v", v.tensor_copy(kvrb[0:1, :], kvr[0:1, :]))
                with nc.Block() as blk:
                    @blk.tensor
                    def _(t):
                        p.inc("t", t.matmul(pbs[:, 0:H], p.onesBr[:, :],
                                            drb[0:1, :], start=True, stop=True))
                        p.inc("t", t.matmul(pbs[:, H:2 * H], p.onesBr[:, :],
                                            msrb[0:1, :], start=True, stop=True))
                        p.inc("t", t.matmul(pbs[:, 2 * H:2 * H + 1],
                                            p.onesBr[:, :], kvrb[0:1, :],
                                            start=True, stop=True))
                with nc.Block() as blk:
                    @blk.vector
                    def _(v):
                        p.inc("v", v.tensor_copy(lndc[:, :], pbs[:, 0:H]))
                        p.inc("v", v.tensor_copy(msc[:, :], pbs[:, H:2 * H]))
                        p.inc("v", v.tensor_copy(kvgc[:, :],
                                                 pbs[:, 2 * H:2 * H + 1]))
                        p.wait(v, "v")
                        p.inc("v", v.tensor_scalar(lnd2c[:, :], lndc[:, :], 0.5,
                                                   None, ALU.mult))

                with nc.Block() as blk:
                    @blk.scalar
                    def _(sc):
                        for h in range(H):
                            p.inc("s", sc.activation(
                                acol[:, h:h + 1], pcol[:, :], AF.Exp,
                                scale=lndc[:, h:h + 1]))
                            for tm in range(4):
                                pass

                with contextlib.ExitStack() as e3:
                    ecol = e3.enter_context(nc.sbuf_tensor(uname("ecol"),
                                                           [128, 4], F32))
                    with nc.Block() as blk:
                        @blk.gpsimd
                        def _(g):
                            for tm in range(4):
                                g.iota(ecol[:, tm:tm + 1], [[0, 1]],
                                       base=511 - tm * 128,
                                       channel_multiplier=-1,
                                       allow_small_or_imprecise_dtypes=True)
                    with nc.Block() as blk:
                        @blk.scalar
                        def _(sc):
                            for h in range(H):
                                for tm in range(4):
                                    p.inc("s", sc.activation(
                                        dcols[:, h * 4 + tm:h * 4 + tm + 1],
                                        ecol[:, tm:tm + 1], AF.Exp,
                                        scale=lnd2c[:, h:h + 1]))
                    with contextlib.ExitStack() as e4:
                        dofr = e4.enter_context(nc.sbuf_tensor(
                            uname("dofr"), [1, H * 16], BF16))
                        pdo = e4.enter_context(nc.psum_tensor(
                            uname("pdo"), [128, H * 16], F32))
                        with nc.Block() as blk:
                            @blk.scalar
                            def _(sc):
                                for h in range(H):
                                    p.inc("s", sc.activation(
                                        dofr[0:1, h * 16:(h + 1) * 16],
                                        offr[0:1, :], AF.Exp,
                                        scale=lndc[0:1, h:h + 1]))
                        with nc.Block() as blk:
                            @blk.tensor
                            def _(t):
                                for j in range(H * 16 // 128):
                                    p.inc("t", t.matmul(
                                        pdo[:, j * 128:(j + 1) * 128],
                                        p.onesBr[:, :],
                                        dofr[0:1, j * 128:(j + 1) * 128],
                                        start=True, stop=True))
                        with nc.Block() as blk:
                            @blk.vector
                            def _(v):
                                p.inc("v", v.tensor_copy(doffb[:, :],
                                                         pdo[:, :]))


            with contextlib.ExitStack() as e3:
                wqq = e3.enter_context(
                    nc.sbuf_tensor(uname("wqq_sb"), [128, NF * D], BF16))
                with nc.Block() as blk:
                    @blk.gpsimd
                    def _(g):
                        load_w_bf16(p, g, wqq, io["qkv_w"], D, D,
                                    src_row_stride=3 * D)
                gl_n = p.gl.n
                psq = e3.enter_context(nc.psum_tensor(uname("psq"),
                                                      [64, T], F32))
                for h in range(H):
                    rbase = (h % 2) * 64
                    grp = h // 2
                    with nc.Block() as blk:
                        @blk.tensor
                        def _(t):
                            p.gl.wait(t, gl_n)
                            last = None
                            for fc in range(NF):
                                lhsT = wqq[:, fc * D + h * 64:
                                           fc * D + h * 64 + 64]
                                for ts in range(NS):
                                    sl = slice(ts * SW, (ts + 1) * SW)
                                    last = t.matmul(
                                        psq[:, sl], lhsT,
                                        xnorm[:, fc * T + ts * SW:
                                              fc * T + (ts + 1) * SW],
                                        start=(fc == 0), stop=(fc == NF - 1))
                            p.inc("t", last)
                    with nc.Block() as blk:
                        @blk.scalar
                        def _(sc):
                            r, c = (h * 64) % 128, (h * 64) // 128
                            p.inc("s", sc.activation(
                                q_all[rbase:rbase + 64,
                                      grp * T:(grp + 1) * T],
                                psq[:, :], AF.Identity,
                                bias=bqs_c[r:r + 64, c:c + 1], scale=scale))

            jwb = es_mem.enter_context(
                nc.sbuf_tensor(uname("jw_b"), [128, NT * 96], BF16))
            rlb = es_mem.enter_context(
                nc.sbuf_tensor(uname("rl_b"), [128, NT * 96], BF16))
            es_ln = es_mem.enter_context(contextlib.ExitStack())
            wln = es_ln.enter_context(
                nc.sbuf_tensor(uname("wline_sb"), [128, NF * 4 * 64], BF16))
            with nc.Block() as blk:
                @blk.gpsimd
                def _(g):
                    for m, hd in enumerate((io["w1_write"], io["w2_write"],
                                            io["w1_read"], io["w2_read"])):
                        load_w_bf16(p, g, wln, hd, D, 64,
                                    dst_row_width=NF * 256,
                                    dst_chunk_stride=256, dst_off=m * 64)
            gl_n = p.gl.n
            wl = es_ln.enter_context(
                nc.sbuf_tensor(uname("wl_slab"), [128, NT * 256], F32))
            stg8 = es_ln.enter_context(
                nc.sbuf_tensor(uname("stg8"), [128, NF * 128], BF16))
            with nc.Block() as blk:
                @blk.gpsimd
                def _(g):
                    for fc in range(NF):
                        p.inc("g", g.memset(
                            stg8[:, fc * 128:fc * 128 + 1], 0.0))

                @blk.vector
                def _(v):
                    for fc in range(NF):
                        p.inc("v", v.tensor_copy(
                            stg8[:, fc * 128 + 1:(fc + 1) * 128],
                            xnorm[:, fc * T:fc * T + 127]))

            with contextlib.ExitStack() as e3:
                psl = e3.enter_context(nc.psum_tensor(uname("psl"),
                                                      [128, 2 * 512], F32))
                with nc.Block() as blk:
                    t0 = p.cnt["t"]
                    v0 = p.cnt["v"]

                    @blk.tensor
                    def _(t):
                        p.gl.wait(t, gl_n)
                        for tc in range(NT):
                            if tc >= 2:
                                p.wait(t, "v", v0 + tc - 1)
                            dst = psl[:, (tc % 2) * 512:(tc % 2) * 512 + 256]
                            last = None
                            for fc in range(NF):
                                last = t.matmul(
                                    dst[:, 64:256],
                                    xnorm[:, fc * T + tc * 128:
                                          fc * T + (tc + 1) * 128],
                                    wln[:, fc * 256 + 64:fc * 256 + 256],
                                    start=(fc == 0), stop=(fc == NF - 1))
                            for fc in range(NF):
                                lhsT = (stg8[:, fc * 128:(fc + 1) * 128]
                                        if tc == 0 else
                                        xnorm[:, fc * T + tc * 128 - 1:
                                              fc * T + (tc + 1) * 128 - 1])
                                last = t.matmul(
                                    dst[:, 0:64], lhsT,
                                    wln[:, fc * 256:fc * 256 + 64],
                                    start=(fc == 0), stop=(fc == NF - 1))
                            p.inc("t", last)

                    @blk.vector
                    def _(v):
                        for tc in range(NT):
                            p.wait(v, "t", t0 + tc + 1)
                            p.inc("v", v.tensor_copy(
                                wl[:, tc * 256:(tc + 1) * 256],
                                psl[:, (tc % 2) * 512:
                                    (tc % 2) * 512 + 256]))

            wli = es_ln.enter_context(
                nc.sbuf_tensor(uname("wlines"), [128, NT * 96], F32))
            rli = es_ln.enter_context(
                nc.sbuf_tensor(uname("rlines"), [128, NT * 96], F32))
            with contextlib.ExitStack() as e3:
                s3 = lambda n, sh, dt: e3.enter_context(
                    nc.sbuf_tensor(uname(n), sh, dt))
                tmp = s3("ext_tmp", [128, NT * 96], F32)
                nsq = s3("ext_nsq", [128, NT * 16], F32)
                for (dst, o1, o2) in ((wli, 0, 64), (rli, 128, 192)):
                    with nc.Block() as blk:
                        @blk.vector
                        def _(v):
                            for i in range(6):
                                w1ap = dap(wl, o1 + _PI_IDX[i],
                                           [[NT * 256, 128], [256, NT],
                                            [4, 16]])
                                w2ap = dap(wl, o2 + _PJ_IDX[i],
                                           [[NT * 256, 128], [256, NT],
                                            [4, 16]])
                                w1bp = dap(wl, o1 + _PJ_IDX[i],
                                           [[NT * 256, 128], [256, NT],
                                            [4, 16]])
                                w2bp = dap(wl, o2 + _PI_IDX[i],
                                           [[NT * 256, 128], [256, NT],
                                            [4, 16]])
                                oap = dap(dst, i,
                                          [[NT * 96, 128], [96, NT],
                                           [6, 16]])
                                tap = dap(tmp, i,
                                          [[NT * 96, 128], [96, NT],
                                           [6, 16]])
                                p.inc("v", v.tensor_mul(oap, w1ap, w2ap))
                                p.inc("v", v.tensor_mul(tap, w1bp, w2bp))
                            p.wait(v, "v")
                            p.inc("v", v.tensor_sub(dst[:, :], dst[:, :],
                                                    tmp[:, :]))

                for (srcl, dstb, scaled) in ((wli, jwb, True),
                                             (rli, rlb, False)):
                    with nc.Block() as blk:
                        @blk.scalar
                        def _(sc):
                            p.inc("s", sc.activation(tmp[:, :], srcl[:, :],
                                                     AF.Square))
                    with nc.Block() as blk:
                        @blk.vector
                        def _(v):
                            p.inc("v", v.tensor_reduce(
                                nsq[:, :],
                                dap(tmp, 0, [[NT * 96, 128], [6, NT * 16],
                                             [1, 6]]),
                                AX.X, ALU.add))
                            p.wait(v, "v")
                            p.inc("v", v.tensor_scalar(
                                nsq[:, :], nsq[:, :], 1e-24, None, ALU.max))

                    with nc.Block() as blk:
                        @blk.scalar
                        def _(sc):
                            p.inc("s", sc.activation(nsq[:, :], nsq[:, :],
                                                     AF.Ln))
                            p.wait(sc, "s")
                            p.inc("s", sc.activation(nsq[:, :], nsq[:, :],
                                                     AF.Exp, scale=-0.5))
                    with nc.Block() as blk:
                        @blk.vector
                        def _(v):
                            if not scaled:
                                p.inc("v", v.tensor_mul(
                                    dap(dstb, 0, [[NT * 96, 128], [96, NT],
                                                  [6, 16], [1, 6]]),
                                    dap(srcl, 0, [[NT * 96, 128], [96, NT],
                                                  [6, 16], [1, 6]]),
                                    dap(nsq, 0, [[NT * 16, 128], [16, NT],
                                                 [1, 16], [0, 6]])))
                            else:
                                for i in range(6):
                                    p.inc("v", v.scalar_tensor_tensor(
                                        dap(dstb, i,
                                            [[NT * 96, 128], [96, NT],
                                             [6, 16]]),
                                        dap(srcl, 5 - i,
                                            [[NT * 96, 128], [96, NT],
                                             [6, 16]]),
                                        _J6_SIGN[i],
                                        dap(nsq, 0,
                                            [[NT * 16, 128], [16, NT],
                                             [1, 16]]),
                                        ALU.mult, ALU.mult))
            es_ln.close()

            rlT = es_mem.enter_context(nc.sbuf_tensor(uname("rlT"),
                                                      [6, T], BF16))
            jwT = es_mem.enter_context(nc.sbuf_tensor(uname("jwT"),
                                                      [6, T], BF16))
            jwsT = es_mem.enter_context(nc.sbuf_tensor(uname("jwsT"),
                                                       [6, T], BF16))
            jwtmp = es_mem.enter_context(nc.sbuf_tensor(uname("jwtmp"),
                                                        [128, 6], BF16))
            mem = es_mem.enter_context(nc.sbuf_tensor(uname("mem"),
                                                      [128, NT], F32))
            c1c = es_mem.enter_context(nc.sbuf_tensor(uname("c1c"),
                                                      [128, 1], F32))
            c2c = es_mem.enter_context(nc.sbuf_tensor(uname("c2c"),
                                                      [128, 1], F32))
            junk = es_mem.enter_context(nc.sbuf_tensor(uname("junk"),
                                                       [128, SW], BF16))
            t1s = es_mem.enter_context(nc.sbuf_tensor(uname("t1s"),
                                                      [128, SW], BF16))
            wdg = es_mem.enter_context(nc.sbuf_tensor(uname("wdg"),
                                                      [128, SW], BF16))
            dmast = es_mem.enter_context(nc.sbuf_tensor(uname("dmast"),
                                                        [128, 4 * SW],
                                                        F32))
            with nc.Block() as blk:
                @blk.gpsimd
                def _(g):
                    for tm in range(4):
                        p.inc("g", g.iota(
                            dmast[:, tm * SW:(tm + 1) * SW], [[-1, SW]],
                            base=tm * 128, channel_multiplier=1,
                            allow_small_or_imprecise_dtypes=True))

            for h in range(H):
                rbase = (h % 2) * 64
                grp = h // 2
                with contextlib.ExitStack() as e3:
                    ptr = e3.enter_context(nc.psum_tensor(uname("ptr"),
                                                          [6, 128], BF16))
                    for tc in range(NT):
                        with nc.Block() as blk:
                            @blk.vector
                            def _(v):
                                p.inc("v", v.tensor_scalar(
                                    jwtmp[:, :],
                                    jwb[:, tc * 96 + h * 6:
                                        tc * 96 + h * 6 + 6],
                                    dcols[:, h * 4 + (tc % 4):
                                          h * 4 + (tc % 4) + 1],
                                    None, ALU.mult))

                            @blk.tensor
                            def _(t):
                                p.wait(t, "v")
                                p.inc("t", t.transpose(
                                    ptr[:, :],
                                    rlb[:, tc * 96 + h * 6:
                                        tc * 96 + h * 6 + 6],
                                    p.identB[:, :]))
                        with nc.Block() as blk:
                            @blk.vector
                            def _(v):
                                p.inc("v", v.tensor_copy(
                                    rlT[:, tc * 128:(tc + 1) * 128],
                                    ptr[:, :]))
                        with nc.Block() as blk:
                            @blk.tensor
                            def _(t):
                                p.inc("t", t.transpose(
                                    ptr[:, :],
                                    jwb[:, tc * 96 + h * 6:
                                        tc * 96 + h * 6 + 6],
                                    p.identB[:, :]))
                        with nc.Block() as blk:
                            @blk.vector
                            def _(v):
                                p.inc("v", v.tensor_copy(
                                    jwT[:, tc * 128:(tc + 1) * 128],
                                    ptr[:, :]))
                        with nc.Block() as blk:
                            @blk.tensor
                            def _(t):
                                p.inc("t", t.transpose(
                                    ptr[:, :], jwtmp[:, :], p.identB[:, :]))
                        with nc.Block() as blk:
                            @blk.vector
                            def _(v):
                                p.inc("v", v.tensor_copy(
                                    jwsT[:, tc * 128:(tc + 1) * 128],
                                    ptr[:, :]))

                with nc.Block() as blk:
                    @blk.gpsimd
                    def _(g):
                        p.inc("g", g.memset(mem[:, :], 0.0))

                with contextlib.ExitStack() as e3:
                    psi = e3.enter_context(nc.psum_tensor(uname("psi"),
                                                          [128, SW], F32))
                    for ti in range(NT):
                        for sc_ in range(ti // (SW // 128) + 1):
                            past = (sc_ + 1) * SW <= ti * 128
                            with nc.Block() as blk:
                                @blk.tensor
                                def _(t):
                                    p.inc("t", t.matmul(
                                        psi[:, :],
                                        rlT[:, ti * 128:(ti + 1) * 128],
                                        (jwsT if past else jwT)[:,
                                            sc_ * SW:(sc_ + 1) * SW],
                                        start=True, stop=True))
                            if past:
                                koff = ti - (SW // 128) * sc_ - (SW // 128)
                                with nc.Block() as blk:
                                    @blk.scalar
                                    def _(scx):
                                        p.inc("s", scx.activation(
                                            junk[:, :], psi[:, :],
                                            AF.Square,
                                            accum_out=c1c[:, :]))
                                with nc.Block() as blk:
                                    @blk.vector
                                    def _(v):
                                        p.inc("v", v.tensor_scalar(
                                            c2c[:, :], c1c[:, :],
                                            acol[:, h:h + 1], None,
                                            ALU.mult))
                                        p.wait(v, "v")
                                        p.inc("v", v.scalar_tensor_tensor(
                                            mem[:, ti:ti + 1], c2c[:, :],
                                            doffb[:, h * 16 + koff:
                                                  h * 16 + koff + 1],
                                            mem[:, ti:ti + 1],
                                            ALU.mult, ALU.add))
                            else:
                                tm = ti % (SW // 128)
                                with nc.Block() as blk:
                                    @blk.scalar
                                    def _(scx):
                                        p.inc("s", scx.activation(
                                            wdg[:, :],
                                            dmast[:, tm * SW:
                                                  (tm + 1) * SW],
                                            AF.Exp,
                                            scale=lndc[:, h:h + 1]))

                                    @blk.gpsimd
                                    def _(g):
                                        p.wait(g, "s")
                                        p.inc("g", g.affine_select(
                                            wdg[:, :], wdg[:, :],
                                            [[-1, SW]], ALU.is_gt, 0.0,
                                            base=tm * 128,
                                            channel_multiplier=1))
                                with nc.Block() as blk:
                                    @blk.vector
                                    def _(v):
                                        p.inc("v", v.tensor_mul(
                                            t1s[:, :], wdg[:, :],
                                            psi[:, :]))
                                with nc.Block() as blk:
                                    @blk.vector
                                    def _(v):
                                        p.inc("v", v.scalar_tensor_tensor(
                                            junk[:, :], t1s[:, :], 1.0,
                                            psi[:, :], ALU.mult, ALU.mult,
                                            accum_out=c1c[:, :]))
                                        p.wait(v, "v")
                                        p.inc("v", v.tensor_add(
                                            mem[:, ti:ti + 1],
                                            mem[:, ti:ti + 1], c1c[:, :]))

                with nc.Block() as blk:
                    @blk.vector
                    def _(v):
                        p.inc("v", v.tensor_scalar(
                            mem[:, :], mem[:, :], msc[:, h:h + 1], None,
                            ALU.mult))

                    @blk.scalar
                    def _(scx):
                        p.wait(scx, "v")
                        p.inc("s", scx.activation(mem[:, :], mem[:, :],
                                                  AF.Sigmoid))
                with nc.Block() as blk:
                    @blk.vector
                    def _(v):
                        for tc in range(NT):
                            p.inc("v", v.tensor_scalar(
                                gmul[:, tc * H + h:tc * H + h + 1],
                                mem[:, tc:tc + 1], kvgc[:, :], None,
                                ALU.mult))

            es_mem.close()
            with contextlib.ExitStack() as e3:
                s3 = lambda n, sh, dt: e3.enter_context(
                    nc.sbuf_tensor(uname(n), sh, dt))
                wkv = s3("wkv_sb", [128, NF * 2 * D], BF16)
                wgeo = s3("wgeo_sb", [128, NF * 2 * D], BF16)
                with nc.Block() as blk:
                    @blk.gpsimd
                    def _(g):
                        load_w_bf16(p, g, wkv, io["qkv_w"], D, 2 * D,
                                    src_row_stride=3 * D, src_off=D)
                        load_w_bf16(p, g, wgeo, io["kv_geo_w"], D, 2 * D)
                gl_n2 = p.gl.n
                kvt = s3("kvt", [128, 128], F32)
                geo = s3("geo", [128, 128], F32)
                ken = s3("ken", [128, 64], BF16)
                pskv = e3.enter_context(nc.psum_tensor(uname("pskv"),
                                                       [128, 128], F32))
                psge = e3.enter_context(nc.psum_tensor(uname("psge"),
                                                       [128, 128], F32))
                pkt = e3.enter_context(nc.psum_tensor(uname("pkt"),
                                                      [64, 128], BF16))
                for h in range(H):
                    rbase = (h % 2) * 64
                    grp = h // 2
                    for tc in range(NT):
                        with nc.Block() as blk:
                            @blk.tensor
                            def _(t):
                                p.gl.wait(t, gl_n2)
                                last = None
                                for fc in range(NF):
                                    rhs = dap(wkv, fc * 2 * D + h * 64,
                                              [[NF * 2 * D, 128], [D, 2],
                                               [1, 64]])
                                    last = t.matmul(
                                        pskv[:, :],
                                        xnorm[:, fc * T + tc * 128:
                                              fc * T + (tc + 1) * 128],
                                        rhs, start=(fc == 0),
                                        stop=(fc == NF - 1))
                                for fc in range(NF):
                                    rhs = dap(wgeo, fc * 2 * D + h * 64,
                                              [[NF * 2 * D, 128], [D, 2],
                                               [1, 64]])
                                    last = t.matmul(
                                        psge[:, :],
                                        xnorm[:, fc * T + tc * 128:
                                              fc * T + (tc + 1) * 128],
                                        rhs, start=(fc == 0),
                                        stop=(fc == NF - 1))
                                p.inc("t", last)
                        with nc.Block() as blk:
                            @blk.vector
                            def _(v):
                                bap = dap(kvbb, h * 64,
                                          [[2 * D, 128], [D, 2], [1, 64]])
                                p.inc("v", v.tensor_add(kvt[:, :],
                                                        pskv[:, :], bap))

                            @blk.scalar
                            def _(scx):
                                p.inc("s", scx.activation(geo[:, :],
                                                          psge[:, :],
                                                          AF.Copy))
                        with nc.Block() as blk:
                            @blk.vector
                            def _(v):
                                gc = gmul[:, tc * H + h:tc * H + h + 1]
                                p.inc("v", v.scalar_tensor_tensor(
                                    ken[:, :], geo[:, 0:64], gc,
                                    kvt[:, 0:64], ALU.mult, ALU.add))
                                doff = tc * H * 65 + h * 65
                                p.inc("v", v.scalar_tensor_tensor(
                                    v_sl[:, doff:doff + 64], geo[:, 64:128],
                                    gc, kvt[:, 64:128], ALU.mult, ALU.add))
                        with nc.Block() as blk:
                            @blk.tensor
                            def _(t):
                                p.inc("t", t.transpose(pkt[:, :], ken[:, :],
                                                       p.identB[:, :]))
                        with nc.Block() as blk:
                            @blk.vector
                            def _(v):
                                p.inc("v", v.tensor_copy(
                                    k_all[rbase:rbase + 64,
                                          grp * T + tc * 128:
                                          grp * T + (tc + 1) * 128],
                                    pkt[:, :]))

        with contextlib.ExitStack() as e2:
            oT = e2.enter_context(nc.sbuf_tensor(uname("oT"),
                                                 [128, NF * T], BF16))
            attn_headloop(p, q_all=q_all, k_all=k_all, v_sl=v_sl, oT=oT,
                          causal=True)
            emit_oproj(p, io["out_w"], bo_c, oT, io["xout"], b)


def build_l1_attn_a_prog():
    nc = bass.Bass("TRN2", target_bir_lowering=False)
    xst = nc.dram_tensor("xst", [B_PER_CORE, D, T], F32,
                         kind="ExternalInput")
    xout = nc.dram_tensor("xout", [B_PER_CORE, D, T], F32,
                          kind="ExternalOutput")
    io = {"xst": xst, "xout": xout}
    for nm, shape in (("qkv_w", [D, 3 * D]), ("qkv_b", [3 * D]),
                      ("w1_write", [D, 64]), ("w2_write", [D, 64]),
                      ("w1_read", [D, 64]), ("w2_read", [D, 64]),
                      ("kv_geo_w", [D, 2 * D]), ("out_w", [D, D]),
                      ("out_b", [D]), ("kv_gate", [1]),
                      ("mem_scale", [H]), ("decay_logits", [H])):
        io[nm] = nc.dram_tensor(nm, shape, F32, kind="ExternalInput")
    p = Prog(nc)
    emit_prelude(p)
    emit_copy_state(p, xout, xst, B_PER_CORE * D * T)
    with nc.Block() as blk:
        @blk.gpsimd
        def _(g):
            p.ld.wait_all(g)
    for b in range(B_PER_CORE):
        emit_l1_attn_a(p, io, b)
    with nc.Block() as blk:
        @blk.sync
        def _(sy):
            p.ld.wait_all(sy)
            p.st.wait_all(sy)
            p.gl.wait_all(sy)
    p.close()
    return nc



EM = 2047


def emit_l0_attn_b(p, io, b):
    nc = p.nc
    scale = 1.0 / math.sqrt(DH)
    with contextlib.ExitStack() as es:
        sb = lambda n, sh, dt: es.enter_context(
            nc.sbuf_tensor(uname(n), sh, dt))
        oT = sb("xno_oT", [128, NF * T], BF16)
        Efm = sb("Efm", [64, EM], BF16)
        Erv = sb("Erv", [64, EM], BF16)
        bqs_c = sb("bqs_c", [128, 8], F32)
        bq_c, _ = emit_vec_cols(p, es, io["bq"], D)
        bk_c, _ = emit_vec_cols(p, es, io["bk"], D)
        bo_c, _ = emit_vec_cols(p, es, io["bo"], D)
        lng_c, _ = emit_vec_cols(p, es, io["ln_gamma"], D)
        lnb_c, _ = emit_vec_cols(p, es, io["ln_beta"], D)
        maxE = sb("maxE", [1, 1], F32)
        es_qkv = es.enter_context(contextlib.ExitStack())
        sbq = lambda n, sh, dt: es_qkv.enter_context(
            nc.sbuf_tensor(uname(n), sh, dt))
        q_all = sbq("q_all", [128, NF * T], BF16)
        k_all = sbq("k_all", [128, NF * T], BF16)
        v_sl = sbq("v_sl", [128, NT * H * 65], BF16)
        bvb = sbq("bvb", [128, D], F32)
        emit_vsl_ones(p, v_sl)

        with nc.Block() as blk:
            @blk.vector
            def _(v):
                p.inc("v", v.tensor_scalar(bqs_c[:, :], bq_c[:, 0:8],
                                           scale, None, ALU.mult))
        emit_vec_bcast(p, bvb, io["bv"], D)

        with contextlib.ExitStack() as e2:
            s2 = lambda n, sh, dt: e2.enter_context(
                nc.sbuf_tensor(uname(n), sh, dt))
            edat = s2("edat", [128, 16 * 64], BF16)
            esq = s2("esq", [64, EM], BF16)
            erow = s2("erow", [1, EM], F32)
            with nc.Block() as blk:
                @blk.gpsimd
                def _(g):
                    p.gl.start(g, dap(edat, 0, [[16 * 64, 128], [64, 15],
                                               [1, 64]]),
                               dap(io["dist_emb"], 0,
                                   [[64, 128], [128 * 64, 15], [1, 64]]))
                    p.gl.start(g, dap(edat, 15 * 64,
                                      [[16 * 64, 127], [1, 64]]),
                               dap(io["dist_emb"], 15 * 128 * 64,
                                   [[64, 127], [1, 64]]))
            gl_n = p.gl.n
            with contextlib.ExitStack() as e3:
                pse = e3.enter_context(nc.psum_tensor(uname("pse"),
                                                      [64, 128], BF16))
                for c in range(16):
                    nrows = 128 if c < 15 else 127
                    with nc.Block() as blk:
                        @blk.tensor
                        def _(t):
                            p.gl.wait(t, gl_n)
                            p.inc("t", t.transpose(
                                pse[:, 0:nrows],
                                edat[0:nrows, c * 64:(c + 1) * 64],
                                p.identB[0:nrows, 0:nrows]))
                    with nc.Block() as blk:
                        @blk.vector
                        def _(v):
                            p.inc("v", v.tensor_copy(
                                Efm[:, c * 128:c * 128 + nrows],
                                pse[:, 0:nrows]))
                            p.inc("v", v.tensor_copy(
                                dap(Erv, 2046 - c * 128 - (nrows - 1),
                                    [[EM, 64], [1, nrows]]),
                                dap(pse, nrows - 1, [[128, 64],
                                                    [-1, nrows]])))

            with nc.Block() as blk:
                @blk.scalar
                def _(sc):
                    p.inc("s", sc.activation(esq[:, :], Efm[:, :],
                                             AF.Square))
            with contextlib.ExitStack() as e3:
                psn = e3.enter_context(nc.psum_tensor(uname("psn"),
                                                      [1, EM], F32))
                with nc.Block() as blk:
                    @blk.tensor
                    def _(t):
                        last = None
                        for j in range(4):
                            w = SW if j < 3 else EM - 3 * SW
                            last = t.matmul(
                                psn[0:1, j * SW:j * SW + w],
                                p.onesB[0:64, :], esq[:, j * SW:j * SW + w],
                                start=True, stop=True)
                        p.inc("t", last)
                with nc.Block() as blk:
                    @blk.vector
                    def _(v):
                        p.inc("v", v.tensor_reduce(erow[0:1, 0:1],
                                                   psn[0:1, :], AX.X,
                                                   ALU.max))

                    @blk.scalar
                    def _(sc):
                        p.wait(sc, "v")
                        p.inc("s", sc.activation(maxE[0:1, :],
                                                 erow[0:1, 0:1], AF.Sqrt))

        with contextlib.ExitStack() as e2:
            xnorm = oT
            emit_ln_fm(p, xnorm, src_dram=io["xst"], src_dram_off=b * D * T)
            with nc.Block() as blk:
                @blk.gpsimd
                def _(g):
                    p.st.start(g, dap(io["hn"], b * D * T,
                                      [[T, 128], [T * 128, NF], [1, T]]),
                               dap(xnorm, 0, [[NF * T, 128], [T, NF],
                                              [1, T]]))
            def qbias(h):
                r, c = (h * 64) % 128, (h * 64) // 128
                return bqs_c[r:r + 64, c:c + 1]

            def kbias(h):
                r, c = (h * 64) % 128, (h * 64) // 128
                return bk_c[r:r + 64, c:c + 1]

            with contextlib.ExitStack() as e3:
                wq = e3.enter_context(
                    nc.sbuf_tensor(uname("wqk_sb"), [128, NF * 2 * D],
                                   BF16))
                with nc.Block() as blk:
                    @blk.gpsimd
                    def _(g):
                        for m, hd in enumerate((io["Wq"], io["Wk"])):
                            load_w_bf16(p, g, wq, hd, D, D,
                                        dst_row_width=NF * 2 * D,
                                        dst_chunk_stride=2 * D,
                                        dst_off=m * D)
                emit_qkv_heads(p, xnorm, wq, 2 * D, 0, D, None,
                               q_all, k_all, v_sl, qscale=scale,
                               qbias=qbias, kbias=kbias, do_v=False)
            with contextlib.ExitStack() as e3:
                wv = e3.enter_context(
                    nc.sbuf_tensor(uname("wv_sb"), [128, NF * D], BF16))
                with nc.Block() as blk:
                    @blk.gpsimd
                    def _(g):
                        load_w_bf16(p, g, wv, io["Wv"], D, D)
                emit_qkv_heads(p, xnorm, wv, D, None, None, 0,
                               q_all, k_all, v_sl, bvb=bvb, do_qk=False)

        with contextlib.ExitStack() as e2:
            s2 = lambda n, sh, dt: e2.enter_context(
                nc.sbuf_tensor(uname(n), sh, dt))
            qaug = s2("qaug", [65, T], BF16)
            kaug = s2("kaug", [65, T], BF16)
            mcols = s2("mcols", [128, NT], F32)
            probsT = s2("probsT", [128, NT * SW], BF16)
            rrow = s2("rrow", [1, SW], BF16)
            otmp = s2("otmp", [64, SW], F32)
            tstg = [s2(f"tstg{i}", [128, EM + 1], BF16) for i in range(2)]
            relq = [s2(f"relq{i}", [128, SW], BF16) for i in range(2)]
            relk = [s2(f"relk{i}", [128, SW], BF16) for i in range(2)]
            qe_r = s2("qe_r", [1, 2 * T], BF16)
            ke_r = s2("ke_r", [1, 2 * T], BF16)
            bnd = s2("bnd", [1, 4], F32)
            ones_sw = s2("ones_sw", [1, SW], BF16)

            with nc.Block() as blk:
                @blk.gpsimd
                def _(g):
                    g.memset(kaug[64:65, :], 1.0)
                    g.memset(ones_sw[0:1, :], 1.0)

            for h in range(H):
                rbase = (h % 2) * 64
                grp = h // 2
                qsl_all = q_all[rbase:rbase + 64, grp * T:(grp + 1) * T]
                ksl_all = k_all[rbase:rbase + 64, grp * T:(grp + 1) * T]
                with nc.Block() as blk:
                    @blk.vector
                    def _(v):
                        p.inc("v", v.tensor_copy(qaug[0:64, :], qsl_all))
                        p.inc("v", v.tensor_copy(kaug[0:64, :], ksl_all))

                for (src, Etab, dst_dram, tscale) in (
                        (qaug, Erv, io["qd"], 1.0),
                        (kaug, Efm, io["kd"], scale)):
                    with contextlib.ExitStack() as e3:
                        pst = e3.enter_context(nc.psum_tensor(
                            uname("pst"), [128, 2048], F32))
                        with nc.Block() as blk:
                            t0 = p.cnt["t"]
                            s0 = p.cnt["s"]
                            st0 = p.st.n

                            @blk.tensor
                            def _(t):
                                for tc in range(NT):
                                    if tc >= 1:
                                        p.wait(t, "s", s0 + tc)
                                    last = None
                                    for j in range(4):
                                        w = SW if j < 3 else EM - 3 * SW
                                        last = t.matmul(
                                            pst[:, j * SW:j * SW + w],
                                            src[0:64,
                                                tc * 128:(tc + 1) * 128],
                                            Etab[:, j * SW:j * SW + w],
                                            start=True, stop=True)
                                    p.inc("t", last)

                            @blk.scalar
                            def _(sc):
                                for tc in range(NT):
                                    p.wait(sc, "t", t0 + tc + 1)
                                    if tc >= 2:
                                        p.st.wait(sc, st0 + 16 * (tc - 1))
                                    p.inc("s", sc.activation(
                                        tstg[tc % 2][:, 0:EM],
                                        pst[:, 0:EM], AF.Copy,
                                        scale=tscale))

                            @blk.gpsimd
                            def _(g):
                                for tc in range(NT):
                                    p.wait(g, "s", s0 + tc + 1)
                                    p.st.start(g, dap(
                                        dst_dram, tc * 128 * EM,
                                        [[EM, 128], [1, EM]]),
                                        tstg[tc % 2][:, 0:EM])


                with contextlib.ExitStack() as e3:
                    psb = e3.enter_context(nc.psum_tensor(
                        uname("psb"), [1, T], F32))
                    psn2 = e3.enter_context(nc.psum_tensor(
                        uname("psn2"), [1, T], F32))
                    for (i, (srcA, col)) in enumerate(
                            ((qaug, 0), (qaug, 2046),
                             (kaug, 0), (kaug, 2046))):
                        with nc.Block() as blk:
                            @blk.tensor
                            def _(t):
                                last = None
                                for ts in range(NS):
                                    last = t.matmul(
                                        psb[0:1, ts * SW:(ts + 1) * SW],
                                        Efm[:, col:col + 1],
                                        srcA[0:64, ts * SW:(ts + 1) * SW],
                                        start=True, stop=True)
                                p.inc("t", last)
                        with nc.Block() as blk:
                            @blk.vector
                            def _(v):
                                dst = (qe_r if i < 2 else ke_r)
                                off = (i % 2) * T
                                if i < 2:
                                    p.inc("v", v.tensor_copy(
                                        dst[0:1, off:off + T], psb[0:1, :]))
                                else:
                                    p.inc("v", v.tensor_scalar(
                                        dst[0:1, off:off + T], psb[0:1, :],
                                        scale, None, ALU.mult))

                    with contextlib.ExitStack() as e4:
                        sqb = e4.enter_context(nc.sbuf_tensor(
                            uname("sqb"), [64, T], BF16))
                        for (i, srcA) in ((0, qaug), (1, kaug)):
                            with nc.Block() as blk:
                                @blk.scalar
                                def _(sc):
                                    p.inc("s", sc.activation(
                                        sqb[:, :], srcA[0:64, :],
                                        AF.Square))
                            with nc.Block() as blk:
                                @blk.tensor
                                def _(t):
                                    last = None
                                    for ts in range(NS):
                                        last = t.matmul(
                                            psn2[0:1,
                                                 ts * SW:(ts + 1) * SW],
                                            p.onesB[0:64, :],
                                            sqb[:, ts * SW:(ts + 1) * SW],
                                            start=True, stop=True)
                                    p.inc("t", last)
                            with nc.Block() as blk:
                                @blk.vector
                                def _(v):
                                    p.inc("v", v.tensor_reduce(
                                        bnd[0:1, i:i + 1], psn2[0:1, :],
                                        AX.X, ALU.max))
                    with nc.Block() as blk:
                        @blk.scalar
                        def _(sc):
                            p.inc("s", sc.activation(bnd[0:1, 0:2],
                                                     bnd[0:1, 0:2],
                                                     AF.Sqrt))
                    with nc.Block() as blk:
                        @blk.vector
                        def _(v):
                            p.inc("v", v.tensor_scalar(
                                bnd[0:1, 1:2], bnd[0:1, 1:2], scale, None,
                                ALU.mult))
                            p.wait(v, "v")
                            p.inc("v", v.tensor_add(bnd[0:1, 2:3],
                                                    bnd[0:1, 0:1],
                                                    bnd[0:1, 1:2]))
                            p.wait(v, "v")
                            p.inc("v", v.tensor_scalar(
                                bnd[0:1, 2:3], bnd[0:1, 2:3],
                                maxE[0:1, :], None, ALU.mult))

                with contextlib.ExitStack() as esA:
                    psA = esA.enter_context(
                        nc.psum_tensor(uname("psA"), [128, 2 * T], F32))
                    with nc.Block() as blk:
                        tA0 = p.cnt["t"]
                        vA0 = p.cnt["v"]

                        @blk.tensor
                        def _(t):
                            for ti in range(NT):
                                if ti >= 2:
                                    p.wait(t, "v", vA0 + ti - 1)
                                dst = psA[:, (ti % 2) * T:(ti % 2) * T + T]
                                last = None
                                for ts in range(NS):
                                    sl = slice(ts * SW, (ts + 1) * SW)
                                    last = t.matmul(
                                        dst[:, sl],
                                        qaug[0:64, ti * 128:(ti + 1) * 128],
                                        kaug[0:64, sl], start=True,
                                        stop=True)
                                p.inc("t", last)

                        @blk.vector
                        def _(v):
                            for ti in range(NT):
                                p.wait(v, "t", tA0 + ti + 1)
                                src = psA[:, (ti % 2) * T:(ti % 2) * T + T]
                                p.inc("v", v.tensor_reduce(
                                    mcols[:, ti:ti + 1], src, AX.X,
                                    ALU.max, negate=True))

                with contextlib.ExitStack() as esM:
                    pmr = esM.enter_context(nc.psum_tensor(uname("pmr"),
                                                           [1, T], F32))
                    with nc.Block() as blk:
                        @blk.tensor
                        def _(t):
                            last = None
                            for ti in range(NT):
                                last = t.transpose(
                                    pmr[0:1, ti * 128:(ti + 1) * 128],
                                    mcols[:, ti:ti + 1], p.identF[:, :])
                            p.inc("t", last)
                    with nc.Block() as blk:
                        @blk.vector
                        def _(v):
                            p.inc("v", v.tensor_scalar(
                                qaug[64:65, :], pmr[0:1, :], bnd[0:1, 2:3],
                                None, ALU.subtract))

                for st in range(NS):
                    with contextlib.ExitStack() as esB:
                        psB = esB.enter_context(nc.psum_tensor(
                            uname("psB"), [128, 2 * SW], F32))
                        psPV = esB.enter_context(nc.psum_tensor(
                            uname("psPV"), [65, SW], F32))
                        pbc = esB.enter_context(nc.psum_tensor(
                            uname("pbc"), [64, SW], F32))
                        for c in range(NT):
                            dlr = st * SW - c * 128
                            fully_in = -896 <= dlr <= 512
                            fully_rc = dlr <= -1535
                            fully_lc = dlr >= 1151
                            pslot = psB[:, (c % 2) * SW:(c % 2 + 1) * SW]
                            if fully_rc or fully_lc:
                                eo = 0 if fully_rc else T
                                with nc.Block() as blk:
                                    @blk.tensor
                                    def _(t):
                                        p.inc("t", t.matmul(
                                            pslot,
                                            kaug[:, c * 128:(c + 1) * 128],
                                            qaug[:, st * SW:(st + 1) * SW],
                                            start=True, stop=False))
                                        p.inc("t", t.matmul(
                                            pslot,
                                            ke_r[0:1, eo + c * 128:
                                                 eo + (c + 1) * 128],
                                            ones_sw[0:1, :], start=False,
                                            stop=False))
                                        p.inc("t", t.matmul(
                                            pslot, p.onesBr[0:1, :],
                                            qe_r[0:1, eo + st * SW:
                                                 eo + (st + 1) * SW],
                                            start=False, stop=True))
                                with nc.Block() as blk:
                                    @blk.scalar
                                    def _(sc):
                                        p.inc("s", sc.activation(
                                            probsT[:, c * SW:(c + 1) * SW],
                                            pslot, AF.Exp))
                                continue

                            ld0 = p.ld.n
                            with nc.Block() as blk:
                                @blk.sync
                                def _(sy):
                                    p.ld.start(sy, relk[c % 2][:, :], dap(
                                        io["kd"],
                                        c * 128 * 2046 + st * SW + 1023,
                                        [[2046, 128], [1, SW]]))
                                    p.ld.start(sy, relq[c % 2][:, :], dap(
                                        io["qd"],
                                        st * SW * 2046 + c * 128 + 1023,
                                        [[2046, SW], [1, 128]]),
                                        transpose=True)

                                @blk.tensor
                                def _(t):
                                    p.inc("t", t.matmul(
                                        pslot,
                                        kaug[:, c * 128:(c + 1) * 128],
                                        qaug[:, st * SW:(st + 1) * SW],
                                        start=True, stop=True))
                            with nc.Block() as blk:
                                @blk.vector
                                def _(v):
                                    p.ld.wait(v, ld0 + 32)
                                    p.inc("v", v.tensor_add(
                                        relk[c % 2][:, :],
                                        relk[c % 2][:, :],
                                        relq[c % 2][:, :]))

                                if not fully_in:
                                    @blk.gpsimd
                                    def _(g):
                                        p.wait(g, "v")
                                        p.inc("g", g.affine_select(
                                            relk[c % 2][:, :],
                                            relk[c % 2][:, :], [[1, SW]],
                                            ALU.is_ge, 0.0,
                                            base=dlr + 1023,
                                            channel_multiplier=-1))
                                        p.inc("g", g.affine_select(
                                            relk[c % 2][:, :],
                                            relk[c % 2][:, :], [[-1, SW]],
                                            ALU.is_ge, 0.0,
                                            base=2046 - dlr - 1023,
                                            channel_multiplier=1))
                            if not fully_in:
                                zone_rc = dlr < -896
                                eo = 0 if zone_rc else T
                                with contextlib.ExitStack() as e4:
                                    pcl = e4.enter_context(
                                        nc.psum_tensor(uname("pcl"),
                                                       [128, SW], F32))
                                    with nc.Block() as blk:
                                        @blk.tensor
                                        def _(t):
                                            p.inc("t", t.matmul(
                                                pcl[:, :],
                                                ke_r[0:1, eo + c * 128:
                                                     eo + (c + 1) * 128],
                                                ones_sw[0:1, :],
                                                start=True, stop=False))
                                            p.inc("t", t.matmul(
                                                pcl[:, :], p.onesBr[0:1, :],
                                                qe_r[0:1, eo + st * SW:
                                                     eo + (st + 1) * SW],
                                                start=False, stop=True))
                                    with nc.Block() as blk:
                                        @blk.vector
                                        def _(v):
                                            p.inc("v", v.tensor_copy(
                                                relq[c % 2][:, :],
                                                pcl[:, :]))
                                    with nc.Block() as blk:
                                        @blk.gpsimd
                                        def _(g):
                                            if zone_rc:
                                                p.inc("g", g.affine_select(
                                                    relq[c % 2][:, :],
                                                    relq[c % 2][:, :],
                                                    [[-1, SW]], ALU.is_gt,
                                                    0.0,
                                                    base=-(dlr + 1023),
                                                    channel_multiplier=1))
                                            else:
                                                p.inc("g", g.affine_select(
                                                    relq[c % 2][:, :],
                                                    relq[c % 2][:, :],
                                                    [[1, SW]], ALU.is_gt,
                                                    0.0,
                                                    base=dlr + 1023 - 2046,
                                                    channel_multiplier=-1))
                                    with nc.Block() as blk:
                                        @blk.vector
                                        def _(v):
                                            p.inc("v", v.tensor_add(
                                                relk[c % 2][:, :],
                                                relk[c % 2][:, :],
                                                relq[c % 2][:, :]))

                            with nc.Block() as blk:
                                @blk.vector
                                def _(v):
                                    p.inc("v", v.tensor_add(
                                        pslot, pslot, relk[c % 2][:, :]))

                                @blk.scalar
                                def _(sc):
                                    p.wait(sc, "v")
                                    p.inc("s", sc.activation(
                                        probsT[:, c * SW:(c + 1) * SW],
                                        pslot, AF.Exp))

                        with nc.Block() as blk:
                            @blk.tensor
                            def _(t):
                                for c in range(NT):
                                    p.inc("t", t.matmul(
                                        psPV[:, :],
                                        v_sl[:, c * (H * 65) + h * 65:
                                             c * (H * 65) + h * 65 + 65],
                                        probsT[:, c * SW:(c + 1) * SW],
                                        start=(c == 0), stop=(c == NT - 1)))
                        with nc.Block() as blk:
                            @blk.scalar
                            def _(sc):
                                p.inc("s", sc.activation(
                                    rrow[0:1, :], psPV[64:65, :], AF.Ln))
                                p.wait(sc, "s")
                                p.inc("s", sc.activation(
                                    rrow[0:1, :], rrow[0:1, :], AF.Exp,
                                    scale=-1.0))
                                p.wait(sc, "s")
                                p.inc("s", sc.activation(
                                    otmp[:, :], psPV[0:64, :], AF.Copy))
                        with nc.Block() as blk:
                            @blk.tensor
                            def _(t):
                                p.inc("t", t.matmul(
                                    pbc[:, :], p.onesBr[0:1, 0:64],
                                    rrow[0:1, :], start=True, stop=True))
                        with nc.Block() as blk:
                            @blk.vector
                            def _(v):
                                dst = oT[rbase:rbase + 64,
                                         grp * T + st * SW:
                                         grp * T + (st + 1) * SW]
                                p.inc("v", v.tensor_mul(dst, otmp[:, :],
                                                        pbc[:, :]))

        es_qkv.close()
        if True:
            with contextlib.ExitStack() as e3:
                s3 = lambda n, sh, dt: e3.enter_context(
                    nc.sbuf_tensor(uname(n), sh, dt))
                ysl = s3("ysl", [128, NF * T], F32)
                wo_sb = s3("wo_sb2", [128, NF * D], BF16)
                hstg = [s3(f"hstg{i}", [128, T], F32) for i in range(2)]
                with nc.Block() as blk:
                    @blk.gpsimd
                    def _(g):
                        load_w_bf16(p, g, wo_sb, io["Wo"], D, D)
                gl_n = p.gl.n
                es_pso = e3.enter_context(contextlib.ExitStack())
                pso = es_pso.enter_context(nc.psum_tensor(uname("pso"),
                                                          [128, 2 * SW],
                                                          F32))
                jobs = [(dc, ts) for dc in range(NF) for ts in range(NS)]
                with nc.Block() as blk:
                    t0 = p.cnt["t"]
                    s0 = p.cnt["s"]

                    @blk.tensor
                    def _(t):
                        p.gl.wait(t, gl_n)
                        for i, (dc, ts) in enumerate(jobs):
                            if i >= 2:
                                p.wait(t, "s", s0 + i - 1)
                            last = None
                            for fc in range(NF):
                                last = t.matmul(
                                    pso[:, (i % 2) * SW:(i % 2 + 1) * SW],
                                    wo_sb[:, fc * D + dc * 128:
                                          fc * D + (dc + 1) * 128],
                                    oT[:, fc * T + ts * SW:
                                       fc * T + (ts + 1) * SW],
                                    start=(fc == 0), stop=(fc == NF - 1))
                            p.inc("t", last)

                    @blk.scalar
                    def _(sc):
                        for i, (dc, ts) in enumerate(jobs):
                            p.wait(sc, "t", t0 + i + 1)
                            p.inc("s", sc.activation(
                                ysl[:, dc * T + ts * SW:
                                    dc * T + (ts + 1) * SW],
                                pso[:, (i % 2) * SW:(i % 2 + 1) * SW],
                                AF.Identity, bias=bo_c[:, dc:dc + 1]))
                with nc.Block() as blk:
                    @blk.vector
                    def _(v):
                        pass
                es_pso.close()

                ld0 = p.ld.n
                with nc.Block() as blk:
                    v0h = p.cnt["v"]

                    @blk.sync
                    def _(sy):
                        p.st.wait(sy, p.st.n)
                        for fc in range(NF):
                            if fc >= 2:
                                p.wait(sy, "v", v0h + fc - 1)
                            p.ld.start(sy, hstg[fc % 2][:, :],
                                       dap(io["hn"], (b * D + fc * 128) * T,
                                           [[T, 128], [1, T]]))

                    @blk.vector
                    def _(v):
                        for fc in range(NF):
                            p.ld.wait(v, ld0 + 16 * (fc + 1))
                            p.inc("v", v.tensor_add(
                                ysl[:, fc * T:(fc + 1) * T],
                                ysl[:, fc * T:(fc + 1) * T],
                                hstg[fc % 2][:, :]))

                emit_ln_fm(p, None, mode="ln",
                           src_fn=lambda fc: ysl[:, fc * T:(fc + 1) * T],
                           gamma=lng_c, beta=lnb_c,
                           out_dram_fn=lambda fc: dap(
                               io["xout"], (b * D + fc * 128) * T,
                               [[T, 128], [1, T]]),
                           out_accum=True, eps=1e-12)


def build_l0_attn_b_prog():
    nc = bass.Bass("TRN2", target_bir_lowering=False)
    xst = nc.dram_tensor("xst", [B_PER_CORE, D, T], F32,
                         kind="ExternalInput")
    xout = nc.dram_tensor("xout", [B_PER_CORE, D, T], F32,
                          kind="ExternalOutput")
    io = {"xst": xst, "xout": xout}
    for nm, shape in (("Wq", [D, D]), ("bq", [D]), ("Wk", [D, D]),
                      ("bk", [D]), ("Wv", [D, D]), ("bv", [D]),
                      ("dist_emb", [EM, 64]), ("Wo", [D, D]), ("bo", [D]),
                      ("ln_gamma", [D]), ("ln_beta", [D])):
        io[nm] = nc.dram_tensor(nm, shape, F32, kind="ExternalInput")
    io["qd"] = nc.dram_tensor("qd", [T * EM], BF16)
    io["kd"] = nc.dram_tensor("kd", [T * EM], BF16)
    io["hn"] = nc.dram_tensor("hn", [B_PER_CORE, D, T], F32)
    p = Prog(nc)
    emit_prelude(p)
    emit_copy_state(p, xout, xst, B_PER_CORE * D * T)
    with nc.Block() as blk:
        @blk.gpsimd
        def _(g):
            p.ld.wait_all(g)
    for b in range(B_PER_CORE):
        emit_l0_attn_b(p, io, b)
    with nc.Block() as blk:
        @blk.sync
        def _(sy):
            p.ld.wait_all(sy)
            p.st.wait_all(sy)
            p.gl.wait_all(sy)
    p.close()
    return nc



def build_l0_attn_a_prog():
    nc = bass.Bass("TRN2", target_bir_lowering=False)
    xst = nc.dram_tensor("xst", [B_PER_CORE, D, T], F32,
                         kind="ExternalInput")
    xout = nc.dram_tensor("xout", [B_PER_CORE, D, T], F32,
                          kind="ExternalOutput")
    w_qkv = nc.dram_tensor("w_qkv", [D, 3 * D], F32, kind="ExternalInput")
    b_qkv = nc.dram_tensor("b_qkv", [3 * D], F32, kind="ExternalInput")
    w_out = nc.dram_tensor("w_out", [D, D], F32, kind="ExternalInput")
    b_out = nc.dram_tensor("b_out", [D], F32, kind="ExternalInput")

    p = Prog(nc)
    emit_prelude(p)
    emit_copy_state(p, xout, xst, B_PER_CORE * D * T)
    with nc.Block() as blk:
        @blk.gpsimd
        def _(g):
            p.ld.wait_all(g)

    io = {"xst": xst, "xout": xout, "w_qkv": w_qkv, "b_qkv": b_qkv,
          "w_out": w_out, "b_out": b_out}
    for b in range(B_PER_CORE):
        emit_attn_l0a(p, io, b)
    with nc.Block() as blk:
        @blk.sync
        def _(sy):
            p.ld.wait_all(sy)
            p.st.wait_all(sy)
            p.gl.wait_all(sy)
    p.close()
    return nc



_CHAIN = [
    ("l0_attn_a", {"w_qkv": "L0_attn_a_w_qkv", "b_qkv": "L0_attn_a_b_qkv",
                   "w_out": "L0_attn_a_w_out", "b_out": "L0_attn_a_b_out"}),
    ("l0_ffn", {"wq": "L0_ffn_wq", "wk": "L0_ffn_wk", "wv": "L0_ffn_wv",
                "wo": "L0_ffn_wo", "router_w": "L0_ffn_router_w",
                "exp_fc1": "L0_ffn_exp_fc1", "exp_fc2": "L0_ffn_exp_fc2"}),
    ("l0_attn_b", {"Wq": "L0_attn_b_Wq", "bq": "L0_attn_b_bq",
                   "Wk": "L0_attn_b_Wk", "bk": "L0_attn_b_bk",
                   "Wv": "L0_attn_b_Wv", "bv": "L0_attn_b_bv",
                   "dist_emb": "L0_attn_b_dist_emb", "Wo": "L0_attn_b_Wo",
                   "bo": "L0_attn_b_bo", "ln_gamma": "L0_attn_b_ln_gamma",
                   "ln_beta": "L0_attn_b_ln_beta"}),
    ("l0_ffn", None),
    ("l1_attn_a", {"qkv_w": "L1_attn_a_qkv_w", "qkv_b": "L1_attn_a_qkv_b",
                   "w1_write": "L1_attn_a_w1_write",
                   "w2_write": "L1_attn_a_w2_write",
                   "w1_read": "L1_attn_a_w1_read",
                   "w2_read": "L1_attn_a_w2_read",
                   "kv_geo_w": "L1_attn_a_kv_geo_w",
                   "out_w": "L1_attn_a_out_w", "out_b": "L1_attn_a_out_b",
                   "kv_gate": "L1_attn_a_kv_gate",
                   "mem_scale": "L1_attn_a_mem_scale",
                   "decay_logits": "L1_attn_a_decay_logits"}),
    ("l1_ffn", {"col_emb": "L1_ffn_col_emb", "gate_w1": "L1_ffn_gate_w1",
                "gate_b1": "L1_ffn_gate_b1", "gate_w2": "L1_ffn_gate_w2",
                "gate_b2": "L1_ffn_gate_b2", "W_cols": "L1_ffn_W_cols",
                "b_cols": "L1_ffn_b_cols", "proj_w": "L1_ffn_proj_w",
                "proj_b": "L1_ffn_proj_b", "ln_g": "L1_ffn_ln_g",
                "ln_b": "L1_ffn_ln_b"}),
    ("l1_attn_b", {"Wq": "L1_attn_b_Wq", "Wk": "L1_attn_b_Wk",
                   "Wv": "L1_attn_b_Wv", "Wo": "L1_attn_b_Wo",
                   "alibi_slopes": "L1_attn_b_alibi_slopes",
                   "power": "L1_attn_b_power"}),
    ("l1_ffn", None),
]

_BUILDERS = {
    "l0_attn_a": build_l0_attn_a_prog,
    "l0_ffn": build_l0_ffn_prog,
    "l0_attn_b": build_l0_attn_b_prog,
    "l1_attn_a": build_l1_attn_a_prog,
    "l1_ffn": build_l1_ffn_prog,
    "l1_attn_b": build_l1_attn_b_prog,
}

_PROG_CACHE = {}


def kernel(**inputs):
    set_dims(2048)
    x = inputs["L0_attn_a_x"]
    n_cores = 8
    xt = np.ascontiguousarray(x.transpose(0, 2, 1)).reshape(
        n_cores, B_PER_CORE, D, T)
    state = [np.ascontiguousarray(xt[c]) for c in range(n_cores)]

    maps = {}
    for name, mp in _CHAIN:
        if mp is not None:
            maps[name] = mp
    for name in _PROG_CACHE.keys() | set(_BUILDERS):
        pass
    for name, mp in _CHAIN:
        mapping = maps[name]
        if name not in _PROG_CACHE:
            _PROG_CACHE[name] = _BUILDERS[name]()
        nc = _PROG_CACHE[name]
        wdict = {}
        for k, src in mapping.items():
            arr = np.asarray(inputs[src])
            if arr.ndim == 0:
                arr = arr.reshape(1)
            wdict[k] = arr
        in_maps = [dict(wdict, xst=state[c]) for c in range(n_cores)]
        res = run_bass_kernel_spmd(nc, in_maps, core_ids=list(range(n_cores)))
        state = [np.asarray(res.results[c]["xout"]) for c in range(n_cores)]

    out = np.stack(state).reshape(16, D, T).transpose(0, 2, 1)
    return np.ascontiguousarray(out)


# version 03
# speedup vs baseline: 1.2703x; 1.2703x over previous
import contextlib
import math

import numpy as np
import concourse.bass as bass
import concourse.mybir as mybir
from concourse.bass_utils import run_bass_kernel_spmd

F32 = mybir.dt.float32
BF16 = mybir.dt.bfloat16
I32 = mybir.dt.int32
AF = mybir.ActivationFunctionType
ALU = mybir.AluOpType
AX = mybir.AxisListType

B_PER_CORE = 2
D = 1024
H = 16
DH = 64
NF = D // 128
LN_EPS = 1e-5
PI = math.pi
PI2 = 2.0 * math.pi

T = 2048
SW = 512
NT = T // 128
NS = T // SW


def set_dims(t, sw=512):
    global T, NT, NS, SW
    T, SW = t, sw
    NT = T // 128
    NS = T // SW



_UID = [0]


def uname(base):
    _UID[0] += 1
    return f"{base}_{_UID[0]}"



class Dma:
    def __init__(self, nc, ctx, name):
        self.sem = ctx.enter_context(nc.semaphore(name))
        self.n = 0

    def start(self, eng, out, in_, **kw):
        eng.dma_start(out, in_, **kw).then_inc(self.sem, 16)
        self.n += 16
        return self.n

    def wait_all(self, eng):
        eng.wait_ge(self.sem, self.n)

    def wait(self, eng, upto):
        eng.wait_ge(self.sem, upto)


class Prog:
    def __init__(self, nc):
        self.nc = nc
        self.ctx = contextlib.ExitStack()
        self.ld = Dma(nc, self.ctx, "d_ld")
        self.gl = Dma(nc, self.ctx, "d_gl")
        self.st = Dma(nc, self.ctx, "d_st")
        self.sem = {}
        self.cnt = {}
        for nm in ("t", "v", "s", "g"):
            self.sem[nm] = self.ctx.enter_context(nc.semaphore("e_" + nm))
            self.cnt[nm] = 0

    def inc(self, key, ins):
        ins.then_inc(self.sem[key], 1)
        self.cnt[key] += 1
        return self.cnt[key]

    def wait(self, eng, key, val=None):
        eng.wait_ge(self.sem[key], self.cnt[key] if val is None else val)

    def sbuf(self, name, shape, dtype):
        return self.ctx.enter_context(self.nc.sbuf_tensor(name, shape, dtype))

    def close(self):
        self.ctx.close()


def dap(h, off, pattern):
    return bass.AP(h, off, pattern)



def emit_prelude(p):
    nc = p.nc
    p.onesB = p.sbuf("onesB", [128, 1], BF16)
    p.onesBr = p.sbuf("onesBr", [1, 128], BF16)
    p.identF = p.sbuf("identF", [128, 128], F32)
    p.identB = p.sbuf("identB", [128, 128], BF16)
    p.r_m10k = p.ctx.enter_context(nc.gpsimd.register("r_m10k"))
    with nc.Block() as b:
        @b.gpsimd
        def _(g):
            g.reg_mov(p.r_m10k, -10000.0)
            g.memset(p.onesB[:, :], 1.0)
            g.memset(p.onesBr[:, :], 1.0)
            g.memset(p.identF[:, :], 1.0)
            g.memset(p.identB[:, :], 1.0)
            g.affine_select(p.identF[:, :], p.identF[:, :], [[-1, 128]],
                            ALU.is_equal, 0.0, base=0, channel_multiplier=1)
            g.affine_select(p.identB[:, :], p.identB[:, :], [[-1, 128]],
                            ALU.is_equal, 0.0, base=0, channel_multiplier=1)


def register_consts(p, vals):
    vals = sorted(set(float(v) for v in vals) - {0.0, 1.0})
    if not vals:
        return
    p.cpool = p.sbuf("cpool", [128, len(vals)], F32)
    for i, v in enumerate(vals):
        p.nc.const_aps.register(F32, v, p.cpool[:, i:i + 1])
    with p.nc.Block() as b:
        @b.gpsimd
        def _(g):
            for i, v in enumerate(vals):
                g.memset(p.cpool[:, i:i + 1], v)



def emit_vec_cols(p, es, handle, n, src_off=0):
    nc = p.nc
    ncol = n // 128
    col_sb = es.enter_context(nc.sbuf_tensor(uname("veccol"), [128, ncol],
                                             F32))
    with contextlib.ExitStack() as e2:
        row_sb = e2.enter_context(nc.sbuf_tensor(uname("vecrow"), [1, n],
                                                 F32))
        with nc.Block() as b:
            @b.sync
            def _(sy):
                p.ld.start(sy, row_sb[0:1, :],
                           dap(handle, src_off, [[n, 1], [1, n]]))
        ldn = p.ld.n
        with contextlib.ExitStack() as e3:
            pc = e3.enter_context(nc.psum_tensor(uname("veccol_ps"),
                                                 [128, ncol], F32))
            with nc.Block() as b:
                @b.tensor
                def _(t):
                    p.ld.wait(t, ldn)
                    last = None
                    for c in range(ncol):
                        last = t.matmul(pc[:, c:c + 1],
                                        row_sb[0:1, c * 128:(c + 1) * 128],
                                        p.identF[0:1, 0:1], start=True,
                                        stop=True)
                    p.inc("t", last)
            with nc.Block() as b:
                @b.vector
                def _(v):
                    p.inc("v", v.tensor_copy(col_sb[:, :], pc[:, :]))
    return col_sb, None


def emit_vec_bcast(p, dst_sb, handle, n, src_off=0):
    nc = p.nc
    with contextlib.ExitStack() as e2:
        row_sb = e2.enter_context(nc.sbuf_tensor(uname("vbrow"), [1, n],
                                                 F32))
        with nc.Block() as b:
            @b.sync
            def _(sy):
                p.ld.start(sy, row_sb[0:1, :],
                           dap(handle, src_off, [[n, 1], [1, n]]))
        ldn = p.ld.n
        with nc.Block() as b:
            @b.vector
            def _(v):
                p.ld.wait(v, ldn)
        emit_bcast_row(p, dst_sb, row_sb, 0, n)


def emit_copy_state(p, dst, src, nelem):
    with p.nc.Block() as b:
        @b.sync
        def _(sy):
            step = nelem // 4
            for i in range(4):
                p.ld.start(sy, dap(dst, i * step, [[step, 1], [1, step]]),
                           dap(src, i * step, [[step, 1], [1, step]]))


def load_w_bf16(p, g, dst_sb, w_dram, rows, cols, dst_row_width=None,
                src_off=0, src_row_stride=None, dst_off=0,
                dst_chunk_stride=None):
    nfc = rows // 128
    rw = dst_row_width if dst_row_width is not None else nfc * cols
    dcs = dst_chunk_stride if dst_chunk_stride is not None else cols
    srs = src_row_stride if src_row_stride is not None else cols
    dst = dap(dst_sb, dst_off, [[rw, 128], [dcs, nfc], [1, cols]])
    src = dap(w_dram, src_off, [[srs, 128], [128 * srs, nfc], [1, cols]])
    p.gl.start(g, dst, src)



def emit_ln_fm(p, xb, src_dram_off=None, src_dram=None, mode="ln",
               gamma=None, beta=None, src_f32_sb=None, out_f32_sb=None,
               out_dram_fn=None, src_fn=None, out_accum=False, eps=LN_EPS):
    nc = p.nc
    if src_fn is not None:
        src_f32_sb = [src_fn(fc) for fc in range(NF)]
    fp32src = src_f32_sb is not None
    with contextlib.ExitStack() as es:
        xq = [es.enter_context(nc.sbuf_tensor(uname(f"lnq{i}"), [128, T], BF16))
              for i in range(2)]
        xc = ([es.enter_context(nc.sbuf_tensor(uname(f"lnc{i}"), [128, T],
                                               BF16)) for i in range(2)]
              if (fp32src and mode == "ln") else None)
        srow_mu = es.enter_context(nc.sbuf_tensor(uname("ln_smu"), [1, T], F32))
        srow_va = es.enter_context(nc.sbuf_tensor(uname("ln_sva"), [1, T], F32))
        brow_mu = es.enter_context(nc.sbuf_tensor(uname("ln_bmu"), [1, T], BF16))
        brow_va = es.enter_context(nc.sbuf_tensor(uname("ln_bva"), [1, T], BF16))
        rstd = es.enter_context(nc.sbuf_tensor(uname("ln_rstd"), [128, T], F32))
        gl0 = p.gl.n
        if src_dram is not None:
            with nc.Block() as b:
                @b.gpsimd
                def _(g):
                    p.gl.start(g, dap(xb, 0,
                                      [[NF * T, 128], [T, NF], [1, T]]),
                               dap(src_dram, src_dram_off,
                                   [[T, 128], [T * 128, NF], [1, T]]))

        def chunk(fc):
            if fp32src:
                c = src_f32_sb[fc]
                return c if src_fn is not None else c[:, :]
            return xb[:, fc * T:(fc + 1) * T]

        if xb is None:
            assert fp32src

        with contextlib.ExitStack() as es2:
            stat = es2.enter_context(nc.psum_tensor(uname("ln_stat"), [33, T], F32))
            with nc.Block() as b:
                s0 = p.cnt["s"]
                t0 = p.cnt["t"]

                g0 = p.cnt["g"]

                @b.scalar
                def _(s):
                    for fc in range(NF):
                        if src_dram is not None:
                            p.gl.wait(s, gl0 + 16)
                        if fc >= 2:
                            p.wait(s, "t", t0 + fc - 1)
                        p.inc("s", s.activation(xq[fc % 2][:, :], chunk(fc),
                                                AF.Square))

                if fp32src and mode == "ln":
                    @b.gpsimd
                    def _(g):
                        for fc in range(NF):
                            if fc >= 2:
                                p.wait(g, "t", t0 + fc - 1)
                            p.inc("g", g.tensor_copy(xc[fc % 2][:, :],
                                                     chunk(fc)))

                @b.tensor
                def _(t):
                    for fc in range(NF):
                        p.wait(t, "s", s0 + fc + 1)
                        if fp32src and mode == "ln":
                            p.wait(t, "g", g0 + fc + 1)
                        last = None
                        for ts in range(NS):
                            sl = slice(ts * SW, (ts + 1) * SW)
                            if mode == "ln":
                                src_sum = (xc[fc % 2][:, sl] if fp32src else
                                           xb[:, fc * T + ts * SW:
                                              fc * T + (ts + 1) * SW])
                                last = t.matmul(
                                    stat[0:1, sl], p.onesB[:, :], src_sum,
                                    start=(fc == 0), stop=(fc == NF - 1))
                            last = t.matmul(
                                stat[32:33, sl], p.onesB[:, :],
                                xq[fc % 2][:, sl], start=(fc == 0),
                                stop=(fc == NF - 1))
                        p.inc("t", last)

            assert not (fp32src and mode == "ln") or True

            with nc.Block() as b:
                @b.vector
                def _(v):
                    if mode == "ln":
                        p.inc("v", v.tensor_scalar(
                            srow_mu[0:1, :], stat[0:1, :], 1.0 / D, None,
                            ALU.mult))
                        p.wait(v, "v")
                        p.inc("v", v.tensor_mul(srow_va[0:1, :],
                                                srow_mu[0:1, :],
                                                srow_mu[0:1, :]))
                        p.wait(v, "v")
                        p.inc("v", v.scalar_tensor_tensor(
                            srow_va[0:1, :], stat[32:33, :], 1.0 / D,
                            srow_va[0:1, :], ALU.mult, ALU.subtract))
                        p.wait(v, "v")
                        p.inc("v", v.tensor_scalar(
                            srow_va[0:1, :], srow_va[0:1, :], eps, None,
                            ALU.add))
                        p.wait(v, "v")
                        p.inc("v", v.tensor_copy(brow_mu[0:1, :], srow_mu[0:1, :]))
                    else:
                        p.inc("v", v.tensor_scalar(
                            srow_va[0:1, :], stat[32:33, :], 1.0 / D, eps,
                            ALU.mult, ALU.add))
                        p.wait(v, "v")
                    p.inc("v", v.tensor_copy(brow_va[0:1, :], srow_va[0:1, :]))

        with contextlib.ExitStack() as es2:
            bc = es2.enter_context(nc.psum_tensor(uname("ln_bc"), [128, 2 * T], F32))
            with nc.Block() as b:
                @b.tensor
                def _(t):
                    last = None
                    for ts in range(NS):
                        sl = slice(ts * SW, (ts + 1) * SW)
                        sl2 = slice(T + ts * SW, T + (ts + 1) * SW)
                        if mode == "ln":
                            last = t.matmul(bc[:, sl], p.onesBr[:, :],
                                            brow_mu[0:1, sl], start=True,
                                            stop=True)
                        last = t.matmul(bc[:, sl2], p.onesBr[:, :],
                                        brow_va[0:1, sl], start=True,
                                        stop=True)
                    p.inc("t", last)

            with nc.Block() as b:
                @b.scalar
                def _(s):
                    p.inc("s", s.activation(rstd[:, :], bc[:, T:2 * T],
                                            AF.Ln))
                    p.wait(s, "s")
                    p.inc("s", s.activation(rstd[:, :], rstd[:, :], AF.Exp,
                                            scale=-0.5))

            with nc.Block() as b:
                @b.vector
                def _(v):
                    for fc in range(NF):
                        src = chunk(fc)
                        dst = (xb[:, fc * T:(fc + 1) * T]
                               if xb is not None else None)
                        if mode == "ln":
                            p.inc("v", v.tensor_sub(src, src, bc[:, 0:T]))
                            p.wait(v, "v")
                        p.inc("v", v.tensor_mul(src, src, rstd[:, :]))
                        p.wait(v, "v")
                        if gamma is not None:
                            p.inc("v", v.tensor_scalar(
                                src, src, gamma[:, fc:fc + 1],
                                beta[:, fc:fc + 1], ALU.mult, ALU.add))
                            p.wait(v, "v")
                        if fp32src and xb is not None:
                            p.inc("v", v.tensor_copy(dst, src))
                        if out_f32_sb is not None and \
                                out_f32_sb is not src_f32_sb:
                            p.inc("v", v.tensor_copy(out_f32_sb[fc][:, :],
                                                     src))
            if out_dram_fn is not None:
                with nc.Block() as b:
                    @b.gpsimd
                    def _(g):
                        for fc in range(NF):
                            kw = {"accum_op": ALU.add} if out_accum else {}
                            p.st.start(g, out_dram_fn(fc),
                                       chunk(fc) if fp32src
                                       else xb[:, fc * T:(fc + 1) * T], **kw)
                with nc.Block() as b:
                    @b.gpsimd
                    def _(g):
                        p.st.wait_all(g)



def attn_headloop(p, *, q_all, k_all, v_sl, oT, causal):
    nc = p.nc
    with contextlib.ExitStack() as es:
        sb = lambda n, sh, dt: es.enter_context(nc.sbuf_tensor(uname(n), sh, dt))
        qaug = sb("qaug", [65, T], BF16)
        kaug = sb("kaug", [65, T], BF16)
        mcols = sb("mcols", [128, NT], F32)
        probsT = sb("probsT", [128, NT * SW], BF16)
        rrow = sb("rrow", [1, SW], BF16)
        otmp = sb("otmp", [64, SW], F32)

        with nc.Block() as b:
            @b.gpsimd
            def _(g):
                g.memset(kaug[64:65, :], 1.0)

        for h in range(H):
            rbase = (h % 2) * 64
            grp = h // 2
            with nc.Block() as b:
                @b.vector
                def _(v):
                    p.inc("v", v.tensor_copy(
                        qaug[0:64, :],
                        q_all[rbase:rbase + 64, grp * T:(grp + 1) * T]))
                    p.inc("v", v.tensor_copy(
                        kaug[0:64, :],
                        k_all[rbase:rbase + 64, grp * T:(grp + 1) * T]))

            with contextlib.ExitStack() as esA:
                psA = esA.enter_context(
                    nc.psum_tensor(uname("psA"), [128, 2 * T], F32))
                with nc.Block() as b:
                    tA0 = p.cnt["t"]
                    vA0 = p.cnt["v"]

                    @b.tensor
                    def _(t):
                        for ti in range(NT):
                            nk = min((ti // (SW // 128) + 1) * SW, T) \
                                if causal else T
                            if ti >= 2:
                                p.wait(t, "v", vA0 + ti - 1)
                            dst = psA[:, (ti % 2) * T:(ti % 2) * T + T]
                            last = None
                            for ts in range(nk // SW):
                                sl = slice(ts * SW, (ts + 1) * SW)
                                last = t.matmul(
                                    dst[:, sl],
                                    qaug[0:64, ti * 128:(ti + 1) * 128],
                                    kaug[0:64, sl], start=True, stop=True)
                            p.inc("t", last)

                    @b.vector
                    def _(v):
                        for ti in range(NT):
                            nk = min((ti // (SW // 128) + 1) * SW, T) \
                                if causal else T
                            p.wait(v, "t", tA0 + ti + 1)
                            src = psA[:, (ti % 2) * T:(ti % 2) * T + nk]
                            p.inc("v", v.tensor_reduce(
                                mcols[:, ti:ti + 1], src, AX.X, ALU.max,
                                negate=True))

            with contextlib.ExitStack() as esM:
                pmr = esM.enter_context(nc.psum_tensor(uname("pmr"), [1, T], F32))
                with nc.Block() as b:
                    @b.tensor
                    def _(t):
                        last = None
                        for ti in range(NT):
                            last = t.transpose(
                                pmr[0:1, ti * 128:(ti + 1) * 128],
                                mcols[:, ti:ti + 1], p.identF[:, :])
                        p.inc("t", last)
                with nc.Block() as b:
                    @b.vector
                    def _(v):
                        p.inc("v", v.tensor_copy(qaug[64:65, :], pmr[0:1, :]))

            for st in range(NS):
                nk_t = min((st + 1) * (SW // 128), NT) if causal else NT
                with contextlib.ExitStack() as esB:
                    psB = esB.enter_context(
                        nc.psum_tensor(uname("psB"), [128, 2 * SW], F32))
                    psPV = esB.enter_context(
                        nc.psum_tensor(uname("psPV"), [65, SW], F32))
                    pbc = esB.enter_context(
                        nc.psum_tensor(uname("pbc"), [64, SW], F32))

                    with nc.Block() as b:
                        tB0 = p.cnt["t"]
                        sB0 = p.cnt["s"]

                        @b.tensor
                        def _(t):
                            for c in range(nk_t):
                                if c >= 2:
                                    p.wait(t, "s", sB0 + c - 1)
                                p.inc("t", t.matmul(
                                    psB[:, (c % 2) * SW:(c % 2 + 1) * SW],
                                    kaug[:, c * 128:(c + 1) * 128],
                                    qaug[:, st * SW:(st + 1) * SW],
                                    start=True, stop=True))

                        @b.scalar
                        def _(s):
                            for c in range(nk_t):
                                p.wait(s, "t", tB0 + c + 1)
                                p.inc("s", s.activation(
                                    probsT[:, c * SW:(c + 1) * SW],
                                    psB[:, (c % 2) * SW:(c % 2 + 1) * SW],
                                    AF.Exp))

                        if causal:
                            @b.gpsimd
                            def _(g):
                                for c in range(nk_t):
                                    if c * 128 + 127 <= st * SW:
                                        continue
                                    p.wait(g, "s", sB0 + c + 1)
                                    p.inc("g", g.affine_select(
                                        probsT[:, c * SW:(c + 1) * SW],
                                        probsT[:, c * SW:(c + 1) * SW],
                                        [[1, SW]], ALU.is_ge, 0.0,
                                        base=st * SW - c * 128,
                                        channel_multiplier=-1))

                    with nc.Block() as b:
                        @b.tensor
                        def _(t):
                            for c in range(nk_t):
                                p.inc("t", t.matmul(
                                    psPV[:, :],
                                    v_sl[:, c * (H * 65) + h * 65:
                                         c * (H * 65) + h * 65 + 65],
                                    probsT[:, c * SW:(c + 1) * SW],
                                    start=(c == 0), stop=(c == nk_t - 1)))

                    with nc.Block() as b:
                        @b.scalar
                        def _(s):
                            p.inc("s", s.activation(rrow[0:1, :],
                                                    psPV[64:65, :], AF.Ln))
                            p.wait(s, "s")
                            p.inc("s", s.activation(rrow[0:1, :],
                                                    rrow[0:1, :], AF.Exp,
                                                    scale=-1.0))
                            p.wait(s, "s")
                            p.inc("s", s.activation(otmp[:, :], psPV[0:64, :],
                                                    AF.Copy))

                    with nc.Block() as b:
                        @b.tensor
                        def _(t):
                            p.inc("t", t.matmul(
                                pbc[:, :], p.onesBr[0:1, 0:64], rrow[0:1, :],
                                start=True, stop=True))

                    with nc.Block() as b:
                        @b.vector
                        def _(v):
                            dst = oT[rbase:rbase + 64,
                                     grp * T + st * SW:
                                     grp * T + (st + 1) * SW]
                            p.inc("v", v.tensor_mul(dst, otmp[:, :],
                                                    pbc[:, :]))



def emit_rope_tables(p, es, theta=500.0, half=32):
    nc = p.nc
    sb = lambda n, sh, dt: es.enter_context(nc.sbuf_tensor(uname(n), sh, dt))
    cosT = sb("cosT", [128, T], BF16)
    sinT = sb("sinT", [128, T], BF16)
    with contextlib.ExitStack() as e2:
        s2 = lambda n, sh, dt: e2.enter_context(nc.sbuf_tensor(uname(n), sh, dt))
        frow = s2("frow", [1, 64], F32)
        trow = s2("trow", [1, T], F32)
        ang = s2("ang", [64, T], F32)
        u = s2("rr_u", [64, T], F32)
        ui = s2("rr_i", [64, T], I32)
        pang = e2.enter_context(nc.psum_tensor(uname("pang"), [64, T], F32))

        with nc.Block() as b:
            @b.gpsimd
            def _(g):
                g.iota(frow[:, :], [[0, 2], [1, 32]], base=0,
                       channel_multiplier=0,
                       allow_small_or_imprecise_dtypes=True)
                g.iota(trow[:, :], [[1, T]], base=0, channel_multiplier=0,
                       allow_small_or_imprecise_dtypes=True)

        with nc.Block() as b:
            @b.scalar
            def _(s):
                p.inc("s", s.activation(frow[:, :], frow[:, :], AF.Exp,
                                        scale=-math.log(theta) / half))

        with nc.Block() as b:
            @b.tensor
            def _(t):
                for ts in range(NS):
                    sl = slice(ts * SW, (ts + 1) * SW)
                    p.inc("t", t.matmul(pang[:, sl], frow[0:1, :],
                                        trow[0:1, sl], start=True, stop=True))

        for which, tab, shift in (("s", sinT, 0.0), ("c", cosT, PI / 2)):
            with nc.Block() as b:
                @b.vector
                def _(v):
                    p.inc("v", v.tensor_scalar(ang[:, :], pang[:, :], 1.0,
                                               shift, ALU.mult, ALU.add))
                    p.wait(v, "v")
                    p.inc("v", v.tensor_scalar(u[:, :], ang[:, :],
                                               1.0 / PI2, None, ALU.mult))
                    p.wait(v, "v")
                    p.inc("v", v.tensor_copy(ui[:, :], u[:, :]))
                    p.wait(v, "v")
                    p.inc("v", v.tensor_copy(u[:, :], ui[:, :]))
                    p.wait(v, "v")
                    p.inc("v", v.scalar_tensor_tensor(
                        ang[:, :], u[:, :], -PI2, ang[:, :], ALU.mult,
                        ALU.add))

                @b.scalar
                def _(s):
                    p.wait(s, "v")
                    p.inc("s", s.activation(tab[0:64, :], ang[:, :], AF.Sin))
                    p.inc("s", s.activation(tab[64:128, :], ang[:, :],
                                            AF.Sin))
    return cosT, sinT


def emit_rope_apply(p, A, cosT, sinT, es):
    nc = p.nc
    with contextlib.ExitStack() as e2:
        tmp = e2.enter_context(nc.sbuf_tensor(uname("rp_tmp"), [128, T],
                                              BF16))
        for grp in range(NF):
            sl = slice(grp * T, (grp + 1) * T)
            with nc.Block() as b:
                @b.vector
                def _(v):
                    p.inc("v", v.tensor_scalar(
                        tmp[0:32, :], A[32:64, sl], -1.0, None, ALU.mult))
                    p.inc("v", v.tensor_copy(tmp[32:64, :], A[0:32, sl]))
                    p.inc("v", v.tensor_scalar(
                        tmp[64:96, :], A[96:128, sl], -1.0, None, ALU.mult))
                    p.inc("v", v.tensor_copy(tmp[96:128, :], A[64:96, sl]))
                    p.wait(v, "v")
                    p.inc("v", v.tensor_mul(A[:, sl], A[:, sl], cosT[:, :]))
                    p.inc("v", v.tensor_mul(tmp[:, :], tmp[:, :],
                                            sinT[:, :]))
                    p.wait(v, "v")
                    p.inc("v", v.tensor_add(A[:, sl], A[:, sl], tmp[:, :]))



def emit_oproj(p, w_dram, bias_col, oT, xout, b, accum=True, w_sb=None):
    nc = p.nc
    with contextlib.ExitStack() as es:
        if w_sb is None:
            wsb = es.enter_context(
                nc.sbuf_tensor(uname("wo_sb"), [128, NF * D], BF16))
            with nc.Block() as blk:
                @blk.gpsimd
                def _(g):
                    load_w_bf16(p, g, wsb, w_dram, D, D)
        else:
            wsb = w_sb
        osb = [es.enter_context(nc.sbuf_tensor(uname(f"osb{i}"), [128, SW], F32))
               for i in range(2)]
        pso = es.enter_context(nc.psum_tensor(uname("pso"), [128, 2 * SW], F32))

        jobs = [(dc, ts) for dc in range(NF) for ts in range(NS)]
        with nc.Block() as blk:
            t0 = p.cnt["t"]
            s0 = p.cnt["s"]
            st0 = p.st.n
            gl_n = p.gl.n

            @blk.tensor
            def _(t):
                p.gl.wait(t, gl_n)
                for i, (dc, ts) in enumerate(jobs):
                    if i >= 2:
                        p.wait(t, "s", s0 + i - 1)
                    last = None
                    for fc in range(NF):
                        last = t.matmul(
                            pso[:, (i % 2) * SW:(i % 2 + 1) * SW],
                            wsb[:, fc * D + dc * 128:fc * D + (dc + 1) * 128],
                            oT[:, fc * T + ts * SW:fc * T + (ts + 1) * SW],
                            start=(fc == 0), stop=(fc == NF - 1))
                    p.inc("t", last)

            @blk.scalar
            def _(s):
                for i, (dc, ts) in enumerate(jobs):
                    p.wait(s, "t", t0 + i + 1)
                    if i >= 2:
                        p.st.wait(s, st0 + 16 * (i - 1))
                    p.inc("s", s.activation(
                        osb[i % 2][:, :],
                        pso[:, (i % 2) * SW:(i % 2 + 1) * SW],
                        AF.Identity if bias_col is not None else AF.Copy,
                        bias=bias_col[:, dc:dc + 1] if bias_col is not None
                        else 0.0))

            @blk.gpsimd
            def _(g):
                for i, (dc, ts) in enumerate(jobs):
                    p.wait(g, "s", s0 + i + 1)
                    p.st.start(g, dap(
                        xout, (b * D + dc * 128) * T + ts * SW,
                        [[T, 128], [1, SW]]),
                        osb[i % 2][:, :], accum_op=ALU.add)

        with nc.Block() as blk:
            @blk.gpsimd
            def _(g):
                p.st.wait_all(g)



def emit_qkv_heads(p, xnorm, wq, rw, qoff, koff, voff, q_all, k_all, v_sl,
                   qscale=1.0, qbias=None, kbias=None, bvb=None,
                   do_qk=True, do_v=True):
    nc = p.nc
    gl_n = p.gl.n
    with contextlib.ExitStack() as e4:
        if not do_qk:
            psq = None
        else:
            psq = e4.enter_context(nc.psum_tensor(uname("psq"),
                                                  [64, 2 * T], F32))
        for h in range(H if do_qk else 0):
            rbase = (h % 2) * 64
            grp = h // 2
            with nc.Block() as blk:
                @blk.tensor
                def _(t):
                    p.gl.wait(t, gl_n)
                    for which, woff0 in ((0, qoff), (1, koff)):
                        woff = woff0 + h * 64
                        dst = psq[:, which * T:(which + 1) * T]
                        last = None
                        for fc in range(NF):
                            lhsT = wq[:, fc * rw + woff:fc * rw + woff + 64]
                            for ts in range(NS):
                                sl = slice(ts * SW, (ts + 1) * SW)
                                last = t.matmul(
                                    dst[:, sl], lhsT,
                                    xnorm[:, fc * T + ts * SW:
                                          fc * T + (ts + 1) * SW],
                                    start=(fc == 0), stop=(fc == NF - 1))
                        p.inc("t", last)

            with nc.Block() as blk:
                @blk.scalar
                def _(s):
                    qdst = q_all[rbase:rbase + 64, grp * T:(grp + 1) * T]
                    kdst = k_all[rbase:rbase + 64, grp * T:(grp + 1) * T]
                    if qbias is not None:
                        p.inc("s", s.activation(qdst, psq[:, 0:T],
                                                AF.Identity, bias=qbias(h),
                                                scale=qscale))
                    else:
                        p.inc("s", s.activation(qdst, psq[:, 0:T], AF.Copy,
                                                scale=qscale))
                    if kbias is not None:
                        p.inc("s", s.activation(kdst, psq[:, T:2 * T],
                                                AF.Identity, bias=kbias(h)))
                    else:
                        p.inc("s", s.activation(kdst, psq[:, T:2 * T],
                                                AF.Copy))

    with contextlib.ExitStack() as e4:
        if not do_v:
            return
        psv = e4.enter_context(nc.psum_tensor(uname("psv"), [128, 2 * SW],
                                              F32))
        for tc in range(NT):
            for hf in range(D // SW):
                slot = tc * (D // SW) + hf
                with nc.Block() as blk:
                    @blk.tensor
                    def _(t):
                        dst = psv[:, (slot % 2) * SW:(slot % 2 + 1) * SW]
                        last = None
                        for fc in range(NF):
                            last = t.matmul(
                                dst,
                                xnorm[:, fc * T + tc * 128:
                                      fc * T + (tc + 1) * 128],
                                wq[:, fc * rw + voff + hf * SW:
                                   fc * rw + voff + (hf + 1) * SW],
                                start=(fc == 0), stop=(fc == NF - 1))
                        p.inc("t", last)
                with nc.Block() as blk:
                    @blk.vector
                    def _(v):
                        src = psv[:, (slot % 2) * SW:(slot % 2 + 1) * SW]
                        for hh in range(SW // 64):
                            h2 = hf * (SW // 64) + hh
                            doff = tc * H * 65 + h2 * 65
                            if bvb is not None:
                                p.inc("v", v.scalar_tensor_tensor(
                                    v_sl[:, doff:doff + 64],
                                    src[:, hh * 64:(hh + 1) * 64], 1.0,
                                    bvb[:, h2 * 64:(h2 + 1) * 64],
                                    ALU.mult, ALU.add))
                            else:
                                p.inc("v", v.tensor_copy(
                                    v_sl[:, doff:doff + 64],
                                    src[:, hh * 64:(hh + 1) * 64]))


def emit_vsl_ones(p, v_sl):
    with p.nc.Block() as blk:
        @blk.gpsimd
        def _(g):
            for tc in range(NT):
                for h in range(H):
                    off = tc * H * 65 + h * 65 + 64
                    g.memset(v_sl[:, off:off + 1], 1.0)


def emit_attn_l0a(p, io, b):
    nc = p.nc
    scale = 1.0 / math.sqrt(DH)
    with contextlib.ExitStack() as es:
        sb = lambda n, sh, dt: es.enter_context(nc.sbuf_tensor(uname(n), sh, dt))
        q_all = sb("q_all", [128, NF * T], BF16)
        k_all = sb("k_all", [128, NF * T], BF16)
        v_sl = sb("v_sl", [128, NT * H * 65], BF16)
        bvb = sb("bvb", [128, D], F32)
        bqs_c = sb("bqs_c", [128, 8], F32)
        bqkv_c, _ = emit_vec_cols(p, es, io["b_qkv"], 3 * D)
        bout_c, _ = emit_vec_cols(p, es, io["b_out"], D)
        emit_vsl_ones(p, v_sl)

        with nc.Block() as blk:
            @blk.vector
            def _(v):
                p.inc("v", v.tensor_scalar(bqs_c[:, :], bqkv_c[:, 0:8],
                                           scale, None, ALU.mult))

        emit_vec_bcast(p, bvb, io["b_qkv"], D, src_off=2 * D)

        with contextlib.ExitStack() as e2:
            xnorm = e2.enter_context(
                nc.sbuf_tensor(uname("xnorm"), [128, NF * T], BF16))
            emit_ln_fm(p, xnorm, src_dram=io["xst"], src_dram_off=b * D * T)
            def qbias(h):
                r, c = (h * 64) % 128, (h * 64) // 128
                return bqs_c[r:r + 64, c:c + 1]

            def kbias(h):
                r, c = (D + h * 64) % 128, (D + h * 64) // 128
                return bqkv_c[r:r + 64, c:c + 1]

            with contextlib.ExitStack() as e3:
                wq = e3.enter_context(
                    nc.sbuf_tensor(uname("wqk_sb"), [128, NF * 2 * D],
                                   BF16))
                with nc.Block() as blk:
                    @blk.gpsimd
                    def _(g):
                        load_w_bf16(p, g, wq, io["w_qkv"], D, 2 * D,
                                    src_row_stride=3 * D)
                emit_qkv_heads(p, xnorm, wq, 2 * D, 0, D, None,
                               q_all, k_all, v_sl, qscale=scale,
                               qbias=qbias, kbias=kbias, do_v=False)
            with contextlib.ExitStack() as e3:
                wv = e3.enter_context(
                    nc.sbuf_tensor(uname("wv_sb"), [128, NF * D], BF16))
                with nc.Block() as blk:
                    @blk.gpsimd
                    def _(g):
                        load_w_bf16(p, g, wv, io["w_qkv"], D, D,
                                    src_row_stride=3 * D, src_off=2 * D)
                emit_qkv_heads(p, xnorm, wv, D, None, None, 0,
                               q_all, k_all, v_sl, bvb=bvb, do_qk=False)

        with contextlib.ExitStack() as e2:
            cosT, sinT = emit_rope_tables(p, e2)
            emit_rope_apply(p, q_all, cosT, sinT, e2)
            emit_rope_apply(p, k_all, cosT, sinT, e2)

        with contextlib.ExitStack() as e2:
            oT = e2.enter_context(nc.sbuf_tensor(uname("oT"),
                                                 [128, NF * T], BF16))
            attn_headloop(p, q_all=q_all, k_all=k_all, v_sl=v_sl, oT=oT,
                          causal=False)
            emit_oproj(p, io["w_out"], bout_c, oT, io["xout"], b)


def emit_bcast_row(p, dst_sb, row_sb, row_off, n):
    nc = p.nc
    with contextlib.ExitStack() as e2:
        bvp = e2.enter_context(nc.psum_tensor(uname("bvp"), [128, SW], F32))
        brb = e2.enter_context(nc.sbuf_tensor(uname("brb"), [1, n], BF16))
        with nc.Block() as blk:
            @blk.vector
            def _(v):
                p.inc("v", v.tensor_copy(brb[0:1, :],
                                         row_sb[0:1, row_off:row_off + n]))
        for j in range(n // SW):
            with nc.Block() as blk:
                @blk.tensor
                def _(t):
                    p.inc("t", t.matmul(bvp[:, :], p.onesBr[:, :],
                                        brb[0:1, j * SW:(j + 1) * SW],
                                        start=True, stop=True))
            with nc.Block() as blk:
                @blk.vector
                def _(v):
                    p.inc("v", v.tensor_copy(dst_sb[:, j * SW:(j + 1) * SW],
                                             bvp[:, :]))



NE = 8
ED = 2048
NH = ED // 128


def emit_router(p, es, xnorm2, rw_handle, sqc_all):
    nc = p.nc
    with contextlib.ExitStack() as e2:
        s2 = lambda n, sh, dt: e2.enter_context(
            nc.sbuf_tensor(uname(n), sh, dt))
        rw_sb = s2("rw_sb", [128, NF * NE], BF16)
        probs = s2("probs", [128, NE], F32)
        nmx = s2("nmx", [128, 1], F32)
        rsum = s2("rsum", [128, 1], F32)
        t1 = s2("t1", [128, 1], F32)
        t2 = s2("t2", [128, 1], F32)
        eq = s2("eq", [128, NE], F32)
        pm = s2("pm", [128, NE], F32)
        cs = s2("cs", [128, 1], F32)
        psr = e2.enter_context(nc.psum_tensor(uname("psr"), [128, NE], F32))
        with nc.Block() as blk:
            @blk.gpsimd
            def _(g):
                load_w_bf16(p, g, rw_sb, rw_handle, D, NE)
        gl_n = p.gl.n
        for tc in range(NT):
            with nc.Block() as blk:
                @blk.tensor
                def _(t):
                    p.gl.wait(t, gl_n)
                    last = None
                    for fc in range(NF):
                        last = t.matmul(
                            psr[:, :],
                            xnorm2[:, fc * T + tc * 128:
                                   fc * T + (tc + 1) * 128],
                            rw_sb[:, fc * NE:(fc + 1) * NE],
                            start=(fc == 0), stop=(fc == NF - 1))
                    p.inc("t", last)
            with nc.Block() as blk:
                @blk.vector
                def _(v):
                    p.inc("v", v.tensor_reduce(nmx[:, :], psr[:, :], AX.X,
                                               ALU.max, negate=True))

                @blk.scalar
                def _(sc):
                    p.wait(sc, "v")
                    p.inc("s", sc.activation(probs[:, :], psr[:, :], AF.Exp,
                                             bias=nmx[:, :]))
            with nc.Block() as blk:
                @blk.vector
                def _(v):
                    p.inc("v", v.tensor_reduce(t1[:, :], probs[:, :], AX.X,
                                               ALU.max))
                    p.wait(v, "v")
                    p.inc("v", v.tensor_scalar(eq[:, :], probs[:, :],
                                               t1[:, :], None, ALU.is_ge))
                    p.wait(v, "v")
                    p.inc("v", v.tensor_scalar(eq[:, :], eq[:, :], 2.0,
                                               None, ALU.mult))
                    p.wait(v, "v")
                    p.inc("v", v.tensor_sub(pm[:, :], probs[:, :], eq[:, :]))
                    p.wait(v, "v")
                    p.inc("v", v.tensor_reduce(t2[:, :], pm[:, :], AX.X,
                                               ALU.max))
                    p.wait(v, "v")
                    p.inc("v", v.tensor_add(cs[:, :], t1[:, :], t2[:, :]))
                    p.wait(v, "v")
                    p.inc("v", v.reciprocal(cs[:, :], cs[:, :]))
                    p.wait(v, "v")
                    p.inc("v", v.tensor_scalar(eq[:, :], probs[:, :],
                                               t2[:, :], None, ALU.is_ge))
                    p.wait(v, "v")
                    p.inc("v", v.tensor_mul(pm[:, :], probs[:, :], eq[:, :]))
                    p.wait(v, "v")
                    p.inc("v", v.tensor_scalar(pm[:, :], pm[:, :], cs[:, :],
                                               None, ALU.mult))

                @blk.scalar
                def _(sc):
                    p.wait(sc, "v")
                    p.inc("s", sc.activation(
                        sqc_all[:, tc * NE:(tc + 1) * NE], pm[:, :],
                        AF.Sqrt))


def emit_expert(p, io, b, e, xnorm2, sqc_all):
    nc = p.nc
    with contextlib.ExitStack() as es:
        sb = lambda n, sh, dt: es.enter_context(
            nc.sbuf_tensor(uname(n), sh, dt))
        h1_sb = sb("h1_sb", [128, NH * T], BF16)
        srow = sb("sqc_row", [1, T], BF16)
        hr = [sb(f"hr{i}", [128, SW], BF16) for i in range(2)]

        with contextlib.ExitStack() as e2:
            s2 = lambda n, sh, dt: e2.enter_context(
                nc.sbuf_tensor(uname(n), sh, dt))
            xfe = s2("xfe", [128, NF * T], BF16)
            fc1_sb = s2("fc1_sb", [128, NF * ED], BF16)
            with nc.Block() as blk:
                @blk.gpsimd
                def _(g):
                    load_w_bf16(p, g, fc1_sb, io["exp_fc1"], D, ED,
                                src_off=e * D * ED)
            gl_n = p.gl.n

            with contextlib.ExitStack() as e3:
                psw = e3.enter_context(nc.psum_tensor(uname("psw"),
                                                      [1, T], F32))
                with nc.Block() as blk:
                    @blk.tensor
                    def _(t):
                        last = None
                        for tc in range(NT):
                            last = t.matmul(
                                psw[0:1, tc * 128:(tc + 1) * 128],
                                sqc_all[:, tc * NE + e:tc * NE + e + 1],
                                p.identF[:, :], start=True, stop=True)
                        p.inc("t", last)
                with nc.Block() as blk:
                    @blk.vector
                    def _(v):
                        p.inc("v", v.tensor_copy(srow[0:1, :], psw[0:1, :]))

            with contextlib.ExitStack() as e3:
                pcb = e3.enter_context(nc.psum_tensor(uname("pcb"),
                                                      [128, SW], F32))
                for ts in range(NS):
                    with nc.Block() as blk:
                        @blk.tensor
                        def _(t):
                            p.inc("t", t.matmul(
                                pcb[:, :], p.onesBr[:, :],
                                srow[0:1, ts * SW:(ts + 1) * SW],
                                start=True, stop=True))
                    with nc.Block() as blk:
                        @blk.vector
                        def _(v):
                            for fc in range(NF):
                                p.inc("v", v.tensor_mul(
                                    xfe[:, fc * T + ts * SW:
                                        fc * T + (ts + 1) * SW],
                                    xnorm2[:, fc * T + ts * SW:
                                           fc * T + (ts + 1) * SW],
                                    pcb[:, :]))

            with contextlib.ExitStack() as e3:
                ps1 = e3.enter_context(nc.psum_tensor(uname("ps1"),
                                                      [128, 2 * SW], F32))
                jobs = [(hc, ts) for hc in range(NH) for ts in range(NS)]
                with nc.Block() as blk:
                    t0 = p.cnt["t"]
                    s0 = p.cnt["s"]

                    @blk.tensor
                    def _(t):
                        p.gl.wait(t, gl_n)
                        for i, (hc, ts) in enumerate(jobs):
                            if i >= 2:
                                p.wait(t, "s", s0 + 2 * (i - 1))
                            last = None
                            for fc in range(NF):
                                last = t.matmul(
                                    ps1[:, (i % 2) * SW:(i % 2 + 1) * SW],
                                    fc1_sb[:, fc * ED + hc * 128:
                                           fc * ED + (hc + 1) * 128],
                                    xfe[:, fc * T + ts * SW:
                                        fc * T + (ts + 1) * SW],
                                    start=(fc == 0), stop=(fc == NF - 1))
                            p.inc("t", last)

                    @blk.scalar
                    def _(sc):
                        for i, (hc, ts) in enumerate(jobs):
                            p.wait(sc, "t", t0 + i + 1)
                            p.inc("s", sc.activation(
                                hr[i % 2][:, :],
                                ps1[:, (i % 2) * SW:(i % 2 + 1) * SW],
                                AF.Relu))
                            p.wait(sc, "s")
                            p.inc("s", sc.activation(
                                h1_sb[:, hc * T + ts * SW:
                                      hc * T + (ts + 1) * SW],
                                hr[i % 2][:, :], AF.Square))

        with contextlib.ExitStack() as e2:
            s2 = lambda n, sh, dt: e2.enter_context(
                nc.sbuf_tensor(uname(n), sh, dt))
            fc2_sb = s2("fc2_sb", [128, NH * D], BF16)
            osb = [s2(f"osb{i}", [128, SW], F32) for i in range(2)]
            ps2 = e2.enter_context(nc.psum_tensor(uname("ps2"),
                                                  [128, 2 * SW], F32))
            with nc.Block() as blk:
                @blk.gpsimd
                def _(g):
                    load_w_bf16(p, g, fc2_sb, io["exp_fc2"], ED, D,
                                src_off=e * ED * D)
            gl_n = p.gl.n
            jobs = [(dc, ts) for dc in range(NF) for ts in range(NS)]
            with nc.Block() as blk:
                t0 = p.cnt["t"]
                s0 = p.cnt["s"]
                st0 = p.st.n

                @blk.tensor
                def _(t):
                    p.gl.wait(t, gl_n)
                    for i, (dc, ts) in enumerate(jobs):
                        if i >= 2:
                            p.wait(t, "s", s0 + i - 1)
                        last = None
                        for hc in range(NH):
                            last = t.matmul(
                                ps2[:, (i % 2) * SW:(i % 2 + 1) * SW],
                                fc2_sb[:, hc * D + dc * 128:
                                       hc * D + (dc + 1) * 128],
                                h1_sb[:, hc * T + ts * SW:
                                      hc * T + (ts + 1) * SW],
                                start=(hc == 0), stop=(hc == NH - 1))
                        p.inc("t", last)

                @blk.scalar
                def _(sc):
                    for i, (dc, ts) in enumerate(jobs):
                        p.wait(sc, "t", t0 + i + 1)
                        if i >= 2:
                            p.st.wait(sc, st0 + 16 * (i - 1))
                        p.inc("s", sc.activation(
                            osb[i % 2][:, :],
                            ps2[:, (i % 2) * SW:(i % 2 + 1) * SW], AF.Copy))

                @blk.gpsimd
                def _(g):
                    for i, (dc, ts) in enumerate(jobs):
                        p.wait(g, "s", s0 + i + 1)
                        p.st.start(g, dap(
                            io["xf"], (b * D + dc * 128) * T + ts * SW,
                            [[T, 128], [1, SW]]),
                            osb[i % 2][:, :], accum_op=ALU.add)

            with nc.Block() as blk:
                @blk.gpsimd
                def _(g):
                    p.st.wait_all(g)


def emit_accum_state(p, dst, src, b):
    nc = p.nc
    with contextlib.ExitStack() as es:
        xs = [es.enter_context(nc.sbuf_tensor(uname(f"acc{i}"), [128, T],
                                              F32)) for i in range(2)]
        ld0 = p.ld.n
        st0 = p.st.n
        with nc.Block() as blk:
            @blk.sync
            def _(sy):
                p.st.wait(sy, st0)
                for fc in range(NF):
                    if fc >= 2:
                        p.st.wait(sy, st0 + 16 * (fc - 1))
                    p.ld.start(sy, xs[fc % 2][:, :],
                               dap(src, (b * D + fc * 128) * T,
                                   [[T, 128], [1, T]]))

            @blk.gpsimd
            def _(g):
                for fc in range(NF):
                    p.ld.wait(g, ld0 + 16 * (fc + 1))
                    p.st.start(g, dap(dst, (b * D + fc * 128) * T,
                                      [[T, 128], [1, T]]),
                               xs[fc % 2][:, :], accum_op=ALU.add)
        with nc.Block() as blk:
            @blk.gpsimd
            def _(g):
                p.st.wait_all(g)


def emit_l0_ffn(p, io, b):
    nc = p.nc
    scale = 1.0 / math.sqrt(DH)
    with contextlib.ExitStack() as es:
        xsrc = [es.enter_context(nc.sbuf_tensor(uname(f"xs{i}"), [128, T],
                                                F32)) for i in range(NF)]
        ld0 = p.ld.n
        with nc.Block() as blk:
            @blk.sync
            def _(sy):
                for fc in range(NF):
                    p.ld.start(sy, xsrc[fc][:, :],
                               dap(io["xst"], (b * D + fc * 128) * T,
                                   [[T, 128], [1, T]]))
        with nc.Block() as blk:
            @blk.vector
            def _(v):
                p.ld.wait(v, ld0 + 16 * NF)
        emit_ln_fm(p, None, mode="ln", src_f32_sb=xsrc,
                   out_dram_fn=lambda fc: dap(
                       io["xf"], (b * D + fc * 128) * T, [[T, 128], [1, T]]))

    with contextlib.ExitStack() as es:
        sb = lambda n, sh, dt: es.enter_context(
            nc.sbuf_tensor(uname(n), sh, dt))
        q_all = sb("q_all", [128, NF * T], BF16)
        k_all = sb("k_all", [128, NF * T], BF16)
        v_sl = sb("v_sl", [128, NT * H * 65], BF16)
        emit_vsl_ones(p, v_sl)
        with contextlib.ExitStack() as e2:
            xnorm = e2.enter_context(
                nc.sbuf_tensor(uname("xnorm"), [128, NF * T], BF16))
            with nc.Block() as blk:
                @blk.gpsimd
                def _(g):
                    p.st.wait_all(g)
            emit_ln_fm(p, xnorm, src_dram=io["xf"], src_dram_off=b * D * T,
                       mode="rms")
            with contextlib.ExitStack() as e3:
                wq = e3.enter_context(
                    nc.sbuf_tensor(uname("wqk_sb"), [128, NF * 2 * D],
                                   BF16))
                with nc.Block() as blk:
                    @blk.gpsimd
                    def _(g):
                        for m, hd in enumerate((io["wq"], io["wk"])):
                            load_w_bf16(p, g, wq, hd, D, D,
                                        dst_row_width=NF * 2 * D,
                                        dst_chunk_stride=2 * D,
                                        dst_off=m * D)
                emit_qkv_heads(p, xnorm, wq, 2 * D, 0, D, None,
                               q_all, k_all, v_sl, qscale=scale,
                               do_v=False)
            with contextlib.ExitStack() as e3:
                wv = e3.enter_context(
                    nc.sbuf_tensor(uname("wv_sb"), [128, NF * D], BF16))
                with nc.Block() as blk:
                    @blk.gpsimd
                    def _(g):
                        load_w_bf16(p, g, wv, io["wv"], D, D)
                emit_qkv_heads(p, xnorm, wv, D, None, None, 0,
                               q_all, k_all, v_sl, do_qk=False)
        with contextlib.ExitStack() as e2:
            oT = e2.enter_context(nc.sbuf_tensor(uname("oT"),
                                                 [128, NF * T], BF16))
            attn_headloop(p, q_all=q_all, k_all=k_all, v_sl=v_sl, oT=oT,
                          causal=True)
            emit_oproj(p, io["wo"], None, oT, io["xf"], b)

    with contextlib.ExitStack() as es:
        xnorm2 = es.enter_context(
            nc.sbuf_tensor(uname("xnorm2"), [128, NF * T], BF16))
        sqc_all = es.enter_context(
            nc.sbuf_tensor(uname("sqc_all"), [128, NT * NE], F32))
        with nc.Block() as blk:
            @blk.gpsimd
            def _(g):
                p.st.wait_all(g)
        emit_ln_fm(p, xnorm2, src_dram=io["xf"], src_dram_off=b * D * T,
                   mode="rms")
        emit_router(p, es, xnorm2, io["router_w"], sqc_all)
        for e in range(NE):
            emit_expert(p, io, b, e, xnorm2, sqc_all)

    emit_accum_state(p, io["xout"], io["xf"], b)


def build_l0_ffn_prog():
    nc = bass.Bass("TRN2", target_bir_lowering=False)
    xst = nc.dram_tensor("xst", [B_PER_CORE, D, T], F32,
                         kind="ExternalInput")
    xout = nc.dram_tensor("xout", [B_PER_CORE, D, T], F32,
                          kind="ExternalOutput")
    io = {"xst": xst, "xout": xout}
    for nm, shape in (("wq", [D, D]), ("wk", [D, D]), ("wv", [D, D]),
                      ("wo", [D, D]), ("router_w", [D, NE]),
                      ("exp_fc1", [NE, D, ED]), ("exp_fc2", [NE, ED, D])):
        io[nm] = nc.dram_tensor(nm, shape, F32, kind="ExternalInput")
    io["xf"] = nc.dram_tensor("xf", [B_PER_CORE, D, T], F32)

    p = Prog(nc)
    emit_prelude(p)
    emit_copy_state(p, xout, xst, B_PER_CORE * D * T)
    with nc.Block() as blk:
        @blk.gpsimd
        def _(g):
            p.ld.wait_all(g)
    for b in range(B_PER_CORE):
        emit_l0_ffn(p, io, b)
    with nc.Block() as blk:
        @blk.sync
        def _(sy):
            p.ld.wait_all(sy)
            p.st.wait_all(sy)
            p.gl.wait_all(sy)
    p.close()
    return nc



NC_COL = 16
NN = 256


def emit_l1_ffn(p, io, b):
    nc = p.nc
    with contextlib.ExitStack() as es:
        sb = lambda n, sh, dt: es.enter_context(
            nc.sbuf_tensor(uname(n), sh, dt))
        xs = sb("xs", [128, NF * T], BF16)
        wcol_all = sb("wcol_all", [128, NT * NC_COL], F32)
        rnorm = sb("rnorm", [128, NT], F32)
        cembT = sb("cembT", [128, NF * NC_COL], BF16)
        gb1_c, _ = emit_vec_cols(p, es, io["gate_b1"], 512)
        bcols_c, _ = emit_vec_cols(p, es, io["b_cols"], NC_COL * NN)
        pb_c, _ = emit_vec_cols(p, es, io["proj_b"], D)
        lng_c, _ = emit_vec_cols(p, es, io["ln_g"], D)
        lnb_c, _ = emit_vec_cols(p, es, io["ln_b"], D)

        emit_ln_fm(p, xs, src_dram=io["xst"], src_dram_off=b * D * T)

        with contextlib.ExitStack() as e2:
            s2 = lambda n, sh, dt: e2.enter_context(
                nc.sbuf_tensor(uname(n), sh, dt))
            xq = [s2(f"xq{i}", [128, T], BF16) for i in range(2)]
            ssr = s2("ssr", [1, T], F32)
            with contextlib.ExitStack() as e3:
                stat = e3.enter_context(nc.psum_tensor(uname("l2st"),
                                                       [1, T], F32))
                with nc.Block() as blk:
                    s0 = p.cnt["s"]
                    t0 = p.cnt["t"]

                    @blk.scalar
                    def _(sc):
                        for fc in range(NF):
                            if fc >= 2:
                                p.wait(sc, "t", t0 + fc - 1)
                            p.inc("s", sc.activation(
                                xq[fc % 2][:, :], xs[:, fc * T:(fc + 1) * T],
                                AF.Square))

                    @blk.tensor
                    def _(t):
                        for fc in range(NF):
                            p.wait(t, "s", s0 + fc + 1)
                            last = None
                            for ts in range(NS):
                                sl = slice(ts * SW, (ts + 1) * SW)
                                last = t.matmul(stat[0:1, sl], p.onesB[:, :],
                                                xq[fc % 2][:, sl],
                                                start=(fc == 0),
                                                stop=(fc == NF - 1))
                            p.inc("t", last)

                with nc.Block() as blk:
                    @blk.vector
                    def _(v):
                        p.inc("v", v.tensor_scalar(ssr[0:1, :], stat[0:1, :],
                                                   1e-24, None, ALU.max))

                    @blk.scalar
                    def _(sc):
                        p.wait(sc, "v")
                        p.inc("s", sc.activation(ssr[0:1, :], ssr[0:1, :],
                                                 AF.Ln))
                        p.wait(sc, "s")
                        p.inc("s", sc.activation(ssr[0:1, :], ssr[0:1, :],
                                                 AF.Exp, scale=-0.5))

            with contextlib.ExitStack() as e3:
                prn = e3.enter_context(nc.psum_tensor(uname("prn"),
                                                      [128, NT], F32))
                with nc.Block() as blk:
                    @blk.tensor
                    def _(t):
                        last = None
                        for tc in range(NT):
                            last = t.matmul(
                                prn[:, tc:tc + 1],
                                ssr[0:1, tc * 128:(tc + 1) * 128],
                                p.identF[0:1, 0:1], start=True, stop=True)
                        p.inc("t", last)
                with nc.Block() as blk:
                    @blk.vector
                    def _(v):
                        p.inc("v", v.tensor_copy(rnorm[:, :], prn[:, :]))

        with contextlib.ExitStack() as e2:
            s2 = lambda n, sh, dt: e2.enter_context(
                nc.sbuf_tensor(uname(n), sh, dt))
            ce = s2("ce", [16, D], F32)
            ce2 = s2("ce2", [16, D], F32)
            cs2 = s2("cs2", [16, 1], F32)
            cen = s2("cen", [16, D], BF16)
            ld0 = p.ld.n
            with nc.Block() as blk:
                @blk.sync
                def _(sy):
                    p.ld.start(sy, ce[:, :], dap(io["col_emb"], 0,
                                                 [[D, 16], [1, D]]))
            with nc.Block() as blk:
                @blk.vector
                def _(v):
                    p.ld.wait(v, ld0 + 16)
                    p.inc("v", v.tensor_mul(ce2[:, :], ce[:, :], ce[:, :]))
                    p.wait(v, "v")
                    p.inc("v", v.tensor_reduce(cs2[:, :], ce2[:, :], AX.X,
                                               ALU.add))
                    p.wait(v, "v")
                    p.inc("v", v.tensor_scalar(cs2[:, :], cs2[:, :], 1e-24,
                                               None, ALU.max))

                @blk.scalar
                def _(sc):
                    p.wait(sc, "v")
                    p.inc("s", sc.activation(cs2[:, :], cs2[:, :], AF.Ln))
                    p.wait(sc, "s")
                    p.inc("s", sc.activation(cs2[:, :], cs2[:, :], AF.Exp,
                                             scale=-0.5))
            with nc.Block() as blk:
                @blk.vector
                def _(v):
                    p.inc("v", v.tensor_scalar(cen[:, :], ce[:, :],
                                               cs2[:, :], None, ALU.mult))
            with contextlib.ExitStack() as e3:
                pce = e3.enter_context(nc.psum_tensor(uname("pce"),
                                                      [128, NC_COL], BF16))
                for fc in range(NF):
                    with nc.Block() as blk:
                        @blk.tensor
                        def _(t):
                            p.inc("t", t.transpose(
                                pce[:, :], cen[:, fc * 128:(fc + 1) * 128],
                                p.identB[0:16, 0:16]))
                    with nc.Block() as blk:
                        @blk.vector
                        def _(v):
                            p.inc("v", v.tensor_copy(
                                cembT[:, fc * NC_COL:(fc + 1) * NC_COL],
                                pce[:, :]))

        with contextlib.ExitStack() as e2:
            s2 = lambda n, sh, dt: e2.enter_context(
                nc.sbuf_tensor(uname(n), sh, dt))
            hgel = s2("hgel", [128, 4 * T], BF16)
            w1sb = s2("w1sb", [128, NF * 512], BF16)
            with nc.Block() as blk:
                @blk.gpsimd
                def _(g):
                    load_w_bf16(p, g, w1sb, io["gate_w1"], D, 512)
            gl_n = p.gl.n
            with contextlib.ExitStack() as e3:
                ps1 = e3.enter_context(nc.psum_tensor(uname("psg1"),
                                                      [128, 2 * SW], F32))
                jobs = [(gc, ts) for gc in range(4) for ts in range(NS)]
                with nc.Block() as blk:
                    t0 = p.cnt["t"]
                    s0 = p.cnt["s"]

                    @blk.tensor
                    def _(t):
                        p.gl.wait(t, gl_n)
                        for i, (gc, ts) in enumerate(jobs):
                            if i >= 2:
                                p.wait(t, "s", s0 + i - 1)
                            last = None
                            for fc in range(NF):
                                last = t.matmul(
                                    ps1[:, (i % 2) * SW:(i % 2 + 1) * SW],
                                    w1sb[:, fc * 512 + gc * 128:
                                         fc * 512 + (gc + 1) * 128],
                                    xs[:, fc * T + ts * SW:
                                       fc * T + (ts + 1) * SW],
                                    start=(fc == 0), stop=(fc == NF - 1))
                            p.inc("t", last)

                    @blk.scalar
                    def _(sc):
                        for i, (gc, ts) in enumerate(jobs):
                            p.wait(sc, "t", t0 + i + 1)
                            p.inc("s", sc.activation(
                                hgel[:, gc * T + ts * SW:
                                     gc * T + (ts + 1) * SW],
                                ps1[:, (i % 2) * SW:(i % 2 + 1) * SW],
                                AF.Gelu, bias=gb1_c[:, gc:gc + 1]))

            w2sb = s2("w2sb", [128, 4 * NC_COL], BF16)
            b2b = s2("b2b", [128, NC_COL], F32)
            b2r = s2("b2r", [1, NC_COL], BF16)
            with nc.Block() as blk:
                @blk.gpsimd
                def _(g):
                    load_w_bf16(p, g, w2sb, io["gate_w2"], 512, NC_COL)
                    p.gl.start(g, b2r[0:1, :],
                               dap(io["gate_b2"], 0,
                                   [[NC_COL, 1], [1, NC_COL]]))
            gl_n = p.gl.n
            with contextlib.ExitStack() as e3:
                pb2 = e3.enter_context(nc.psum_tensor(uname("pb2"),
                                                      [128, NC_COL], F32))
                with nc.Block() as blk:
                    @blk.tensor
                    def _(t):
                        p.gl.wait(t, gl_n)
                        p.inc("t", t.matmul(pb2[:, :], p.onesBr[:, :],
                                            b2r[0:1, :], start=True,
                                            stop=True))
                with nc.Block() as blk:
                    @blk.vector
                    def _(v):
                        p.inc("v", v.tensor_copy(b2b[:, :], pb2[:, :]))

            with contextlib.ExitStack() as e3:
                s3 = lambda n, sh, dt: e3.enter_context(
                    nc.sbuf_tensor(uname(n), sh, dt))
                logit = s3("logit", [128, NC_COL], F32)
                gv = s3("gv", [128, NC_COL], F32)
                nmx = s3("nmx", [128, 1], F32)
                rsum = s3("rsum", [128, 1], F32)
                t1c = s3("t1c", [128, 1], F32)
                msk = s3("msk", [128, NC_COL], F32)
                psim = e3.enter_context(nc.psum_tensor(uname("psim"),
                                                       [128, NC_COL], F32))
                pg2 = e3.enter_context(nc.psum_tensor(uname("pg2"),
                                                      [128, NC_COL], F32))
                for tc in range(NT):
                    with nc.Block() as blk:
                        @blk.tensor
                        def _(t):
                            last = None
                            for fc in range(NF):
                                last = t.matmul(
                                    psim[:, :],
                                    xs[:, fc * T + tc * 128:
                                       fc * T + (tc + 1) * 128],
                                    cembT[:, fc * NC_COL:(fc + 1) * NC_COL],
                                    start=(fc == 0), stop=(fc == NF - 1))
                            p.inc("t", last)
                            for gc in range(4):
                                last = t.matmul(
                                    pg2[:, :],
                                    hgel[:, gc * T + tc * 128:
                                         gc * T + (tc + 1) * 128],
                                    w2sb[:, gc * NC_COL:(gc + 1) * NC_COL],
                                    start=(gc == 0), stop=(gc == 3))
                            p.inc("t", last)

                    with nc.Block() as blk:
                        @blk.vector
                        def _(v):
                            p.inc("v", v.tensor_add(gv[:, :], pg2[:, :],
                                                    b2b[:, :]))

                        @blk.scalar
                        def _(sc):
                            p.wait(sc, "v")
                            p.inc("s", sc.activation(gv[:, :], gv[:, :],
                                                     AF.Sigmoid))

                    with nc.Block() as blk:
                        @blk.vector
                        def _(v):
                            p.inc("v", v.tensor_scalar(
                                logit[:, :], psim[:, :],
                                rnorm[:, tc:tc + 1], None, ALU.mult))
                            p.wait(v, "v")
                            p.inc("v", v.tensor_add(logit[:, :], logit[:, :],
                                                    gv[:, :]))
                            p.wait(v, "v")
                            p.inc("v", v.tensor_reduce(nmx[:, :],
                                                       logit[:, :], AX.X,
                                                       ALU.max, negate=True))

                        @blk.scalar
                        def _(sc):
                            p.wait(sc, "v")
                            p.inc("s", sc.activation(
                                msk[:, :], logit[:, :], AF.Exp,
                                bias=nmx[:, :], accum_out=rsum[:, :]))

                    with nc.Block() as blk:
                        @blk.vector
                        def _(v):
                            p.inc("v", v.reciprocal(rsum[:, :], rsum[:, :]))
                            p.inc("v", v.tensor_reduce(t1c[:, :], logit[:, :],
                                                       AX.X, ALU.max))
                            p.wait(v, "v")
                            p.inc("v", v.tensor_scalar(
                                wcol_all[:, tc * NC_COL:(tc + 1) * NC_COL],
                                logit[:, :], t1c[:, :], None, ALU.is_ge))
                            p.wait(v, "v")
                            p.inc("v", v.tensor_mul(
                                wcol_all[:, tc * NC_COL:(tc + 1) * NC_COL],
                                wcol_all[:, tc * NC_COL:(tc + 1) * NC_COL],
                                msk[:, :]))
                            p.wait(v, "v")
                            p.inc("v", v.tensor_scalar(
                                wcol_all[:, tc * NC_COL:(tc + 1) * NC_COL],
                                wcol_all[:, tc * NC_COL:(tc + 1) * NC_COL],
                                rsum[:, :], None, ALU.mult))

        with contextlib.ExitStack() as e2:
            s2 = lambda n, sh, dt: e2.enter_context(
                nc.sbuf_tensor(uname(n), sh, dt))
            acc = s2("acc", [128, NF * T], F32)
            with nc.Block() as blk:
                @blk.gpsimd
                def _(g):
                    p.inc("g", g.memset(acc[:, :], 0.0))

            for grp in range(4):
                with contextlib.ExitStack() as e3:
                    s3 = lambda n, sh, dt: e3.enter_context(
                        nc.sbuf_tensor(uname(n), sh, dt))
                    cgel = s3("cgel", [128, 8 * T], BF16)
                    wcsb = s3("wcsb", [128, NF * 1024], BF16)
                    pjsb = s3("pjsb", [128, 8 * D], BF16)
                    wrow = s3("wrow", [1, T], BF16)
                    with nc.Block() as blk:
                        @blk.gpsimd
                        def _(g):
                            for cl in range(4):
                                load_w_bf16(
                                    p, g, wcsb, io["W_cols"], D, NN,
                                    dst_row_width=NF * 1024,
                                    dst_chunk_stride=1024,
                                    dst_off=cl * NN,
                                    src_off=(grp * 4 + cl) * D * NN)
                            load_w_bf16(p, g, pjsb, io["proj_w"], 1024, D,
                                        src_off=grp * 1024 * D)
                    gl_n = p.gl.n

                    with contextlib.ExitStack() as e4:
                        psc = e4.enter_context(nc.psum_tensor(
                            uname("psc"), [128, 2 * SW], F32))
                        jobs = [(cl, cc, ts) for cl in range(4)
                                for cc in range(2) for ts in range(NS)]
                        with nc.Block() as blk:
                            t0 = p.cnt["t"]
                            s0 = p.cnt["s"]

                            @blk.tensor
                            def _(t):
                                p.gl.wait(t, gl_n)
                                for i, (cl, cc, ts) in enumerate(jobs):
                                    if i >= 2:
                                        p.wait(t, "s", s0 + i - 1)
                                    last = None
                                    for fc in range(NF):
                                        last = t.matmul(
                                            psc[:, (i % 2) * SW:
                                                (i % 2 + 1) * SW],
                                            wcsb[:, fc * 1024 + cl * NN +
                                                 cc * 128:
                                                 fc * 1024 + cl * NN +
                                                 (cc + 1) * 128],
                                            xs[:, fc * T + ts * SW:
                                               fc * T + (ts + 1) * SW],
                                            start=(fc == 0),
                                            stop=(fc == NF - 1))
                                    p.inc("t", last)

                            @blk.scalar
                            def _(sc):
                                for i, (cl, cc, ts) in enumerate(jobs):
                                    p.wait(sc, "t", t0 + i + 1)
                                    c_abs = grp * 4 + cl
                                    bidx = c_abs * 2 + cc
                                    p.inc("s", sc.activation(
                                        cgel[:, (cl * 2 + cc) * T + ts * SW:
                                             (cl * 2 + cc) * T +
                                             (ts + 1) * SW],
                                        psc[:, (i % 2) * SW:
                                            (i % 2 + 1) * SW],
                                        AF.Gelu,
                                        bias=bcols_c[:, bidx:bidx + 1]))

                    for cl in range(4):
                        c_abs = grp * 4 + cl
                        with contextlib.ExitStack() as e4:
                            pwr = e4.enter_context(nc.psum_tensor(
                                uname("pwr"), [1, T], F32))
                            with nc.Block() as blk:
                                @blk.tensor
                                def _(t):
                                    last = None
                                    for tc in range(NT):
                                        last = t.matmul(
                                            pwr[0:1, tc * 128:(tc + 1) * 128],
                                            wcol_all[:, tc * NC_COL + c_abs:
                                                     tc * NC_COL + c_abs + 1],
                                            p.identF[:, :], start=True,
                                            stop=True)
                                    p.inc("t", last)
                            with nc.Block() as blk:
                                @blk.vector
                                def _(v):
                                    p.inc("v", v.tensor_copy(wrow[0:1, :],
                                                             pwr[0:1, :]))
                        with contextlib.ExitStack() as e4:
                            pwb = e4.enter_context(nc.psum_tensor(
                                uname("pwb"), [128, SW], F32))
                            for ts in range(NS):
                                with nc.Block() as blk:
                                    @blk.tensor
                                    def _(t):
                                        p.inc("t", t.matmul(
                                            pwb[:, :], p.onesBr[:, :],
                                            wrow[0:1, ts * SW:(ts + 1) * SW],
                                            start=True, stop=True))
                                with nc.Block() as blk:
                                    @blk.vector
                                    def _(v):
                                        for cc in range(2):
                                            sl = slice(
                                                (cl * 2 + cc) * T + ts * SW,
                                                (cl * 2 + cc) * T +
                                                (ts + 1) * SW)
                                            p.inc("v", v.tensor_mul(
                                                cgel[:, sl], cgel[:, sl],
                                                pwb[:, :]))

                    with contextlib.ExitStack() as e4:
                        psp = e4.enter_context(nc.psum_tensor(
                            uname("psp"), [128, 2 * SW], F32))
                        jobs = [(dc, ts) for dc in range(NF)
                                for ts in range(NS)]
                        with nc.Block() as blk:
                            t0 = p.cnt["t"]
                            v0 = p.cnt["v"]

                            @blk.tensor
                            def _(t):
                                for i, (dc, ts) in enumerate(jobs):
                                    if i >= 2:
                                        p.wait(t, "v", v0 + i - 1)
                                    last = None
                                    for hc in range(8):
                                        last = t.matmul(
                                            psp[:, (i % 2) * SW:
                                                (i % 2 + 1) * SW],
                                            pjsb[:, hc * D + dc * 128:
                                                 hc * D + (dc + 1) * 128],
                                            cgel[:, hc * T + ts * SW:
                                                 hc * T + (ts + 1) * SW],
                                            start=(hc == 0), stop=(hc == 7))
                                    p.inc("t", last)

                            @blk.vector
                            def _(v):
                                for i, (dc, ts) in enumerate(jobs):
                                    p.wait(v, "t", t0 + i + 1)
                                    sl = slice(dc * T + ts * SW,
                                               dc * T + (ts + 1) * SW)
                                    p.inc("v", v.tensor_add(
                                        acc[:, sl], acc[:, sl],
                                        psp[:, (i % 2) * SW:
                                            (i % 2 + 1) * SW]))

            with nc.Block() as blk:
                @blk.vector
                def _(v):
                    for fc in range(NF):
                        sl = slice(fc * T, (fc + 1) * T)
                        p.inc("v", v.tensor_scalar(
                            acc[:, sl], acc[:, sl], pb_c[:, fc:fc + 1],
                            None, ALU.add))
                        p.wait(v, "v")
                        p.inc("v", v.tensor_add(acc[:, sl], acc[:, sl],
                                                xs[:, sl]))

            emit_ln_fm(p, None, mode="ln",
                       src_fn=lambda fc: acc[:, fc * T:(fc + 1) * T],
                       gamma=lng_c, beta=lnb_c,
                       out_dram_fn=lambda fc: dap(
                           io["xout"], (b * D + fc * 128) * T,
                           [[T, 128], [1, T]]),
                       out_accum=True)


def build_l1_ffn_prog():
    nc = bass.Bass("TRN2", target_bir_lowering=False)
    xst = nc.dram_tensor("xst", [B_PER_CORE, D, T], F32,
                         kind="ExternalInput")
    xout = nc.dram_tensor("xout", [B_PER_CORE, D, T], F32,
                          kind="ExternalOutput")
    io = {"xst": xst, "xout": xout}
    for nm, shape in (("col_emb", [NC_COL, D]), ("gate_w1", [D, 512]),
                      ("gate_b1", [512]), ("gate_w2", [512, NC_COL]),
                      ("gate_b2", [NC_COL]), ("W_cols", [NC_COL, D, NN]),
                      ("b_cols", [NC_COL, NN]), ("proj_w", [4096, D]),
                      ("proj_b", [D]), ("ln_g", [D]), ("ln_b", [D])):
        io[nm] = nc.dram_tensor(nm, shape, F32, kind="ExternalInput")

    p = Prog(nc)
    emit_prelude(p)
    emit_copy_state(p, xout, xst, B_PER_CORE * D * T)
    with nc.Block() as blk:
        @blk.gpsimd
        def _(g):
            p.ld.wait_all(g)
    for b in range(B_PER_CORE):
        emit_l1_ffn(p, io, b)
    with nc.Block() as blk:
        @blk.sync
        def _(sy):
            p.ld.wait_all(sy)
            p.st.wait_all(sy)
            p.gl.wait_all(sy)
    p.close()
    return nc



def emit_wT_load(p, es, handles, dst_sb, n_mats):
    nc = p.nc
    rw = NF * n_mats * D
    with contextlib.ExitStack() as e2:
        wnat = e2.enter_context(nc.sbuf_tensor(uname("wnat"),
                                               [128, NF * D], BF16))
        pst = e2.enter_context(nc.psum_tensor(uname("pst"), [128, 128],
                                              BF16))
        for m, hd in enumerate(handles):
            with nc.Block() as blk:
                @blk.gpsimd
                def _(g):
                    load_w_bf16(p, g, wnat, hd, D, D)
            gl_n = p.gl.n
            for oc in range(NF):
                for fc in range(NF):
                    with nc.Block() as blk:
                        @blk.tensor
                        def _(t):
                            p.gl.wait(t, gl_n)
                            p.inc("t", t.transpose(
                                pst[:, :],
                                wnat[:, oc * D + fc * 128:
                                     oc * D + (fc + 1) * 128],
                                p.identB[:, :]))
                    with nc.Block() as blk:
                        @blk.vector
                        def _(v):
                            dst = dst_sb[:, fc * (n_mats * D) + m * D +
                                         oc * 128:
                                         fc * (n_mats * D) + m * D +
                                         (oc + 1) * 128]
                            p.inc("v", v.tensor_copy(dst, pst[:, :]))


def emit_l1_attn_b(p, io, b):
    nc = p.nc
    with contextlib.ExitStack() as es:
        sb = lambda n, sh, dt: es.enter_context(
            nc.sbuf_tensor(uname(n), sh, dt))
        q_all = sb("q_all", [128, NF * T], BF16)
        k_all = sb("k_all", [128, NF * T], BF16)
        v_sl = sb("v_sl", [128, NT * H * 65], BF16)
        oT = sb("xno_oT_b", [128, NF * T], BF16)
        slope_cols = sb("slope_cols", [128, H], F32)
        power_col = sb("power_col", [128, 1], F32)
        pfut_tile = sb("pfut_tile", [128, SW], BF16)
        woT = sb("woT", [128, NF * D], BF16)
        emit_vsl_ones(p, v_sl)

        with contextlib.ExitStack() as e2:
            s2 = lambda n, sh, dt: e2.enter_context(
                nc.sbuf_tensor(uname(n), sh, dt))
            srow = s2("srow", [1, H], BF16)
            prow = s2("prow", [1, 1], BF16)
            tmpr = s2("tmpr", [1, H], F32)
            tmpp = s2("tmpp", [1, 1], F32)
            ld0 = p.ld.n
            with nc.Block() as blk:
                @blk.sync
                def _(sy):
                    p.ld.start(sy, tmpr[0:1, :],
                               dap(io["alibi_slopes"], 0, [[H, 1], [1, H]]))
                    p.ld.start(sy, tmpp[0:1, :],
                               dap(io["power"], 0, [[1, 1], [1, 1]]))
            with nc.Block() as blk:
                @blk.vector
                def _(v):
                    p.ld.wait(v, ld0 + 32)
                    p.inc("v", v.tensor_copy(srow[0:1, :], tmpr[0:1, :]))
                    p.inc("v", v.tensor_copy(prow[0:1, :], tmpp[0:1, :]))
            with contextlib.ExitStack() as e3:
                pbs = e3.enter_context(nc.psum_tensor(uname("pbs"),
                                                      [128, H + 1], F32))
                with nc.Block() as blk:
                    @blk.tensor
                    def _(t):
                        p.inc("t", t.matmul(pbs[:, 0:H], p.onesBr[:, :],
                                            srow[0:1, :], start=True,
                                            stop=True))
                        p.inc("t", t.matmul(pbs[:, H:H + 1], p.onesBr[:, :],
                                            prow[0:1, :], start=True,
                                            stop=True))
                with nc.Block() as blk:
                    @blk.vector
                    def _(v):
                        p.inc("v", v.tensor_scalar(slope_cols[:, :],
                                                   pbs[:, 0:H], -1.0, None,
                                                   ALU.mult))
                        p.inc("v", v.tensor_copy(power_col[:, :],
                                                 pbs[:, H:H + 1]))

        with contextlib.ExitStack() as e2:
            s2 = lambda n, sh, dt: e2.enter_context(
                nc.sbuf_tensor(uname(n), sh, dt))
            prow2 = s2("prow2", [1, SW], F32)
            prow2b = s2("prow2b", [1, SW], BF16)
            pfp = e2.enter_context(nc.psum_tensor(uname("pfp"),
                                                  [128, SW], F32))
            with nc.Block() as blk:
                @blk.scalar
                def _(sc):
                    p.inc("s", sc.activation(
                        prow2[0:1, 0:1], power_col[0:1, 0:1], AF.Exp,
                        scale=-math.log(20002.0)))
            with nc.Block() as blk:
                @blk.gpsimd
                def _(g):
                    p.inc("g", g.memset(prow2b[0:1, :], 1.0))
            with nc.Block() as blk:
                @blk.vector
                def _(v):
                    p.inc("v", v.tensor_scalar(prow2b[0:1, :],
                                               prow2b[0:1, :],
                                               prow2[0:1, 0:1], None,
                                               ALU.mult))
            with nc.Block() as blk:
                @blk.tensor
                def _(t):
                    p.inc("t", t.matmul(pfp[:, :], p.onesBr[:, :],
                                        prow2b[0:1, :], start=True,
                                        stop=True))
            with nc.Block() as blk:
                @blk.vector
                def _(v):
                    p.inc("v", v.tensor_copy(pfut_tile[:, :], pfp[:, :]))

        with contextlib.ExitStack() as e2:
            xnorm = oT
            emit_ln_fm(p, xnorm, src_dram=io["xst"], src_dram_off=b * D * T)
            with contextlib.ExitStack() as e3:
                wqkT = e3.enter_context(
                    nc.sbuf_tensor(uname("wqkT"), [128, NF * 2 * D], BF16))
                emit_wT_load(p, e3, (io["Wq"], io["Wk"]), wqkT, 2)
                emit_qkv_heads(p, xnorm, wqkT, 2 * D, 0, D, None,
                               q_all, k_all, v_sl, qscale=2.0, do_v=False)
            with contextlib.ExitStack() as e3:
                wvT = e3.enter_context(
                    nc.sbuf_tensor(uname("wvT"), [128, NF * D], BF16))
                emit_wT_load(p, e3, (io["Wv"],), wvT, 1)
                emit_qkv_heads(p, xnorm, wvT, D, None, None, 0,
                               q_all, k_all, v_sl, do_qk=False)
        emit_wT_load(p, es, (io["Wo"],), woT, 1)

        with contextlib.ExitStack() as e2:
            s2 = lambda n, sh, dt: e2.enter_context(
                nc.sbuf_tensor(uname(n), sh, dt))
            aslab = s2("aslab", [128, NT * SW], F32)
            probsT = [s2(f"probsT{i}", [128, SW], BF16) for i in range(2)]
            sc_sb = [s2(f"scsb{i}", [128, SW], F32) for i in range(2)]
            r_sb = [s2(f"rsb{i}", [128, SW], F32) for i in range(2)]
            a_sb = [s2(f"asb{i}", [128, SW], F32) for i in range(2)]
            d_sb = [s2(f"dsb{i}", [128, SW], F32) for i in range(2)]
            rrow = s2("rrow", [1, SW], BF16)
            otmp = s2("otmp", [64, SW], F32)

            for st in range(NS):
                ncz = min((st + 1) * (SW // 128), NT)
                with nc.Block() as blk:
                    @blk.gpsimd
                    def _(g):
                        for c in range(ncz):
                            p.inc("g", g.iota(
                                aslab[:, c * SW:(c + 1) * SW], [[-1, SW]],
                                base=c * 128 - st * SW,
                                channel_multiplier=1,
                                allow_small_or_imprecise_dtypes=True))
                with nc.Block() as blk:
                    @blk.scalar
                    def _(sc):
                        for c in range(ncz):
                            p.inc("s", sc.activation(
                                aslab[:, c * SW:(c + 1) * SW],
                                aslab[:, c * SW:(c + 1) * SW], AF.Abs))

                for h in range(H):
                    rbase = (h % 2) * 64
                    grp = h // 2
                    qsl = q_all[rbase:rbase + 64,
                                grp * T + st * SW:grp * T + (st + 1) * SW]
                    with contextlib.ExitStack() as e3:
                        psB = e3.enter_context(nc.psum_tensor(
                            uname("psB"), [128, 2 * SW], F32))
                        psPV = e3.enter_context(nc.psum_tensor(
                            uname("psPV"), [65, SW], F32))
                        pbc = e3.enter_context(nc.psum_tensor(
                            uname("pbc"), [64, SW], F32))

                        with nc.Block() as blk:
                            t0 = p.cnt["t"]
                            v0 = p.cnt["v"]
                            g0 = p.cnt["g"]
                            s0 = p.cnt["s"]

                            seq = [("mmB", 0), ("mmB", 1)][:min(2, ncz)]
                            for c in range(2, ncz):
                                seq.append(("PV", c - 2))
                                seq.append(("mmB", c))
                            for j in range(max(0, ncz - 2), NT):
                                seq.append(("PV", j))
                            t_at = {}

                            @blk.tensor
                            def _(t):
                                for kind, c in seq:
                                    if kind == "mmB":
                                        if c >= 2:
                                            p.wait(t, "v",
                                                   v0 + (c - 2) * 4 + 1)
                                        p.inc("t", t.matmul(
                                            psB[:, (c % 2) * SW:
                                                (c % 2 + 1) * SW],
                                            k_all[rbase:rbase + 64,
                                                  grp * T + c * 128:
                                                  grp * T + (c + 1) * 128],
                                            qsl, start=True, stop=True))
                                    else:
                                        if c < ncz:
                                            p.wait(t, "s", s0 + c * 3 + 3)
                                            rhs = probsT[c % 2][:, :]
                                        else:
                                            rhs = pfut_tile[:, :]
                                        p.inc("t", t.matmul(
                                            psPV[:, :],
                                            v_sl[:, c * (H * 65) + h * 65:
                                                 c * (H * 65) + h * 65 + 65],
                                            rhs, start=(c == 0),
                                            stop=(c == NT - 1)))
                                    t_at[(kind, c)] = p.cnt["t"]

                            @blk.vector
                            def _(v):
                                for c in range(ncz):
                                    p.wait(v, "t", t_at[("mmB", c)])
                                    if c >= 2:
                                        p.wait(v, "g", g0 + c - 1)
                                    p.inc("v", v.scalar_tensor_tensor(
                                        sc_sb[c % 2][:, :],
                                        aslab[:, c * SW:(c + 1) * SW],
                                        slope_cols[:, h:h + 1],
                                        psB[:, (c % 2) * SW:
                                            (c % 2 + 1) * SW],
                                        ALU.mult, ALU.add))
                                    p.wait(v, "g", g0 + c + 1)
                                    p.inc("v", v.tensor_scalar(
                                        r_sb[c % 2][:, :],
                                        sc_sb[c % 2][:, :], 0.0, None,
                                        ALU.max))
                                    p.wait(v, "v")
                                    p.inc("v", v.scalar_tensor_tensor(
                                        a_sb[c % 2][:, :],
                                        r_sb[c % 2][:, :], 2.0,
                                        sc_sb[c % 2][:, :], ALU.mult,
                                        ALU.subtract))
                                    p.wait(v, "s", s0 + c * 3 + 2)
                                    p.inc("v", v.tensor_sub(
                                        d_sb[c % 2][:, :],
                                        r_sb[c % 2][:, :],
                                        a_sb[c % 2][:, :]))

                            @blk.gpsimd
                            def _(g):
                                for c in range(ncz):
                                    p.wait(g, "v", v0 + c * 4 + 1)
                                    p.inc("g", g.affine_select(
                                        sc_sb[c % 2][:, :],
                                        sc_sb[c % 2][:, :], [[1, SW]],
                                        ALU.is_ge, p.r_m10k,
                                        base=st * SW - c * 128,
                                        channel_multiplier=-1))

                            @blk.scalar
                            def _(sc):
                                for c in range(ncz):
                                    p.wait(sc, "v", v0 + c * 4 + 3)
                                    p.inc("s", sc.activation(
                                        r_sb[c % 2][:, :],
                                        r_sb[c % 2][:, :], AF.Ln,
                                        scale=2.0, bias=1.0))
                                    p.inc("s", sc.activation(
                                        a_sb[c % 2][:, :],
                                        a_sb[c % 2][:, :], AF.Ln,
                                        scale=2.0, bias=2.0))
                                    p.wait(sc, "v", v0 + c * 4 + 4)
                                    if c >= 2:
                                        p.wait(sc, "t",
                                               t_at[("PV", c - 2)])
                                    p.inc("s", sc.activation(
                                        probsT[c % 2][:, :],
                                        d_sb[c % 2][:, :], AF.Exp,
                                        scale=power_col[:, :]))

                        with nc.Block() as blk:
                            @blk.scalar
                            def _(sc):
                                p.inc("s", sc.activation(
                                    rrow[0:1, :], psPV[64:65, :], AF.Ln,
                                    bias=1e-6))
                                p.wait(sc, "s")
                                p.inc("s", sc.activation(
                                    rrow[0:1, :], rrow[0:1, :], AF.Exp,
                                    scale=-1.0))
                                p.wait(sc, "s")
                                p.inc("s", sc.activation(
                                    otmp[:, :], psPV[0:64, :], AF.Copy))

                        with nc.Block() as blk:
                            @blk.tensor
                            def _(t):
                                p.inc("t", t.matmul(
                                    pbc[:, :], p.onesBr[0:1, 0:64],
                                    rrow[0:1, :], start=True, stop=True))

                        with nc.Block() as blk:
                            @blk.vector
                            def _(v):
                                dst = oT[rbase:rbase + 64,
                                         grp * T + st * SW:
                                         grp * T + (st + 1) * SW]
                                p.inc("v", v.tensor_mul(dst, otmp[:, :],
                                                        pbc[:, :]))

        emit_oproj(p, None, None, oT, io["xout"], b, w_sb=woT)


def build_l1_attn_b_prog():
    nc = bass.Bass("TRN2", target_bir_lowering=False)
    xst = nc.dram_tensor("xst", [B_PER_CORE, D, T], F32,
                         kind="ExternalInput")
    xout = nc.dram_tensor("xout", [B_PER_CORE, D, T], F32,
                          kind="ExternalOutput")
    io = {"xst": xst, "xout": xout}
    for nm, shape in (("Wq", [D, D]), ("Wk", [D, D]), ("Wv", [D, D]),
                      ("Wo", [D, D]), ("alibi_slopes", [H]),
                      ("power", [1])):
        io[nm] = nc.dram_tensor(nm, shape, F32, kind="ExternalInput")
    p = Prog(nc)
    emit_prelude(p)
    register_consts(p, [2.0, 1e-6])
    emit_copy_state(p, xout, xst, B_PER_CORE * D * T)
    with nc.Block() as blk:
        @blk.gpsimd
        def _(g):
            p.ld.wait_all(g)
    for b in range(B_PER_CORE):
        emit_l1_attn_b(p, io, b)
    with nc.Block() as blk:
        @blk.sync
        def _(sy):
            p.ld.wait_all(sy)
            p.st.wait_all(sy)
            p.gl.wait_all(sy)
    p.close()
    return nc



PD = 4
_PI_IDX = [0, 0, 0, 1, 1, 2]
_PJ_IDX = [1, 2, 3, 2, 3, 3]
_J6_SIGN = [1.0, -1.0, 1.0, 1.0, -1.0, 1.0]


def emit_l1_attn_a(p, io, b):
    nc = p.nc
    scale = 1.0 / math.sqrt(DH)
    with contextlib.ExitStack() as es:
        sb = lambda n, sh, dt: es.enter_context(
            nc.sbuf_tensor(uname(n), sh, dt))
        q_all = sb("q_all", [128, NF * T], BF16)
        k_all = sb("k_all", [128, NF * T], BF16)
        v_sl = sb("v_sl", [128, NT * H * 65], BF16)
        gmul = sb("gmul", [128, NT * H], F32)
        bq_c, _ = emit_vec_cols(p, es, io["qkv_b"], 3 * D)
        bo_c, _ = emit_vec_cols(p, es, io["out_b"], D)
        bqs_c = sb("bqs_c", [128, 8], F32)
        kvbb = sb("kvbb", [128, 2 * D], F32)
        emit_vsl_ones(p, v_sl)

        with nc.Block() as blk:
            @blk.vector
            def _(v):
                p.inc("v", v.tensor_scalar(bqs_c[:, :], bq_c[:, 0:8],
                                           scale, None, ALU.mult))
        emit_vec_bcast(p, kvbb, io["qkv_b"], 2 * D, src_off=D)

        with contextlib.ExitStack() as e2:
            xnorm = e2.enter_context(
                nc.sbuf_tensor(uname("xnorm"), [128, NF * T], BF16))
            emit_ln_fm(p, xnorm, src_dram=io["xst"], src_dram_off=b * D * T)
            es_mem = e2.enter_context(contextlib.ExitStack())
            sbm = lambda n, sh, dt: es_mem.enter_context(
                nc.sbuf_tensor(uname(n), sh, dt))
            lndc = sbm("lndc", [128, H], F32)
            lnd2c = sbm("lnd2c", [128, H], F32)
            acol = sbm("acol", [128, H], F32)
            msc = sbm("msc", [128, H], F32)
            kvgc = sbm("kvgc", [128, 1], F32)
            dcols = sbm("dcols", [128, H * 4], F32)
            doffb = sbm("doffb", [128, H * 16], F32)
            pcol = sbm("pcol", [128, 1], F32)
            with contextlib.ExitStack() as e2:
                s2 = lambda n, sh, dt: e2.enter_context(
                    nc.sbuf_tensor(uname(n), sh, dt))
                dr = s2("dr", [1, H], F32)
                msr = s2("msr", [1, H], F32)
                kvr = s2("kvr", [1, 1], F32)
                drb = s2("drb", [1, H], BF16)
                msrb = s2("msrb", [1, H], BF16)
                kvrb = s2("kvrb", [1, 1], BF16)
                offr = s2("offr", [1, 16], F32)
                pbs = e2.enter_context(nc.psum_tensor(uname("pbs"),
                                                      [128, 2 * H + 1], F32))
                ld0 = p.ld.n
                with nc.Block() as blk:
                    @blk.sync
                    def _(sy):
                        p.ld.start(sy, dr[0:1, :],
                                   dap(io["decay_logits"], 0, [[H, 1], [1, H]]))
                        p.ld.start(sy, msr[0:1, :],
                                   dap(io["mem_scale"], 0, [[H, 1], [1, H]]))
                        p.ld.start(sy, kvr[0:1, :],
                                   dap(io["kv_gate"], 0, [[1, 1], [1, 1]]))

                    @blk.gpsimd
                    def _(g):
                        g.iota(offr[0:1, :], [[128, 16]], base=1,
                               channel_multiplier=0,
                               allow_small_or_imprecise_dtypes=True)
                        g.iota(pcol[:, :], [[0, 1]], base=0,
                               channel_multiplier=1,
                               allow_small_or_imprecise_dtypes=True)

                with nc.Block() as blk:
                    @blk.scalar
                    def _(sc):
                        p.ld.wait(sc, ld0 + 48)
                        p.inc("s", sc.activation(dr[0:1, :], dr[0:1, :],
                                                 AF.Sigmoid))
                        p.wait(sc, "s")
                        p.inc("s", sc.activation(dr[0:1, :], dr[0:1, :],
                                                 AF.Ln))

                with nc.Block() as blk:
                    @blk.vector
                    def _(v):
                        p.inc("v", v.tensor_copy(drb[0:1, :], dr[0:1, :]))
                        p.inc("v", v.tensor_copy(msrb[0:1, :], msr[0:1, :]))
                        p.inc("v", v.tensor_copy(kvrb[0:1, :], kvr[0:1, :]))
                with nc.Block() as blk:
                    @blk.tensor
                    def _(t):
                        p.inc("t", t.matmul(pbs[:, 0:H], p.onesBr[:, :],
                                            drb[0:1, :], start=True, stop=True))
                        p.inc("t", t.matmul(pbs[:, H:2 * H], p.onesBr[:, :],
                                            msrb[0:1, :], start=True, stop=True))
                        p.inc("t", t.matmul(pbs[:, 2 * H:2 * H + 1],
                                            p.onesBr[:, :], kvrb[0:1, :],
                                            start=True, stop=True))
                with nc.Block() as blk:
                    @blk.vector
                    def _(v):
                        p.inc("v", v.tensor_copy(lndc[:, :], pbs[:, 0:H]))
                        p.inc("v", v.tensor_copy(msc[:, :], pbs[:, H:2 * H]))
                        p.inc("v", v.tensor_copy(kvgc[:, :],
                                                 pbs[:, 2 * H:2 * H + 1]))
                        p.wait(v, "v")
                        p.inc("v", v.tensor_scalar(lnd2c[:, :], lndc[:, :], 0.5,
                                                   None, ALU.mult))

                with nc.Block() as blk:
                    @blk.scalar
                    def _(sc):
                        for h in range(H):
                            p.inc("s", sc.activation(
                                acol[:, h:h + 1], pcol[:, :], AF.Exp,
                                scale=lndc[:, h:h + 1]))
                            for tm in range(4):
                                pass

                with contextlib.ExitStack() as e3:
                    ecol = e3.enter_context(nc.sbuf_tensor(uname("ecol"),
                                                           [128, 4], F32))
                    with nc.Block() as blk:
                        @blk.gpsimd
                        def _(g):
                            for tm in range(4):
                                g.iota(ecol[:, tm:tm + 1], [[0, 1]],
                                       base=511 - tm * 128,
                                       channel_multiplier=-1,
                                       allow_small_or_imprecise_dtypes=True)
                    with nc.Block() as blk:
                        @blk.scalar
                        def _(sc):
                            for h in range(H):
                                for tm in range(4):
                                    p.inc("s", sc.activation(
                                        dcols[:, h * 4 + tm:h * 4 + tm + 1],
                                        ecol[:, tm:tm + 1], AF.Exp,
                                        scale=lnd2c[:, h:h + 1]))
                    with contextlib.ExitStack() as e4:
                        dofr = e4.enter_context(nc.sbuf_tensor(
                            uname("dofr"), [1, H * 16], BF16))
                        pdo = e4.enter_context(nc.psum_tensor(
                            uname("pdo"), [128, H * 16], F32))
                        with nc.Block() as blk:
                            @blk.scalar
                            def _(sc):
                                for h in range(H):
                                    p.inc("s", sc.activation(
                                        dofr[0:1, h * 16:(h + 1) * 16],
                                        offr[0:1, :], AF.Exp,
                                        scale=lndc[0:1, h:h + 1]))
                        with nc.Block() as blk:
                            @blk.tensor
                            def _(t):
                                for j in range(H * 16 // 128):
                                    p.inc("t", t.matmul(
                                        pdo[:, j * 128:(j + 1) * 128],
                                        p.onesBr[:, :],
                                        dofr[0:1, j * 128:(j + 1) * 128],
                                        start=True, stop=True))
                        with nc.Block() as blk:
                            @blk.vector
                            def _(v):
                                p.inc("v", v.tensor_copy(doffb[:, :],
                                                         pdo[:, :]))


            with contextlib.ExitStack() as e3:
                wqq = e3.enter_context(
                    nc.sbuf_tensor(uname("wqq_sb"), [128, NF * D], BF16))
                with nc.Block() as blk:
                    @blk.gpsimd
                    def _(g):
                        load_w_bf16(p, g, wqq, io["qkv_w"], D, D,
                                    src_row_stride=3 * D)
                gl_n = p.gl.n
                psq = e3.enter_context(nc.psum_tensor(uname("psq"),
                                                      [64, T], F32))
                for h in range(H):
                    rbase = (h % 2) * 64
                    grp = h // 2
                    with nc.Block() as blk:
                        @blk.tensor
                        def _(t):
                            p.gl.wait(t, gl_n)
                            last = None
                            for fc in range(NF):
                                lhsT = wqq[:, fc * D + h * 64:
                                           fc * D + h * 64 + 64]
                                for ts in range(NS):
                                    sl = slice(ts * SW, (ts + 1) * SW)
                                    last = t.matmul(
                                        psq[:, sl], lhsT,
                                        xnorm[:, fc * T + ts * SW:
                                              fc * T + (ts + 1) * SW],
                                        start=(fc == 0), stop=(fc == NF - 1))
                            p.inc("t", last)
                    with nc.Block() as blk:
                        @blk.scalar
                        def _(sc):
                            r, c = (h * 64) % 128, (h * 64) // 128
                            p.inc("s", sc.activation(
                                q_all[rbase:rbase + 64,
                                      grp * T:(grp + 1) * T],
                                psq[:, :], AF.Identity,
                                bias=bqs_c[r:r + 64, c:c + 1], scale=scale))

            jwb = es_mem.enter_context(
                nc.sbuf_tensor(uname("jw_b"), [128, NT * 96], BF16))
            rlb = es_mem.enter_context(
                nc.sbuf_tensor(uname("rl_b"), [128, NT * 96], BF16))
            es_ln = es_mem.enter_context(contextlib.ExitStack())
            wln = es_ln.enter_context(
                nc.sbuf_tensor(uname("wline_sb"), [128, NF * 4 * 64], BF16))
            with nc.Block() as blk:
                @blk.gpsimd
                def _(g):
                    for m, hd in enumerate((io["w1_write"], io["w2_write"],
                                            io["w1_read"], io["w2_read"])):
                        load_w_bf16(p, g, wln, hd, D, 64,
                                    dst_row_width=NF * 256,
                                    dst_chunk_stride=256, dst_off=m * 64)
            gl_n = p.gl.n
            wl = es_ln.enter_context(
                nc.sbuf_tensor(uname("wl_slab"), [128, NT * 256], F32))
            stg8 = es_ln.enter_context(
                nc.sbuf_tensor(uname("stg8"), [128, NF * 128], BF16))
            with nc.Block() as blk:
                @blk.gpsimd
                def _(g):
                    for fc in range(NF):
                        p.inc("g", g.memset(
                            stg8[:, fc * 128:fc * 128 + 1], 0.0))

                @blk.vector
                def _(v):
                    for fc in range(NF):
                        p.inc("v", v.tensor_copy(
                            stg8[:, fc * 128 + 1:(fc + 1) * 128],
                            xnorm[:, fc * T:fc * T + 127]))

            with contextlib.ExitStack() as e3:
                psl = e3.enter_context(nc.psum_tensor(uname("psl"),
                                                      [128, 2 * 512], F32))
                with nc.Block() as blk:
                    t0 = p.cnt["t"]
                    v0 = p.cnt["v"]

                    @blk.tensor
                    def _(t):
                        p.gl.wait(t, gl_n)
                        for tc in range(NT):
                            if tc >= 2:
                                p.wait(t, "v", v0 + tc - 1)
                            dst = psl[:, (tc % 2) * 512:(tc % 2) * 512 + 256]
                            last = None
                            for fc in range(NF):
                                last = t.matmul(
                                    dst[:, 64:256],
                                    xnorm[:, fc * T + tc * 128:
                                          fc * T + (tc + 1) * 128],
                                    wln[:, fc * 256 + 64:fc * 256 + 256],
                                    start=(fc == 0), stop=(fc == NF - 1))
                            for fc in range(NF):
                                lhsT = (stg8[:, fc * 128:(fc + 1) * 128]
                                        if tc == 0 else
                                        xnorm[:, fc * T + tc * 128 - 1:
                                              fc * T + (tc + 1) * 128 - 1])
                                last = t.matmul(
                                    dst[:, 0:64], lhsT,
                                    wln[:, fc * 256:fc * 256 + 64],
                                    start=(fc == 0), stop=(fc == NF - 1))
                            p.inc("t", last)

                    @blk.vector
                    def _(v):
                        for tc in range(NT):
                            p.wait(v, "t", t0 + tc + 1)
                            p.inc("v", v.tensor_copy(
                                wl[:, tc * 256:(tc + 1) * 256],
                                psl[:, (tc % 2) * 512:
                                    (tc % 2) * 512 + 256]))

            wli = es_ln.enter_context(
                nc.sbuf_tensor(uname("wlines"), [128, NT * 96], F32))
            rli = es_ln.enter_context(
                nc.sbuf_tensor(uname("rlines"), [128, NT * 96], F32))
            with contextlib.ExitStack() as e3:
                s3 = lambda n, sh, dt: e3.enter_context(
                    nc.sbuf_tensor(uname(n), sh, dt))
                tmp = s3("ext_tmp", [128, NT * 96], F32)
                nsq = s3("ext_nsq", [128, NT * 16], F32)
                for (dst, o1, o2) in ((wli, 0, 64), (rli, 128, 192)):
                    with nc.Block() as blk:
                        @blk.vector
                        def _(v):
                            for i in range(6):
                                w1ap = dap(wl, o1 + _PI_IDX[i],
                                           [[NT * 256, 128], [256, NT],
                                            [4, 16]])
                                w2ap = dap(wl, o2 + _PJ_IDX[i],
                                           [[NT * 256, 128], [256, NT],
                                            [4, 16]])
                                w1bp = dap(wl, o1 + _PJ_IDX[i],
                                           [[NT * 256, 128], [256, NT],
                                            [4, 16]])
                                w2bp = dap(wl, o2 + _PI_IDX[i],
                                           [[NT * 256, 128], [256, NT],
                                            [4, 16]])
                                oap = dap(dst, i,
                                          [[NT * 96, 128], [96, NT],
                                           [6, 16]])
                                tap = dap(tmp, i,
                                          [[NT * 96, 128], [96, NT],
                                           [6, 16]])
                                p.inc("v", v.tensor_mul(oap, w1ap, w2ap))
                                p.inc("v", v.tensor_mul(tap, w1bp, w2bp))
                            p.wait(v, "v")
                            p.inc("v", v.tensor_sub(dst[:, :], dst[:, :],
                                                    tmp[:, :]))

                for (srcl, dstb, scaled) in ((wli, jwb, True),
                                             (rli, rlb, False)):
                    with nc.Block() as blk:
                        @blk.scalar
                        def _(sc):
                            p.inc("s", sc.activation(tmp[:, :], srcl[:, :],
                                                     AF.Square))
                    with nc.Block() as blk:
                        @blk.vector
                        def _(v):
                            p.inc("v", v.tensor_reduce(
                                nsq[:, :],
                                dap(tmp, 0, [[NT * 96, 128], [6, NT * 16],
                                             [1, 6]]),
                                AX.X, ALU.add))
                            p.wait(v, "v")
                            p.inc("v", v.tensor_scalar(
                                nsq[:, :], nsq[:, :], 1e-24, None, ALU.max))

                    with nc.Block() as blk:
                        @blk.scalar
                        def _(sc):
                            p.inc("s", sc.activation(nsq[:, :], nsq[:, :],
                                                     AF.Ln))
                            p.wait(sc, "s")
                            p.inc("s", sc.activation(nsq[:, :], nsq[:, :],
                                                     AF.Exp, scale=-0.5))
                    with nc.Block() as blk:
                        @blk.vector
                        def _(v):
                            if not scaled:
                                p.inc("v", v.tensor_mul(
                                    dap(dstb, 0, [[NT * 96, 128], [96, NT],
                                                  [6, 16], [1, 6]]),
                                    dap(srcl, 0, [[NT * 96, 128], [96, NT],
                                                  [6, 16], [1, 6]]),
                                    dap(nsq, 0, [[NT * 16, 128], [16, NT],
                                                 [1, 16], [0, 6]])))
                            else:
                                for i in range(6):
                                    p.inc("v", v.scalar_tensor_tensor(
                                        dap(dstb, i,
                                            [[NT * 96, 128], [96, NT],
                                             [6, 16]]),
                                        dap(srcl, 5 - i,
                                            [[NT * 96, 128], [96, NT],
                                             [6, 16]]),
                                        _J6_SIGN[i],
                                        dap(nsq, 0,
                                            [[NT * 16, 128], [16, NT],
                                             [1, 16]]),
                                        ALU.mult, ALU.mult))
            es_ln.close()

            rlT = es_mem.enter_context(nc.sbuf_tensor(uname("rlT"),
                                                      [6, T], BF16))
            jwT = es_mem.enter_context(nc.sbuf_tensor(uname("jwT"),
                                                      [6, T], BF16))
            jwsT = es_mem.enter_context(nc.sbuf_tensor(uname("jwsT"),
                                                       [6, T], BF16))
            jwtmp = es_mem.enter_context(nc.sbuf_tensor(uname("jwtmp"),
                                                        [128, 6], BF16))
            mem = es_mem.enter_context(nc.sbuf_tensor(uname("mem"),
                                                      [128, NT], F32))
            c1c = es_mem.enter_context(nc.sbuf_tensor(uname("c1c"),
                                                      [128, 1], F32))
            c2c = es_mem.enter_context(nc.sbuf_tensor(uname("c2c"),
                                                      [128, 1], F32))
            junk = es_mem.enter_context(nc.sbuf_tensor(uname("junk"),
                                                       [128, SW], BF16))
            t1s = es_mem.enter_context(nc.sbuf_tensor(uname("t1s"),
                                                      [128, SW], BF16))
            wdg = es_mem.enter_context(nc.sbuf_tensor(uname("wdg"),
                                                      [128, SW], BF16))
            dmast = es_mem.enter_context(nc.sbuf_tensor(uname("dmast"),
                                                        [128, 4 * SW],
                                                        F32))
            with nc.Block() as blk:
                @blk.gpsimd
                def _(g):
                    for tm in range(4):
                        p.inc("g", g.iota(
                            dmast[:, tm * SW:(tm + 1) * SW], [[-1, SW]],
                            base=tm * 128, channel_multiplier=1,
                            allow_small_or_imprecise_dtypes=True))

            for h in range(H):
                rbase = (h % 2) * 64
                grp = h // 2
                with contextlib.ExitStack() as e3:
                    ptr = e3.enter_context(nc.psum_tensor(uname("ptr"),
                                                          [6, 2048], BF16))
                    with nc.Block() as blk:
                        t0 = p.cnt["t"]
                        v0 = p.cnt["v"]

                        @blk.vector
                        def _(v):
                            for tc in range(NT):
                                sl6 = slice(tc * 96 + h * 6,
                                            tc * 96 + h * 6 + 6)
                                p.inc("v", v.tensor_scalar(
                                    jwtmp[:, :], jwb[:, sl6],
                                    dcols[:, h * 4 + (tc % 4):
                                          h * 4 + (tc % 4) + 1],
                                    None, ALU.mult))
                                p.wait(v, "t", t0 + 3 * (tc + 1))
                                so = (tc % 2) * 1024
                                p.inc("v", v.tensor_copy(
                                    rlT[:, tc * 128:(tc + 1) * 128],
                                    ptr[:, so:so + 128]))
                                p.inc("v", v.tensor_copy(
                                    jwT[:, tc * 128:(tc + 1) * 128],
                                    ptr[:, so + 256:so + 384]))
                                p.inc("v", v.tensor_copy(
                                    jwsT[:, tc * 128:(tc + 1) * 128],
                                    ptr[:, so + 512:so + 640]))

                        @blk.tensor
                        def _(t):
                            for tc in range(NT):
                                sl6 = slice(tc * 96 + h * 6,
                                            tc * 96 + h * 6 + 6)
                                p.wait(t, "v", v0 + 4 * tc + 1)
                                so = (tc % 2) * 1024
                                p.inc("t", t.transpose(
                                    ptr[:, so:so + 128], rlb[:, sl6],
                                    p.identB[:, :]))
                                p.inc("t", t.transpose(
                                    ptr[:, so + 256:so + 384], jwb[:, sl6],
                                    p.identB[:, :]))
                                p.inc("t", t.transpose(
                                    ptr[:, so + 512:so + 640], jwtmp[:, :],
                                    p.identB[:, :]))

                with contextlib.ExitStack() as e3:
                    psi = e3.enter_context(nc.psum_tensor(uname("psi"),
                                                          [128, SW], F32))
                    pairs = []
                    for ti in range(NT):
                        for sc_ in range(ti // (SW // 128) + 1):
                            pairs.append((ti, sc_,
                                          (sc_ + 1) * SW <= ti * 128))
                    with nc.Block() as blk:
                        t0 = p.cnt["t"]
                        v0 = p.cnt["v"]
                        s0 = p.cnt["s"]
                        g0 = p.cnt["g"]
                        t_of, s_of, v_of, g_of = [], [], [], []
                        tt = ss = vv = 0
                        gg = 1
                        for (ti, sc_, past) in pairs:
                            tt += 1
                            ss += 1
                            vv += 2 if past else 3
                            gg += 0 if past else 1
                            t_of.append(tt)
                            s_of.append(ss)
                            v_of.append(vv)
                            g_of.append(gg)

                        @blk.gpsimd
                        def _(g):
                            p.inc("g", g.memset(mem[:, :], 0.0))
                            for i, (ti, sc_, past) in enumerate(pairs):
                                if not past:
                                    p.wait(g, "s", s0 + s_of[i])
                                    tm = ti % (SW // 128)
                                    p.inc("g", g.affine_select(
                                        wdg[:, :], wdg[:, :], [[-1, SW]],
                                        ALU.is_gt, 0.0, base=tm * 128,
                                        channel_multiplier=1))

                        @blk.tensor
                        def _(t):
                            for i, (ti, sc_, past) in enumerate(pairs):
                                if i >= 1:
                                    if pairs[i - 1][2]:
                                        p.wait(t, "s", s0 + s_of[i - 1])
                                    else:
                                        p.wait(t, "v", v0 + v_of[i - 1])
                                p.inc("t", t.matmul(
                                    psi[:, :],
                                    rlT[:, ti * 128:(ti + 1) * 128],
                                    (jwsT if past else jwT)[:,
                                        sc_ * SW:(sc_ + 1) * SW],
                                    start=True, stop=True))

                        @blk.scalar
                        def _(scx):
                            for i, (ti, sc_, past) in enumerate(pairs):
                                p.wait(scx, "t", t0 + t_of[i])
                                if past:
                                    if i >= 1:
                                        p.wait(scx, "v", v0 + v_of[i - 1])
                                    p.inc("s", scx.activation(
                                        junk[:, :], psi[:, :], AF.Square,
                                        accum_out=c1c[:, :]))
                                else:
                                    tm = ti % (SW // 128)
                                    p.inc("s", scx.activation(
                                        wdg[:, :],
                                        dmast[:, tm * SW:(tm + 1) * SW],
                                        AF.Exp, scale=lndc[:, h:h + 1]))

                        @blk.vector
                        def _(v):
                            for i, (ti, sc_, past) in enumerate(pairs):
                                if past:
                                    if i == 0:
                                        p.wait(v, "g", g0 + 1)
                                    p.wait(v, "s", s0 + s_of[i])
                                    p.inc("v", v.tensor_scalar(
                                        c2c[:, :], c1c[:, :],
                                        acol[:, h:h + 1], None, ALU.mult))
                                    p.wait(v, "v")
                                    koff = ti - (SW // 128) * sc_ - \
                                        (SW // 128)
                                    p.inc("v", v.scalar_tensor_tensor(
                                        mem[:, ti:ti + 1], c2c[:, :],
                                        doffb[:, h * 16 + koff:
                                              h * 16 + koff + 1],
                                        mem[:, ti:ti + 1], ALU.mult,
                                        ALU.add))
                                else:
                                    p.wait(v, "g", g0 + g_of[i])
                                    p.wait(v, "t", t0 + t_of[i])
                                    p.inc("v", v.tensor_mul(
                                        t1s[:, :], wdg[:, :], psi[:, :]))
                                    p.wait(v, "v")
                                    p.inc("v", v.scalar_tensor_tensor(
                                        junk[:, :], t1s[:, :], 1.0,
                                        psi[:, :], ALU.mult, ALU.mult,
                                        accum_out=c1c[:, :]))
                                    p.wait(v, "v")
                                    p.inc("v", v.tensor_add(
                                        mem[:, ti:ti + 1],
                                        mem[:, ti:ti + 1], c1c[:, :]))

                with nc.Block() as blk:
                    @blk.vector
                    def _(v):
                        p.inc("v", v.tensor_scalar(
                            mem[:, :], mem[:, :], msc[:, h:h + 1], None,
                            ALU.mult))

                    @blk.scalar
                    def _(scx):
                        p.wait(scx, "v")
                        p.inc("s", scx.activation(mem[:, :], mem[:, :],
                                                  AF.Sigmoid))
                with nc.Block() as blk:
                    @blk.vector
                    def _(v):
                        for tc in range(NT):
                            p.inc("v", v.tensor_scalar(
                                gmul[:, tc * H + h:tc * H + h + 1],
                                mem[:, tc:tc + 1], kvgc[:, :], None,
                                ALU.mult))

            es_mem.close()
            with contextlib.ExitStack() as e3:
                s3 = lambda n, sh, dt: e3.enter_context(
                    nc.sbuf_tensor(uname(n), sh, dt))
                wkv = s3("wkv_sb", [128, NF * 2 * D], BF16)
                wgeo = s3("wgeo_sb", [128, NF * 2 * D], BF16)
                with nc.Block() as blk:
                    @blk.gpsimd
                    def _(g):
                        load_w_bf16(p, g, wkv, io["qkv_w"], D, 2 * D,
                                    src_row_stride=3 * D, src_off=D)
                        load_w_bf16(p, g, wgeo, io["kv_geo_w"], D, 2 * D)
                gl_n2 = p.gl.n
                kvt = s3("kvt", [128, 128], F32)
                geo = s3("geo", [128, 128], F32)
                ken = s3("ken", [128, 64], BF16)
                pskv = e3.enter_context(nc.psum_tensor(uname("pskv"),
                                                       [128, 128], F32))
                psge = e3.enter_context(nc.psum_tensor(uname("psge"),
                                                       [128, 128], F32))
                pkt = e3.enter_context(nc.psum_tensor(uname("pkt"),
                                                      [64, 128], BF16))
                for h in range(H):
                    rbase = (h % 2) * 64
                    grp = h // 2
                    with nc.Block() as blk:
                        t0 = p.cnt["t"]
                        v0 = p.cnt["v"]
                        s0 = p.cnt["s"]

                        @blk.tensor
                        def _(t):
                            p.gl.wait(t, gl_n2)
                            for tc in range(NT):
                                if tc >= 1:
                                    p.wait(t, "v", v0 + 4 * (tc - 1) + 2)
                                    p.wait(t, "s", s0 + tc)
                                last = None
                                for fc in range(NF):
                                    rhs = dap(wkv, fc * 2 * D + h * 64,
                                              [[NF * 2 * D, 128], [D, 2],
                                               [1, 64]])
                                    last = t.matmul(
                                        pskv[:, :],
                                        xnorm[:, fc * T + tc * 128:
                                              fc * T + (tc + 1) * 128],
                                        rhs, start=(fc == 0),
                                        stop=(fc == NF - 1))
                                p.inc("t", last)
                                for fc in range(NF):
                                    rhs = dap(wgeo, fc * 2 * D + h * 64,
                                              [[NF * 2 * D, 128], [D, 2],
                                               [1, 64]])
                                    last = t.matmul(
                                        psge[:, :],
                                        xnorm[:, fc * T + tc * 128:
                                              fc * T + (tc + 1) * 128],
                                        rhs, start=(fc == 0),
                                        stop=(fc == NF - 1))
                                p.inc("t", last)
                                p.wait(t, "v", v0 + 4 * tc + 2)
                                p.inc("t", t.transpose(
                                    pkt[:, :], ken[:, :], p.identB[:, :]))

                        @blk.scalar
                        def _(scx):
                            for tc in range(NT):
                                p.wait(scx, "t", t0 + 3 * tc + 2)
                                if tc >= 1:
                                    p.wait(scx, "v", v0 + 4 * (tc - 1) + 3)
                                p.inc("s", scx.activation(
                                    geo[:, :], psge[:, :], AF.Copy))

                        @blk.vector
                        def _(v):
                            for tc in range(NT):
                                gc = gmul[:, tc * H + h:tc * H + h + 1]
                                p.wait(v, "t", t0 + 3 * tc + 1)
                                bap = dap(kvbb, h * 64,
                                          [[2 * D, 128], [D, 2], [1, 64]])
                                p.inc("v", v.tensor_add(kvt[:, :],
                                                        pskv[:, :], bap))
                                p.wait(v, "s", s0 + tc + 1)
                                p.wait(v, "v")
                                p.inc("v", v.scalar_tensor_tensor(
                                    ken[:, :], geo[:, 0:64], gc,
                                    kvt[:, 0:64], ALU.mult, ALU.add))
                                doff = tc * H * 65 + h * 65
                                p.inc("v", v.scalar_tensor_tensor(
                                    v_sl[:, doff:doff + 64],
                                    geo[:, 64:128], gc, kvt[:, 64:128],
                                    ALU.mult, ALU.add))
                                p.wait(v, "t", t0 + 3 * tc + 3)
                                p.inc("v", v.tensor_copy(
                                    k_all[rbase:rbase + 64,
                                          grp * T + tc * 128:
                                          grp * T + (tc + 1) * 128],
                                    pkt[:, :]))

        with contextlib.ExitStack() as e2:
            oT = e2.enter_context(nc.sbuf_tensor(uname("oT"),
                                                 [128, NF * T], BF16))
            attn_headloop(p, q_all=q_all, k_all=k_all, v_sl=v_sl, oT=oT,
                          causal=True)
            emit_oproj(p, io["out_w"], bo_c, oT, io["xout"], b)


def build_l1_attn_a_prog():
    nc = bass.Bass("TRN2", target_bir_lowering=False)
    xst = nc.dram_tensor("xst", [B_PER_CORE, D, T], F32,
                         kind="ExternalInput")
    xout = nc.dram_tensor("xout", [B_PER_CORE, D, T], F32,
                          kind="ExternalOutput")
    io = {"xst": xst, "xout": xout}
    for nm, shape in (("qkv_w", [D, 3 * D]), ("qkv_b", [3 * D]),
                      ("w1_write", [D, 64]), ("w2_write", [D, 64]),
                      ("w1_read", [D, 64]), ("w2_read", [D, 64]),
                      ("kv_geo_w", [D, 2 * D]), ("out_w", [D, D]),
                      ("out_b", [D]), ("kv_gate", [1]),
                      ("mem_scale", [H]), ("decay_logits", [H])):
        io[nm] = nc.dram_tensor(nm, shape, F32, kind="ExternalInput")
    p = Prog(nc)
    emit_prelude(p)
    emit_copy_state(p, xout, xst, B_PER_CORE * D * T)
    with nc.Block() as blk:
        @blk.gpsimd
        def _(g):
            p.ld.wait_all(g)
    for b in range(B_PER_CORE):
        emit_l1_attn_a(p, io, b)
    with nc.Block() as blk:
        @blk.sync
        def _(sy):
            p.ld.wait_all(sy)
            p.st.wait_all(sy)
            p.gl.wait_all(sy)
    p.close()
    return nc



EM = 2047


def emit_l0_attn_b(p, io, b):
    nc = p.nc
    scale = 1.0 / math.sqrt(DH)
    with contextlib.ExitStack() as es:
        sb = lambda n, sh, dt: es.enter_context(
            nc.sbuf_tensor(uname(n), sh, dt))
        oT = sb("xno_oT", [128, NF * T], BF16)
        Efm = sb("Efm", [64, EM], BF16)
        Erv = sb("Erv", [64, EM], BF16)
        bqs_c = sb("bqs_c", [128, 8], F32)
        bq_c, _ = emit_vec_cols(p, es, io["bq"], D)
        bk_c, _ = emit_vec_cols(p, es, io["bk"], D)
        bo_c, _ = emit_vec_cols(p, es, io["bo"], D)
        lng_c, _ = emit_vec_cols(p, es, io["ln_gamma"], D)
        lnb_c, _ = emit_vec_cols(p, es, io["ln_beta"], D)
        maxE = sb("maxE", [1, 1], F32)
        es_qkv = es.enter_context(contextlib.ExitStack())
        sbq = lambda n, sh, dt: es_qkv.enter_context(
            nc.sbuf_tensor(uname(n), sh, dt))
        q_all = sbq("q_all", [128, NF * T], BF16)
        k_all = sbq("k_all", [128, NF * T], BF16)
        v_sl = sbq("v_sl", [128, NT * H * 65], BF16)
        bvb = sbq("bvb", [128, D], F32)
        emit_vsl_ones(p, v_sl)

        with nc.Block() as blk:
            @blk.vector
            def _(v):
                p.inc("v", v.tensor_scalar(bqs_c[:, :], bq_c[:, 0:8],
                                           scale, None, ALU.mult))
        emit_vec_bcast(p, bvb, io["bv"], D)

        with contextlib.ExitStack() as e2:
            s2 = lambda n, sh, dt: e2.enter_context(
                nc.sbuf_tensor(uname(n), sh, dt))
            edat = s2("edat", [128, 16 * 64], BF16)
            esq = s2("esq", [64, EM], BF16)
            erow = s2("erow", [1, EM], F32)
            with nc.Block() as blk:
                @blk.gpsimd
                def _(g):
                    p.gl.start(g, dap(edat, 0, [[16 * 64, 128], [64, 15],
                                               [1, 64]]),
                               dap(io["dist_emb"], 0,
                                   [[64, 128], [128 * 64, 15], [1, 64]]))
                    p.gl.start(g, dap(edat, 15 * 64,
                                      [[16 * 64, 127], [1, 64]]),
                               dap(io["dist_emb"], 15 * 128 * 64,
                                   [[64, 127], [1, 64]]))
            gl_n = p.gl.n
            with contextlib.ExitStack() as e3:
                pse = e3.enter_context(nc.psum_tensor(uname("pse"),
                                                      [64, 128], BF16))
                for c in range(16):
                    nrows = 128 if c < 15 else 127
                    with nc.Block() as blk:
                        @blk.tensor
                        def _(t):
                            p.gl.wait(t, gl_n)
                            p.inc("t", t.transpose(
                                pse[:, 0:nrows],
                                edat[0:nrows, c * 64:(c + 1) * 64],
                                p.identB[0:nrows, 0:nrows]))
                    with nc.Block() as blk:
                        @blk.vector
                        def _(v):
                            p.inc("v", v.tensor_copy(
                                Efm[:, c * 128:c * 128 + nrows],
                                pse[:, 0:nrows]))
                            p.inc("v", v.tensor_copy(
                                dap(Erv, 2046 - c * 128 - (nrows - 1),
                                    [[EM, 64], [1, nrows]]),
                                dap(pse, nrows - 1, [[128, 64],
                                                    [-1, nrows]])))

            with nc.Block() as blk:
                @blk.scalar
                def _(sc):
                    p.inc("s", sc.activation(esq[:, :], Efm[:, :],
                                             AF.Square))
            with contextlib.ExitStack() as e3:
                psn = e3.enter_context(nc.psum_tensor(uname("psn"),
                                                      [1, EM], F32))
                with nc.Block() as blk:
                    @blk.tensor
                    def _(t):
                        last = None
                        for j in range(4):
                            w = SW if j < 3 else EM - 3 * SW
                            last = t.matmul(
                                psn[0:1, j * SW:j * SW + w],
                                p.onesB[0:64, :], esq[:, j * SW:j * SW + w],
                                start=True, stop=True)
                        p.inc("t", last)
                with nc.Block() as blk:
                    @blk.vector
                    def _(v):
                        p.inc("v", v.tensor_reduce(erow[0:1, 0:1],
                                                   psn[0:1, :], AX.X,
                                                   ALU.max))

                    @blk.scalar
                    def _(sc):
                        p.wait(sc, "v")
                        p.inc("s", sc.activation(maxE[0:1, :],
                                                 erow[0:1, 0:1], AF.Sqrt))

        with contextlib.ExitStack() as e2:
            xnorm = oT
            emit_ln_fm(p, xnorm, src_dram=io["xst"], src_dram_off=b * D * T)
            with nc.Block() as blk:
                @blk.gpsimd
                def _(g):
                    p.st.start(g, dap(io["hn"], b * D * T,
                                      [[T, 128], [T * 128, NF], [1, T]]),
                               dap(xnorm, 0, [[NF * T, 128], [T, NF],
                                              [1, T]]))
            def qbias(h):
                r, c = (h * 64) % 128, (h * 64) // 128
                return bqs_c[r:r + 64, c:c + 1]

            def kbias(h):
                r, c = (h * 64) % 128, (h * 64) // 128
                return bk_c[r:r + 64, c:c + 1]

            with contextlib.ExitStack() as e3:
                wq = e3.enter_context(
                    nc.sbuf_tensor(uname("wqk_sb"), [128, NF * 2 * D],
                                   BF16))
                with nc.Block() as blk:
                    @blk.gpsimd
                    def _(g):
                        for m, hd in enumerate((io["Wq"], io["Wk"])):
                            load_w_bf16(p, g, wq, hd, D, D,
                                        dst_row_width=NF * 2 * D,
                                        dst_chunk_stride=2 * D,
                                        dst_off=m * D)
                emit_qkv_heads(p, xnorm, wq, 2 * D, 0, D, None,
                               q_all, k_all, v_sl, qscale=scale,
                               qbias=qbias, kbias=kbias, do_v=False)
            with contextlib.ExitStack() as e3:
                wv = e3.enter_context(
                    nc.sbuf_tensor(uname("wv_sb"), [128, NF * D], BF16))
                with nc.Block() as blk:
                    @blk.gpsimd
                    def _(g):
                        load_w_bf16(p, g, wv, io["Wv"], D, D)
                emit_qkv_heads(p, xnorm, wv, D, None, None, 0,
                               q_all, k_all, v_sl, bvb=bvb, do_qk=False)

        with contextlib.ExitStack() as e2:
            s2 = lambda n, sh, dt: e2.enter_context(
                nc.sbuf_tensor(uname(n), sh, dt))
            qaug = s2("qaug", [65, T], BF16)
            kaug = s2("kaug", [65, T], BF16)
            mcols = s2("mcols", [128, NT], F32)
            probsT = s2("probsT", [128, NT * SW], BF16)
            rrow = s2("rrow", [1, SW], BF16)
            otmp = s2("otmp", [64, SW], F32)
            tstg = [s2(f"tstg{i}", [128, EM + 1], BF16) for i in range(2)]
            relq = [s2(f"relq{i}", [128, SW], BF16) for i in range(2)]
            relk = [s2(f"relk{i}", [128, SW], BF16) for i in range(2)]
            qe_r = s2("qe_r", [1, 2 * T], BF16)
            ke_r = s2("ke_r", [1, 2 * T], BF16)
            bnd = s2("bnd", [1, 4], F32)
            ones_sw = s2("ones_sw", [1, SW], BF16)

            with nc.Block() as blk:
                @blk.gpsimd
                def _(g):
                    g.memset(kaug[64:65, :], 1.0)
                    g.memset(ones_sw[0:1, :], 1.0)

            for h in range(H):
                rbase = (h % 2) * 64
                grp = h // 2
                qsl_all = q_all[rbase:rbase + 64, grp * T:(grp + 1) * T]
                ksl_all = k_all[rbase:rbase + 64, grp * T:(grp + 1) * T]
                with nc.Block() as blk:
                    @blk.vector
                    def _(v):
                        p.inc("v", v.tensor_copy(qaug[0:64, :], qsl_all))
                        p.inc("v", v.tensor_copy(kaug[0:64, :], ksl_all))

                for (src, Etab, dst_dram, tscale) in (
                        (qaug, Erv, io["qd"], 1.0),
                        (kaug, Efm, io["kd"], scale)):
                    with contextlib.ExitStack() as e3:
                        pst = e3.enter_context(nc.psum_tensor(
                            uname("pst"), [128, 2048], F32))
                        with nc.Block() as blk:
                            t0 = p.cnt["t"]
                            s0 = p.cnt["s"]
                            st0 = p.st.n

                            @blk.tensor
                            def _(t):
                                for tc in range(NT):
                                    if tc >= 1:
                                        p.wait(t, "s", s0 + tc)
                                    last = None
                                    for j in range(4):
                                        w = SW if j < 3 else EM - 3 * SW
                                        last = t.matmul(
                                            pst[:, j * SW:j * SW + w],
                                            src[0:64,
                                                tc * 128:(tc + 1) * 128],
                                            Etab[:, j * SW:j * SW + w],
                                            start=True, stop=True)
                                    p.inc("t", last)

                            @blk.scalar
                            def _(sc):
                                for tc in range(NT):
                                    p.wait(sc, "t", t0 + tc + 1)
                                    if tc >= 2:
                                        p.st.wait(sc, st0 + 16 * (tc - 1))
                                    p.inc("s", sc.activation(
                                        tstg[tc % 2][:, 0:EM],
                                        pst[:, 0:EM], AF.Copy,
                                        scale=tscale))

                            @blk.gpsimd
                            def _(g):
                                for tc in range(NT):
                                    p.wait(g, "s", s0 + tc + 1)
                                    p.st.start(g, dap(
                                        dst_dram, tc * 128 * EM,
                                        [[EM, 128], [1, EM]]),
                                        tstg[tc % 2][:, 0:EM])


                with contextlib.ExitStack() as e3:
                    psb = e3.enter_context(nc.psum_tensor(
                        uname("psb"), [1, T], F32))
                    psn2 = e3.enter_context(nc.psum_tensor(
                        uname("psn2"), [1, T], F32))
                    for (i, (srcA, col)) in enumerate(
                            ((qaug, 0), (qaug, 2046),
                             (kaug, 0), (kaug, 2046))):
                        with nc.Block() as blk:
                            @blk.tensor
                            def _(t):
                                last = None
                                for ts in range(NS):
                                    last = t.matmul(
                                        psb[0:1, ts * SW:(ts + 1) * SW],
                                        Efm[:, col:col + 1],
                                        srcA[0:64, ts * SW:(ts + 1) * SW],
                                        start=True, stop=True)
                                p.inc("t", last)
                        with nc.Block() as blk:
                            @blk.vector
                            def _(v):
                                dst = (qe_r if i < 2 else ke_r)
                                off = (i % 2) * T
                                if i < 2:
                                    p.inc("v", v.tensor_copy(
                                        dst[0:1, off:off + T], psb[0:1, :]))
                                else:
                                    p.inc("v", v.tensor_scalar(
                                        dst[0:1, off:off + T], psb[0:1, :],
                                        scale, None, ALU.mult))

                    with contextlib.ExitStack() as e4:
                        sqb = e4.enter_context(nc.sbuf_tensor(
                            uname("sqb"), [64, T], BF16))
                        for (i, srcA) in ((0, qaug), (1, kaug)):
                            with nc.Block() as blk:
                                @blk.scalar
                                def _(sc):
                                    p.inc("s", sc.activation(
                                        sqb[:, :], srcA[0:64, :],
                                        AF.Square))
                            with nc.Block() as blk:
                                @blk.tensor
                                def _(t):
                                    last = None
                                    for ts in range(NS):
                                        last = t.matmul(
                                            psn2[0:1,
                                                 ts * SW:(ts + 1) * SW],
                                            p.onesB[0:64, :],
                                            sqb[:, ts * SW:(ts + 1) * SW],
                                            start=True, stop=True)
                                    p.inc("t", last)
                            with nc.Block() as blk:
                                @blk.vector
                                def _(v):
                                    p.inc("v", v.tensor_reduce(
                                        bnd[0:1, i:i + 1], psn2[0:1, :],
                                        AX.X, ALU.max))
                    with nc.Block() as blk:
                        @blk.scalar
                        def _(sc):
                            p.inc("s", sc.activation(bnd[0:1, 0:2],
                                                     bnd[0:1, 0:2],
                                                     AF.Sqrt))
                    with nc.Block() as blk:
                        @blk.vector
                        def _(v):
                            p.inc("v", v.tensor_scalar(
                                bnd[0:1, 1:2], bnd[0:1, 1:2], scale, None,
                                ALU.mult))
                            p.wait(v, "v")
                            p.inc("v", v.tensor_add(bnd[0:1, 2:3],
                                                    bnd[0:1, 0:1],
                                                    bnd[0:1, 1:2]))
                            p.wait(v, "v")
                            p.inc("v", v.tensor_scalar(
                                bnd[0:1, 2:3], bnd[0:1, 2:3],
                                maxE[0:1, :], None, ALU.mult))

                with contextlib.ExitStack() as esA:
                    psA = esA.enter_context(
                        nc.psum_tensor(uname("psA"), [128, 2 * T], F32))
                    with nc.Block() as blk:
                        tA0 = p.cnt["t"]
                        vA0 = p.cnt["v"]

                        @blk.tensor
                        def _(t):
                            for ti in range(NT):
                                if ti >= 2:
                                    p.wait(t, "v", vA0 + ti - 1)
                                dst = psA[:, (ti % 2) * T:(ti % 2) * T + T]
                                last = None
                                for ts in range(NS):
                                    sl = slice(ts * SW, (ts + 1) * SW)
                                    last = t.matmul(
                                        dst[:, sl],
                                        qaug[0:64, ti * 128:(ti + 1) * 128],
                                        kaug[0:64, sl], start=True,
                                        stop=True)
                                p.inc("t", last)

                        @blk.vector
                        def _(v):
                            for ti in range(NT):
                                p.wait(v, "t", tA0 + ti + 1)
                                src = psA[:, (ti % 2) * T:(ti % 2) * T + T]
                                p.inc("v", v.tensor_reduce(
                                    mcols[:, ti:ti + 1], src, AX.X,
                                    ALU.max, negate=True))

                with contextlib.ExitStack() as esM:
                    pmr = esM.enter_context(nc.psum_tensor(uname("pmr"),
                                                           [1, T], F32))
                    with nc.Block() as blk:
                        @blk.tensor
                        def _(t):
                            last = None
                            for ti in range(NT):
                                last = t.transpose(
                                    pmr[0:1, ti * 128:(ti + 1) * 128],
                                    mcols[:, ti:ti + 1], p.identF[:, :])
                            p.inc("t", last)
                    with nc.Block() as blk:
                        @blk.vector
                        def _(v):
                            p.inc("v", v.tensor_scalar(
                                qaug[64:65, :], pmr[0:1, :], bnd[0:1, 2:3],
                                None, ALU.subtract))

                for st in range(NS):
                    with contextlib.ExitStack() as esB:
                        psB = esB.enter_context(nc.psum_tensor(
                            uname("psB"), [128, 2 * SW], F32))
                        psPV = esB.enter_context(nc.psum_tensor(
                            uname("psPV"), [65, SW], F32))
                        pbc = esB.enter_context(nc.psum_tensor(
                            uname("pbc"), [64, SW], F32))
                        pcl = esB.enter_context(nc.psum_tensor(
                            uname("pcl"), [128, SW], F32))

                        plans = []
                        for c in range(NT):
                            dlr = st * SW - c * 128
                            if dlr <= -1535 or dlr >= 1151:
                                plans.append((c, "A", dlr))
                            elif -896 <= dlr <= 512:
                                plans.append((c, "B", dlr))
                            else:
                                plans.append((c, "C", dlr))

                        t_end, v_end, s_end, g_end, ld_end = \
                            [], [], [], [], []
                        tt = vv = ss = gg = 0
                        ldn0 = p.ld.n
                        ldc = 0
                        for (c, typ, dlr) in plans:
                            if typ != "A":
                                ldc += 2
                            tt += 1 if typ == "A" else (2 if typ == "C"
                                                        else 1)
                            if typ == "B":
                                vv += 2
                            elif typ == "C":
                                vv += 4
                            gg += 3 if typ == "C" else 0
                            ss += 1
                            t_end.append(tt)
                            v_end.append(vv)
                            s_end.append(ss)
                            g_end.append(gg)
                            ld_end.append(ldc)

                        with nc.Block() as blk:
                            t0 = p.cnt["t"]
                            v0 = p.cnt["v"]
                            s0 = p.cnt["s"]
                            g0 = p.cnt["g"]

                            @blk.sync
                            def _(sy):
                                for i, (c, typ, dlr) in enumerate(plans):
                                    if typ == "A":
                                        continue
                                    if i >= 2 and plans[i - 2][1] != "A":
                                        p.wait(sy, "v", v0 + v_end[i - 2])
                                    p.ld.start(sy, relk[c % 2][:, :], dap(
                                        io["kd"],
                                        c * 128 * 2046 + st * SW + 1023,
                                        [[2046, 128], [1, SW]]))
                                    p.ld.start(sy, relq[c % 2][:, :], dap(
                                        io["qd"],
                                        st * SW * 2046 + c * 128 + 1023,
                                        [[2046, SW], [1, 128]]),
                                        transpose=True)

                            @blk.tensor
                            def _(t):
                                for i, (c, typ, dlr) in enumerate(plans):
                                    pslot = psB[:, (c % 2) * SW:
                                                (c % 2 + 1) * SW]
                                    if i >= 2:
                                        p.wait(t, "s", s0 + s_end[i - 2])
                                    if typ == "A":
                                        eo = 0 if dlr <= -1535 else T
                                        t.matmul(
                                            pslot,
                                            kaug[:, c * 128:(c + 1) * 128],
                                            qaug[:, st * SW:(st + 1) * SW],
                                            start=True, stop=False)
                                        t.matmul(
                                            pslot,
                                            ke_r[0:1, eo + c * 128:
                                                 eo + (c + 1) * 128],
                                            ones_sw[0:1, :], start=False,
                                            stop=False)
                                        p.inc("t", t.matmul(
                                            pslot, p.onesBr[0:1, :],
                                            qe_r[0:1, eo + st * SW:
                                                 eo + (st + 1) * SW],
                                            start=False, stop=True))
                                    else:
                                        p.inc("t", t.matmul(
                                            pslot,
                                            kaug[:, c * 128:(c + 1) * 128],
                                            qaug[:, st * SW:(st + 1) * SW],
                                            start=True, stop=True))
                                        if typ == "C":
                                            prev = [j for j in range(i)
                                                    if plans[j][1] == "C"]
                                            if prev:
                                                p.wait(t, "v", v0 +
                                                       v_end[prev[-1]] - 2)
                                            eo = 0 if dlr < -896 else T
                                            t.matmul(
                                                pcl[:, :],
                                                ke_r[0:1, eo + c * 128:
                                                     eo + (c + 1) * 128],
                                                ones_sw[0:1, :],
                                                start=True, stop=False)
                                            p.inc("t", t.matmul(
                                                pcl[:, :], p.onesBr[0:1, :],
                                                qe_r[0:1, eo + st * SW:
                                                     eo + (st + 1) * SW],
                                                start=False, stop=True))

                            @blk.vector
                            def _(v):
                                for i, (c, typ, dlr) in enumerate(plans):
                                    if typ == "A":
                                        continue
                                    p.ld.wait(v, ldn0 + 16 * ld_end[i])
                                    p.inc("v", v.tensor_add(
                                        relk[c % 2][:, :],
                                        relk[c % 2][:, :],
                                        relq[c % 2][:, :]))
                                    if typ == "C":
                                        p.wait(v, "t", t0 + t_end[i])
                                        p.inc("v", v.tensor_copy(
                                            relq[c % 2][:, :], pcl[:, :]))
                                        p.wait(v, "g", g0 + g_end[i])
                                        p.inc("v", v.tensor_add(
                                            relk[c % 2][:, :],
                                            relk[c % 2][:, :],
                                            relq[c % 2][:, :]))
                                    else:
                                        p.wait(v, "t", t0 + t_end[i])
                                    pslot = psB[:, (c % 2) * SW:
                                                (c % 2 + 1) * SW]
                                    p.wait(v, "v")
                                    p.inc("v", v.tensor_add(
                                        pslot, pslot, relk[c % 2][:, :]))

                            @blk.gpsimd
                            def _(g):
                                for i, (c, typ, dlr) in enumerate(plans):
                                    if typ != "C":
                                        continue
                                    p.wait(g, "v", v0 + v_end[i] - 3)
                                    p.inc("g", g.affine_select(
                                        relk[c % 2][:, :],
                                        relk[c % 2][:, :], [[1, SW]],
                                        ALU.is_ge, 0.0, base=dlr + 1023,
                                        channel_multiplier=-1))
                                    p.inc("g", g.affine_select(
                                        relk[c % 2][:, :],
                                        relk[c % 2][:, :], [[-1, SW]],
                                        ALU.is_ge, 0.0,
                                        base=2046 - dlr - 1023,
                                        channel_multiplier=1))
                                    p.wait(g, "v", v0 + v_end[i] - 2)
                                    if dlr < -896:
                                        p.inc("g", g.affine_select(
                                            relq[c % 2][:, :],
                                            relq[c % 2][:, :], [[-1, SW]],
                                            ALU.is_gt, 0.0,
                                            base=-(dlr + 1023),
                                            channel_multiplier=1))
                                    else:
                                        p.inc("g", g.affine_select(
                                            relq[c % 2][:, :],
                                            relq[c % 2][:, :], [[1, SW]],
                                            ALU.is_gt, 0.0,
                                            base=dlr + 1023 - 2046,
                                            channel_multiplier=-1))

                            @blk.scalar
                            def _(sc):
                                for i, (c, typ, dlr) in enumerate(plans):
                                    if typ == "A":
                                        p.wait(sc, "t", t0 + t_end[i])
                                    else:
                                        p.wait(sc, "v", v0 + v_end[i])
                                    p.inc("s", sc.activation(
                                        probsT[:, c * SW:(c + 1) * SW],
                                        psB[:, (c % 2) * SW:
                                            (c % 2 + 1) * SW], AF.Exp))

                        with nc.Block() as blk:
                            @blk.tensor
                            def _(t):
                                for c in range(NT):
                                    p.inc("t", t.matmul(
                                        psPV[:, :],
                                        v_sl[:, c * (H * 65) + h * 65:
                                             c * (H * 65) + h * 65 + 65],
                                        probsT[:, c * SW:(c + 1) * SW],
                                        start=(c == 0), stop=(c == NT - 1)))
                        with nc.Block() as blk:
                            @blk.scalar
                            def _(sc):
                                p.inc("s", sc.activation(
                                    rrow[0:1, :], psPV[64:65, :], AF.Ln))
                                p.wait(sc, "s")
                                p.inc("s", sc.activation(
                                    rrow[0:1, :], rrow[0:1, :], AF.Exp,
                                    scale=-1.0))
                                p.wait(sc, "s")
                                p.inc("s", sc.activation(
                                    otmp[:, :], psPV[0:64, :], AF.Copy))
                        with nc.Block() as blk:
                            @blk.tensor
                            def _(t):
                                p.inc("t", t.matmul(
                                    pbc[:, :], p.onesBr[0:1, 0:64],
                                    rrow[0:1, :], start=True, stop=True))
                        with nc.Block() as blk:
                            @blk.vector
                            def _(v):
                                dst = oT[rbase:rbase + 64,
                                         grp * T + st * SW:
                                         grp * T + (st + 1) * SW]
                                p.inc("v", v.tensor_mul(dst, otmp[:, :],
                                                        pbc[:, :]))

        es_qkv.close()
        if True:
            with contextlib.ExitStack() as e3:
                s3 = lambda n, sh, dt: e3.enter_context(
                    nc.sbuf_tensor(uname(n), sh, dt))
                ysl = s3("ysl", [128, NF * T], F32)
                wo_sb = s3("wo_sb2", [128, NF * D], BF16)
                hstg = [s3(f"hstg{i}", [128, T], F32) for i in range(2)]
                with nc.Block() as blk:
                    @blk.gpsimd
                    def _(g):
                        load_w_bf16(p, g, wo_sb, io["Wo"], D, D)
                gl_n = p.gl.n
                es_pso = e3.enter_context(contextlib.ExitStack())
                pso = es_pso.enter_context(nc.psum_tensor(uname("pso"),
                                                          [128, 2 * SW],
                                                          F32))
                jobs = [(dc, ts) for dc in range(NF) for ts in range(NS)]
                with nc.Block() as blk:
                    t0 = p.cnt["t"]
                    s0 = p.cnt["s"]

                    @blk.tensor
                    def _(t):
                        p.gl.wait(t, gl_n)
                        for i, (dc, ts) in enumerate(jobs):
                            if i >= 2:
                                p.wait(t, "s", s0 + i - 1)
                            last = None
                            for fc in range(NF):
                                last = t.matmul(
                                    pso[:, (i % 2) * SW:(i % 2 + 1) * SW],
                                    wo_sb[:, fc * D + dc * 128:
                                          fc * D + (dc + 1) * 128],
                                    oT[:, fc * T + ts * SW:
                                       fc * T + (ts + 1) * SW],
                                    start=(fc == 0), stop=(fc == NF - 1))
                            p.inc("t", last)

                    @blk.scalar
                    def _(sc):
                        for i, (dc, ts) in enumerate(jobs):
                            p.wait(sc, "t", t0 + i + 1)
                            p.inc("s", sc.activation(
                                ysl[:, dc * T + ts * SW:
                                    dc * T + (ts + 1) * SW],
                                pso[:, (i % 2) * SW:(i % 2 + 1) * SW],
                                AF.Identity, bias=bo_c[:, dc:dc + 1]))
                with nc.Block() as blk:
                    @blk.vector
                    def _(v):
                        pass
                es_pso.close()

                ld0 = p.ld.n
                with nc.Block() as blk:
                    v0h = p.cnt["v"]

                    @blk.sync
                    def _(sy):
                        p.st.wait(sy, p.st.n)
                        for fc in range(NF):
                            if fc >= 2:
                                p.wait(sy, "v", v0h + fc - 1)
                            p.ld.start(sy, hstg[fc % 2][:, :],
                                       dap(io["hn"], (b * D + fc * 128) * T,
                                           [[T, 128], [1, T]]))

                    @blk.vector
                    def _(v):
                        for fc in range(NF):
                            p.ld.wait(v, ld0 + 16 * (fc + 1))
                            p.inc("v", v.tensor_add(
                                ysl[:, fc * T:(fc + 1) * T],
                                ysl[:, fc * T:(fc + 1) * T],
                                hstg[fc % 2][:, :]))

                emit_ln_fm(p, None, mode="ln",
                           src_fn=lambda fc: ysl[:, fc * T:(fc + 1) * T],
                           gamma=lng_c, beta=lnb_c,
                           out_dram_fn=lambda fc: dap(
                               io["xout"], (b * D + fc * 128) * T,
                               [[T, 128], [1, T]]),
                           out_accum=True, eps=1e-12)


def build_l0_attn_b_prog():
    nc = bass.Bass("TRN2", target_bir_lowering=False)
    xst = nc.dram_tensor("xst", [B_PER_CORE, D, T], F32,
                         kind="ExternalInput")
    xout = nc.dram_tensor("xout", [B_PER_CORE, D, T], F32,
                          kind="ExternalOutput")
    io = {"xst": xst, "xout": xout}
    for nm, shape in (("Wq", [D, D]), ("bq", [D]), ("Wk", [D, D]),
                      ("bk", [D]), ("Wv", [D, D]), ("bv", [D]),
                      ("dist_emb", [EM, 64]), ("Wo", [D, D]), ("bo", [D]),
                      ("ln_gamma", [D]), ("ln_beta", [D])):
        io[nm] = nc.dram_tensor(nm, shape, F32, kind="ExternalInput")
    io["qd"] = nc.dram_tensor("qd", [T * EM], BF16)
    io["kd"] = nc.dram_tensor("kd", [T * EM], BF16)
    io["hn"] = nc.dram_tensor("hn", [B_PER_CORE, D, T], F32)
    p = Prog(nc)
    emit_prelude(p)
    emit_copy_state(p, xout, xst, B_PER_CORE * D * T)
    with nc.Block() as blk:
        @blk.gpsimd
        def _(g):
            p.ld.wait_all(g)
    for b in range(B_PER_CORE):
        emit_l0_attn_b(p, io, b)
    with nc.Block() as blk:
        @blk.sync
        def _(sy):
            p.ld.wait_all(sy)
            p.st.wait_all(sy)
            p.gl.wait_all(sy)
    p.close()
    return nc



def build_l0_attn_a_prog():
    nc = bass.Bass("TRN2", target_bir_lowering=False)
    xst = nc.dram_tensor("xst", [B_PER_CORE, D, T], F32,
                         kind="ExternalInput")
    xout = nc.dram_tensor("xout", [B_PER_CORE, D, T], F32,
                          kind="ExternalOutput")
    w_qkv = nc.dram_tensor("w_qkv", [D, 3 * D], F32, kind="ExternalInput")
    b_qkv = nc.dram_tensor("b_qkv", [3 * D], F32, kind="ExternalInput")
    w_out = nc.dram_tensor("w_out", [D, D], F32, kind="ExternalInput")
    b_out = nc.dram_tensor("b_out", [D], F32, kind="ExternalInput")

    p = Prog(nc)
    emit_prelude(p)
    emit_copy_state(p, xout, xst, B_PER_CORE * D * T)
    with nc.Block() as blk:
        @blk.gpsimd
        def _(g):
            p.ld.wait_all(g)

    io = {"xst": xst, "xout": xout, "w_qkv": w_qkv, "b_qkv": b_qkv,
          "w_out": w_out, "b_out": b_out}
    for b in range(B_PER_CORE):
        emit_attn_l0a(p, io, b)
    with nc.Block() as blk:
        @blk.sync
        def _(sy):
            p.ld.wait_all(sy)
            p.st.wait_all(sy)
            p.gl.wait_all(sy)
    p.close()
    return nc



def emit_state_barrier(p):
    with p.nc.Block() as blk:
        @blk.gpsimd
        def _(g):
            p.st.wait_all(g)
            p.gl.wait_all(g)

        @blk.sync
        def _(sy):
            p.st.wait_all(sy)
            p.ld.wait_all(sy)


def build_full_prog():
    nc = bass.Bass("TRN2", target_bir_lowering=False)
    xin = nc.dram_tensor("L0_attn_a_x", [B_PER_CORE, D, T], F32,
                         kind="ExternalInput")
    xout = nc.dram_tensor("xout", [B_PER_CORE, D, T], F32,
                          kind="ExternalOutput")
    hd = {}
    shapes = {
        "L0_attn_a_w_qkv": [D, 3 * D], "L0_attn_a_b_qkv": [3 * D],
        "L0_attn_a_w_out": [D, D], "L0_attn_a_b_out": [D],
        "L0_attn_b_Wq": [D, D], "L0_attn_b_bq": [D],
        "L0_attn_b_Wk": [D, D], "L0_attn_b_bk": [D],
        "L0_attn_b_Wv": [D, D], "L0_attn_b_bv": [D],
        "L0_attn_b_dist_emb": [EM, 64], "L0_attn_b_Wo": [D, D],
        "L0_attn_b_bo": [D], "L0_attn_b_ln_gamma": [D],
        "L0_attn_b_ln_beta": [D],
        "L0_ffn_wq": [D, D], "L0_ffn_wk": [D, D], "L0_ffn_wv": [D, D],
        "L0_ffn_wo": [D, D], "L0_ffn_router_w": [D, NE],
        "L0_ffn_exp_fc1": [NE, D, ED], "L0_ffn_exp_fc2": [NE, ED, D],
        "L1_attn_a_qkv_w": [D, 3 * D], "L1_attn_a_qkv_b": [3 * D],
        "L1_attn_a_w1_write": [D, 64], "L1_attn_a_w2_write": [D, 64],
        "L1_attn_a_w1_read": [D, 64], "L1_attn_a_w2_read": [D, 64],
        "L1_attn_a_kv_geo_w": [D, 2 * D], "L1_attn_a_out_w": [D, D],
        "L1_attn_a_out_b": [D], "L1_attn_a_kv_gate": [1],
        "L1_attn_a_mem_scale": [H], "L1_attn_a_decay_logits": [H],
        "L1_attn_b_Wq": [D, D], "L1_attn_b_Wk": [D, D],
        "L1_attn_b_Wv": [D, D], "L1_attn_b_Wo": [D, D],
        "L1_attn_b_alibi_slopes": [H], "L1_attn_b_power": [1],
        "L1_ffn_col_emb": [NC_COL, D], "L1_ffn_gate_w1": [D, 512],
        "L1_ffn_gate_b1": [512], "L1_ffn_gate_w2": [512, NC_COL],
        "L1_ffn_gate_b2": [NC_COL], "L1_ffn_W_cols": [NC_COL, D, NN],
        "L1_ffn_b_cols": [NC_COL, NN], "L1_ffn_proj_w": [4096, D],
        "L1_ffn_proj_b": [D], "L1_ffn_ln_g": [D], "L1_ffn_ln_b": [D],
    }
    for nm, sh in shapes.items():
        hd[nm] = nc.dram_tensor(nm, sh, F32, kind="ExternalInput")
    scr = {
        "xf": nc.dram_tensor("scr_xf", [B_PER_CORE, D, T], F32),
        "hn": nc.dram_tensor("scr_hn", [B_PER_CORE, D, T], F32),
        "qd": nc.dram_tensor("scr_qd", [T * EM], BF16),
        "kd": nc.dram_tensor("scr_kd", [T * EM], BF16),
    }

    p = Prog(nc)
    emit_prelude(p)
    register_consts(p, [2.0, 1e-6])
    emit_copy_state(p, xout, xin, B_PER_CORE * D * T)
    with nc.Block() as blk:
        @blk.gpsimd
        def _(g):
            p.ld.wait_all(g)

    steps = []
    for name, mp in _CHAIN:
        mapping = mp if mp is not None else dict(
            s for nm2, m2 in _CHAIN if m2 is not None and nm2 == name
            for s in m2.items())
        steps.append((name, mapping))

    emitters = {
        "l0_attn_a": emit_attn_l0a, "l0_ffn": emit_l0_ffn,
        "l0_attn_b": emit_l0_attn_b, "l1_attn_a": emit_l1_attn_a,
        "l1_ffn": emit_l1_ffn, "l1_attn_b": emit_l1_attn_b,
    }
    for name, mapping in steps:
        io = {"xst": xout, "xout": xout}
        io.update(scr)
        for k, src in mapping.items():
            io[k] = hd[src]
        for b in range(B_PER_CORE):
            emitters[name](p, io, b)
        emit_state_barrier(p)

    with nc.Block() as blk:
        @blk.sync
        def _(sy):
            p.ld.wait_all(sy)
            p.st.wait_all(sy)
            p.gl.wait_all(sy)
    p.close()
    return nc



_CHAIN = [
    ("l0_attn_a", {"w_qkv": "L0_attn_a_w_qkv", "b_qkv": "L0_attn_a_b_qkv",
                   "w_out": "L0_attn_a_w_out", "b_out": "L0_attn_a_b_out"}),
    ("l0_ffn", {"wq": "L0_ffn_wq", "wk": "L0_ffn_wk", "wv": "L0_ffn_wv",
                "wo": "L0_ffn_wo", "router_w": "L0_ffn_router_w",
                "exp_fc1": "L0_ffn_exp_fc1", "exp_fc2": "L0_ffn_exp_fc2"}),
    ("l0_attn_b", {"Wq": "L0_attn_b_Wq", "bq": "L0_attn_b_bq",
                   "Wk": "L0_attn_b_Wk", "bk": "L0_attn_b_bk",
                   "Wv": "L0_attn_b_Wv", "bv": "L0_attn_b_bv",
                   "dist_emb": "L0_attn_b_dist_emb", "Wo": "L0_attn_b_Wo",
                   "bo": "L0_attn_b_bo", "ln_gamma": "L0_attn_b_ln_gamma",
                   "ln_beta": "L0_attn_b_ln_beta"}),
    ("l0_ffn", None),
    ("l1_attn_a", {"qkv_w": "L1_attn_a_qkv_w", "qkv_b": "L1_attn_a_qkv_b",
                   "w1_write": "L1_attn_a_w1_write",
                   "w2_write": "L1_attn_a_w2_write",
                   "w1_read": "L1_attn_a_w1_read",
                   "w2_read": "L1_attn_a_w2_read",
                   "kv_geo_w": "L1_attn_a_kv_geo_w",
                   "out_w": "L1_attn_a_out_w", "out_b": "L1_attn_a_out_b",
                   "kv_gate": "L1_attn_a_kv_gate",
                   "mem_scale": "L1_attn_a_mem_scale",
                   "decay_logits": "L1_attn_a_decay_logits"}),
    ("l1_ffn", {"col_emb": "L1_ffn_col_emb", "gate_w1": "L1_ffn_gate_w1",
                "gate_b1": "L1_ffn_gate_b1", "gate_w2": "L1_ffn_gate_w2",
                "gate_b2": "L1_ffn_gate_b2", "W_cols": "L1_ffn_W_cols",
                "b_cols": "L1_ffn_b_cols", "proj_w": "L1_ffn_proj_w",
                "proj_b": "L1_ffn_proj_b", "ln_g": "L1_ffn_ln_g",
                "ln_b": "L1_ffn_ln_b"}),
    ("l1_attn_b", {"Wq": "L1_attn_b_Wq", "Wk": "L1_attn_b_Wk",
                   "Wv": "L1_attn_b_Wv", "Wo": "L1_attn_b_Wo",
                   "alibi_slopes": "L1_attn_b_alibi_slopes",
                   "power": "L1_attn_b_power"}),
    ("l1_ffn", None),
]

_BUILDERS = {
    "l0_attn_a": build_l0_attn_a_prog,
    "l0_ffn": build_l0_ffn_prog,
    "l0_attn_b": build_l0_attn_b_prog,
    "l1_attn_a": build_l1_attn_a_prog,
    "l1_ffn": build_l1_ffn_prog,
    "l1_attn_b": build_l1_attn_b_prog,
}

_PROG_CACHE = {}


FUSED = True


def kernel(**inputs):
    set_dims(2048)
    x = inputs["L0_attn_a_x"]
    n_cores = 8
    xt = np.ascontiguousarray(x.transpose(0, 2, 1)).reshape(
        n_cores, B_PER_CORE, D, T)

    if FUSED:
        if "full" not in _PROG_CACHE:
            _PROG_CACHE["full"] = build_full_prog()
        nc = _PROG_CACHE["full"]
        wdict = {}
        for k, arr in inputs.items():
            if k == "L0_attn_a_x":
                continue
            a = np.asarray(arr)
            if a.ndim == 0:
                a = a.reshape(1)
            wdict[k] = a
        in_maps = [dict(wdict, L0_attn_a_x=np.ascontiguousarray(xt[c]))
                   for c in range(n_cores)]
        res = run_bass_kernel_spmd(nc, in_maps, core_ids=list(range(n_cores)))
        state = [np.asarray(res.results[c]["xout"]) for c in range(n_cores)]
    else:
        state = [np.ascontiguousarray(xt[c]) for c in range(n_cores)]
        for name, mp in _CHAIN:
            mapping = mp if mp is not None else dict(
                s for nm2, m2 in _CHAIN
                if m2 is not None and nm2 == name for s in m2.items())
            if name not in _PROG_CACHE:
                _PROG_CACHE[name] = _BUILDERS[name]()
            nc = _PROG_CACHE[name]
            wdict = {}
            for k, src in mapping.items():
                arr = np.asarray(inputs[src])
                if arr.ndim == 0:
                    arr = arr.reshape(1)
                wdict[k] = arr
            in_maps = [dict(wdict, xst=state[c]) for c in range(n_cores)]
            res = run_bass_kernel_spmd(nc, in_maps,
                                       core_ids=list(range(n_cores)))
            state = [np.asarray(res.results[c]["xout"])
                     for c in range(n_cores)]

    out = np.stack(state).reshape(16, D, T).transpose(0, 2, 1)
    return np.ascontiguousarray(out)
